# version 07
# baseline (speedup 1.0000x reference)
import math
from contextlib import ExitStack

import numpy as np
import ml_dtypes

import concourse.bass as bass
import concourse.mybir as mybir
from concourse.bass_utils import run_bass_kernel_spmd

F32 = mybir.dt.float32
BF16 = mybir.dt.bfloat16
F16 = mybir.dt.float16
I32 = mybir.dt.int32
AF = mybir.ActivationFunctionType
OP = mybir.AluOpType
AX = mybir.AxisListType

NPBF16 = ml_dtypes.bfloat16

D = 2048
CD = D // 128
T = 1024
NCORES = 8
H = 16
HD = 128
I0 = 5504
C0 = I0 // 128
C0P = 48
I1 = 8192
C1 = I1 // 128
LN_EPS = 1e-5
RMS_EPS = float(np.finfo(np.float32).eps)
INVSQRT_HD = 1.0 / math.sqrt(HD)

_PROG_CACHE = {}



def _pw(w, pad_oc=None):
    K, O = w.shape
    KC, OC = K // 128, O // 128
    t = np.ascontiguousarray(
        w.reshape(KC, 128, OC, 128).transpose(2, 1, 0, 3).reshape(OC, 128, KC * 128))
    if pad_oc is not None and pad_oc > OC:
        t = np.concatenate([t, np.zeros((pad_oc - OC, 128, KC * 128), dtype=t.dtype)], axis=0)
    return t


def _pb(b):
    O = b.shape[0]
    return np.ascontiguousarray(np.asarray(b, dtype=np.float32).reshape(O // 128, 128).T)


def _fm(x):
    return np.ascontiguousarray(np.asarray(x, dtype=np.float32).T)


def _hadamard(n):
    h = np.array([[1.0]], dtype=np.float32)
    while h.shape[0] < n:
        h = np.block([[h, h], [h, -h]])
    return np.ascontiguousarray(h.astype(np.float32))


class Ctr:

    def __init__(self, st, nc, name):
        self.h = st.enter_context(nc.semaphore(name))
        self.n = 0

    def inc(self, instr, k=1):
        instr.then_inc(self.h, k)
        self.n += k
        return instr



def build_ghost():
    nc = bass.Bass("TRN2", target_bir_lowering=False)
    gu_idx = nc.dram_tensor("gu_idx", [12, 128, CD * 128], I32, kind="ExternalInput").ap()
    dn_idx = nc.dram_tensor("dn_idx", [2, 128, C0 * 128], I32, kind="ExternalInput").ap()
    lut = nc.dram_tensor("lut", [1, 16], F32, kind="ExternalInput").ap()
    gu_w = nc.dram_tensor("gu_w", [12, 128, CD * 128], BF16, kind="ExternalOutput").ap()
    dn_w = nc.dram_tensor("dn_w", [2, 128, C0 * 128], BF16, kind="ExternalOutput").ap()
    WMAX = C0 * 128
    with ExitStack() as st:
        idx_sb = [st.enter_context(nc.sbuf_tensor(f"idx{p}", [128, WMAX], I32)) for p in range(2)]
        idxb = st.enter_context(nc.sbuf_tensor("idxb", [128, WMAX], BF16))
        w_bf = [st.enter_context(nc.sbuf_tensor(f"wbf{p}", [128, WMAX], BF16)) for p in range(2)]
        msk = st.enter_context(nc.sbuf_tensor("msk", [128, WMAX], BF16))
        lut_sb = st.enter_context(nc.sbuf_tensor("lut_sb", [1, 16], F32))
        lut_col = st.enter_context(nc.sbuf_tensor("lut_col", [128, 16], F32))
        ones1 = st.enter_context(nc.sbuf_tensor("ones1", [1, 128], F32))
        lut_ps = st.enter_context(nc.psum_tensor("lut_ps", [128, 16], F32))
        s_lut = Ctr(st, nc, "s_lut")
        s_in = [Ctr(st, nc, f"s_in{p}") for p in range(2)]
        s_pe = Ctr(st, nc, "s_pe")
        s_v = Ctr(st, nc, "s_v")
        s_out = Ctr(st, nc, "s_out")
        with nc.Block() as block:
            @block.sync
            def _(sync):
                s_lut.inc(nc.sync.dma_start(out=lut_sb[:, :], in_=lut), 16)
                nc.vector.wait_ge(s_lut.h, 16)
                s_v.inc(nc.vector.memset(ones1[:, :], 1.0))
                nc.tensor.wait_ge(s_v.h, 1)
                s_pe.inc(nc.tensor.matmul(out=lut_ps[:, :], lhsT=ones1[:, :], rhs=lut_sb[:, :],
                                          start=True, stop=True))
                nc.scalar.wait_ge(s_pe.h, 1)
                s_v.inc(nc.scalar.copy(out=lut_col[:, :], in_=lut_ps[:, :]))

                units = [(gu_idx, gu_w, CD * 128)] * 12 + [(dn_idx, dn_w, C0 * 128)] * 2
                for ui, (src, dst, width) in enumerate(units):
                    u = ui if ui < 12 else ui - 12
                    p = ui % 2
                    if ui >= 2:
                        nc.sync.wait_ge(s_v.h, 2 + (ui - 2) + 1)
                    s_in[p].inc(nc.sync.dma_start(out=idx_sb[p][:, :width], in_=src[u]), 16)
                    nc.vector.wait_ge(s_in[p].h, s_in[p].n)
                    nc.vector.wait_ge(s_v.h, 2)
                    if ui >= 2:
                        nc.vector.wait_ge(s_out.h, 16 * (ui - 1))
                    nc.vector.tensor_copy(out=idxb[:, :width], in_=idx_sb[p][:, :width])
                    last = None
                    for k in range(16):
                        if k == 0:
                            nc.vector.tensor_scalar(out=w_bf[p][:, :width], in0=idxb[:, :width],
                                                    scalar1=0.0, scalar2=lut_col[:, 0:1],
                                                    op0=OP.is_equal, op1=OP.mult)
                        else:
                            nc.vector.tensor_scalar(out=msk[:, :width], in0=idxb[:, :width],
                                                    scalar1=float(k), scalar2=lut_col[:, k:k + 1],
                                                    op0=OP.is_equal, op1=OP.mult)
                            last = nc.vector.tensor_tensor(out=w_bf[p][:, :width],
                                                           in0=w_bf[p][:, :width],
                                                           in1=msk[:, :width], op=OP.add)
                    s_v.inc(last)
                    nc.gpsimd.wait_ge(s_v.h, 2 + ui + 1)
                    s_out.inc(nc.gpsimd.dma_start(out=dst[u], in_=w_bf[p][:, :width]), 16)
                nc.gpsimd.wait_ge(s_out.h, 16 * 14)
    return nc



def build_w2fold():
    nc = bass.Bass("TRN2", target_bir_lowering=False)
    w2t = nc.dram_tensor("w2t", [C1, 128, 256], F32, kind="ExternalInput").ap()
    h128 = nc.dram_tensor("h128", [128, 128], F32, kind="ExternalInput").ap()
    w2f = nc.dram_tensor("w2f", [C1, 128, 256], BF16, kind="ExternalOutput").ap()
    W = 256
    with ExitStack() as st:
        a_bf = st.enter_context(nc.sbuf_tensor("arena_abf", [128, C1 * W], BF16))
        a = st.enter_context(nc.sbuf_tensor("arena_a", [128, C1 * W], F32))
        b = st.enter_context(nc.sbuf_tensor("arena_b", [128, C1 * W], F32))
        h_sb = st.enter_context(nc.sbuf_tensor("h_sb", [128, 128], BF16))
        ps = [st.enter_context(nc.psum_tensor(f"ps{p}", [128, W], F32)) for p in range(2)]
        s_in = Ctr(st, nc, "s_in")
        s_pe = Ctr(st, nc, "s_pe")
        s_a = Ctr(st, nc, "s_a")
        s_out = Ctr(st, nc, "s_out")
        with nc.Block() as block:
            @block.sync
            def _(sync):
                s_in.inc(nc.gpsimd.dma_start(out=h_sb[:, :], in_=h128), 16)
                for c in range(C1):
                    s_in.inc(nc.gpsimd.dma_start(out=a_bf[:, c * W:(c + 1) * W], in_=w2t[c]), 16)
                nc.tensor.wait_ge(s_in.h, 16 * (C1 + 1))
                for c in range(C1):
                    p = c % 2
                    if c >= 2:
                        nc.tensor.wait_ge(s_a.h, c - 1)
                    s_pe.inc(nc.tensor.matmul(out=ps[p][:, :], lhsT=h_sb[:, :],
                                              rhs=a_bf[:, c * W:(c + 1) * W], start=True, stop=True))
                    nc.scalar.wait_ge(s_pe.h, c + 1)
                    s_a.inc(nc.scalar.activation(out=b[:, c * W:(c + 1) * W], in_=ps[p][:, :],
                                                 func=AF.Identity, scale=1.0 / math.sqrt(I1)))
                nc.vector.wait_ge(s_a.h, C1)
                src, dst = b, a
                last = None
                for s in range(6):
                    step = 1 << s
                    for c in range(C1):
                        if c & step:
                            continue
                        cc = c + step
                        nc.vector.tensor_tensor(out=dst[:, c * W:(c + 1) * W],
                                                in0=src[:, c * W:(c + 1) * W],
                                                in1=src[:, cc * W:(cc + 1) * W], op=OP.add)
                        last = nc.vector.tensor_tensor(out=dst[:, cc * W:(cc + 1) * W],
                                                       in0=src[:, c * W:(c + 1) * W],
                                                       in1=src[:, cc * W:(cc + 1) * W],
                                                       op=OP.subtract)
                    src, dst = dst, src
                s_a.inc(last)
                nc.gpsimd.wait_ge(s_a.h, C1 + 1)
                for c in range(C1):
                    s_out.inc(nc.gpsimd.dma_start(out=w2f[c], in_=src[:, c * W:(c + 1) * W]), 16)
                nc.gpsimd.wait_ge(s_out.h, 16 * C1)
    return nc



def emit_ln_stats(nc, s_pe, s_act, s_v, x_sb, scr_a, scr_b, stat_ps, rows, ones_bf,
                  eps_ap, inv_n, rms=False, n_tok=T):
    nh = n_tok // 512
    for dk in range(CD):
        sl = slice(dk * n_tok, (dk + 1) * n_tok)
        pa = dk % 2
        if dk >= 2:
            nc.scalar.wait_ge(s_pe.h, s_pe.n)
        if not rms:
            s_act.inc(nc.scalar.activation(out=scr_a[pa][:, :n_tok], in_=x_sb[:, sl], func=AF.Identity))
        s_act.inc(nc.scalar.activation(out=scr_b[pa][:, :n_tok], in_=x_sb[:, sl], func=AF.Square))
        nc.tensor.wait_ge(s_act.h, s_act.n)
        for hh in range(nh):
            ts = slice(hh * 512, (hh + 1) * 512)
            if not rms:
                s_pe.inc(nc.tensor.matmul(out=stat_ps[0][:, ts], lhsT=ones_bf[:, :],
                                          rhs=scr_a[pa][:, ts], start=(dk == 0), stop=(dk == CD - 1)))
            s_pe.inc(nc.tensor.matmul(out=stat_ps[1][:, ts], lhsT=ones_bf[:, :],
                                      rhs=scr_b[pa][:, ts], start=(dk == 0), stop=(dk == CD - 1)))
    nc.scalar.wait_ge(s_pe.h, s_pe.n)
    if not rms:
        s_act.inc(nc.scalar.activation(out=rows[0][0:1, :n_tok], in_=stat_ps[0][0:1, :n_tok],
                                       func=AF.Identity, scale=inv_n))
    s_act.inc(nc.scalar.activation(out=rows[1][0:1, :n_tok], in_=stat_ps[1][0:1, :n_tok],
                                   func=AF.Identity, scale=inv_n))
    nc.vector.wait_ge(s_act.h, s_act.n)
    if not rms:
        s_v.inc(nc.vector.scalar_tensor_tensor(out=rows[2][0:1, :n_tok], in0=rows[0][0:1, :n_tok],
                                               scalar=-1.0, in1=rows[0][0:1, :n_tok],
                                               op0=OP.mult, op1=OP.mult))
        s_v.inc(nc.vector.tensor_tensor(out=rows[2][0:1, :n_tok], in0=rows[1][0:1, :n_tok],
                                        in1=rows[2][0:1, :n_tok], op=OP.add))
        var_row = rows[2][0:1, :n_tok]
        nc.scalar.wait_ge(s_v.h, s_v.n)
    else:
        var_row = rows[1][0:1, :n_tok]
    s_act.inc(nc.scalar.activation(out=rows[3][0:1, :n_tok], in_=var_row, func=AF.Ln, bias=eps_ap))
    s_act.inc(nc.scalar.activation(out=rows[3][0:1, :n_tok], in_=rows[3][0:1, :n_tok], func=AF.Exp,
                                   scale=-0.5))
    if not rms:
        nc.vector.wait_ge(s_act.h, s_act.n)
        s_v.inc(nc.vector.scalar_tensor_tensor(out=rows[2][0:1, :n_tok], in0=rows[0][0:1, :n_tok],
                                               scalar=-1.0, in1=rows[3][0:1, :n_tok],
                                               op0=OP.mult, op1=OP.mult))


def emit_bcast_rows(nc, s_pe, srcs, dsts, ones1, n_tok, wait_sem=None, wait_val=None):
    if wait_sem is not None:
        nc.tensor.wait_ge(wait_sem, wait_val)
    for src, dst in zip(srcs, dsts):
        for hh in range(n_tok // 512):
            ts = slice(hh * 512, (hh + 1) * 512)
            s_pe.inc(nc.tensor.matmul(out=dst[:, ts], lhsT=ones1[:, :], rhs=src[:, ts],
                                      start=True, stop=True))



DEBUG_FFN = False


def build_ffn(C, wdtype, wdtype_d=None):
    if wdtype_d is None:
        wdtype_d = wdtype
    nc = bass.Bass("TRN2", target_bir_lowering=False)
    tb = 512
    if DEBUG_FFN:
        xh_dbg = nc.dram_tensor("xh_dbg", [CD, 128, 512], BF16, kind="ExternalOutput").ap()
        h_dbg = nc.dram_tensor("h_dbg", [C, 128, 512], BF16, kind="ExternalOutput").ap()
        wd_dbg = nc.dram_tensor("wd_dbg", [2, 128, C * 128], BF16, kind="ExternalOutput").ap()
    x_in = nc.dram_tensor("x", [CD, 128, T], F32, kind="ExternalInput").ap()
    wg = nc.dram_tensor("wg", [C, 128, CD * 128], wdtype, kind="ExternalInput").ap()
    wu = nc.dram_tensor("wu", [C, 128, CD * 128], wdtype, kind="ExternalInput").ap()
    wd = nc.dram_tensor("wd", [CD, 128, C * 128], wdtype_d, kind="ExternalInput").ap()
    x_out = nc.dram_tensor("x_out", [CD, 128, T], F32, kind="ExternalOutput").ap()
    n_tb = T // tb
    with ExitStack() as st:
        x_sb = st.enter_context(nc.sbuf_tensor("x_sb", [128, CD * T], F32))
        xhat = st.enter_context(nc.sbuf_tensor("xhat", [128, CD * tb], BF16))
        h_sb = st.enter_context(nc.sbuf_tensor("h_sb", [128, C * tb], BF16))
        scr = [st.enter_context(nc.sbuf_tensor(f"scr{p}", [128, T], BF16)) for p in range(2)]
        scrq = [st.enter_context(nc.sbuf_tensor(f"scrq{p}", [128, T], BF16)) for p in range(2)]
        scr_f = st.enter_context(nc.sbuf_tensor("scr_f", [128, tb], F32))
        sg = [st.enter_context(nc.sbuf_tensor(f"sg{p}", [128, tb], BF16)) for p in range(2)]
        rows = [st.enter_context(nc.sbuf_tensor(f"rows{i}", [1, T], F32)) for i in range(4)]
        ones_bf = st.enter_context(nc.sbuf_tensor("ones_bf", [128, 1], BF16))
        ones1 = st.enter_context(nc.sbuf_tensor("ones1", [1, 128], F32))
        eps_sb = st.enter_context(nc.sbuf_tensor("eps_sb", [1, 1], F32))
        wg_sb = [st.enter_context(nc.sbuf_tensor(f"wg{p}", [128, CD * 128], BF16)) for p in range(2)]
        wu_sb = [st.enter_context(nc.sbuf_tensor(f"wu{p}", [128, CD * 128], BF16)) for p in range(2)]
        wd1 = st.enter_context(nc.sbuf_tensor("wd1", [128, C * 128], BF16))
        s_x = Ctr(st, nc, "s_x")
        s_wA = [Ctr(st, nc, f"s_wA{p}") for p in range(2)]
        s_wD = [Ctr(st, nc, f"s_wD{p}") for p in range(2)]
        s_pe = Ctr(st, nc, "s_pe")
        s_act = Ctr(st, nc, "s_act")
        s_v = Ctr(st, nc, "s_v")
        s_out = Ctr(st, nc, "s_out")
        with nc.Block() as block:
            @block.sync
            def _(sync):
                for dk in range(CD):
                    s_x.inc(nc.sync.dma_start(out=x_sb[:, dk * T:(dk + 1) * T], in_=x_in[dk]), 16)
                s_v.inc(nc.vector.memset(ones_bf[:, :], 1.0))
                s_v.inc(nc.vector.memset(ones1[:, :], 1.0))
                s_v.inc(nc.vector.memset(eps_sb[:, :], LN_EPS))
                nc.scalar.wait_ge(s_x.h, 16 * CD)
                nc.scalar.wait_ge(s_v.h, 3)
                nc.tensor.wait_ge(s_v.h, 3)
                nc.vector.wait_ge(s_x.h, 16 * CD)
                with nc.psum_tensor("stat0", [1, T], F32) as st0, \
                     nc.psum_tensor("stat1", [1, T], F32) as st1:
                    emit_ln_stats(nc, s_pe, s_act, s_v, x_sb, scr, scrq, [st0, st1], rows,
                                  ones_bf, eps_sb[0:1, 0:1], 1.0 / D)
                for tbi in range(n_tb):
                    t0 = tbi * tb
                    with nc.psum_tensor(f"bc0_{tbi}", [128, tb], F32) as bc0, \
                         nc.psum_tensor(f"bc1_{tbi}", [128, tb], F32) as bc1:
                        emit_bcast_rows(nc, s_pe, [rows[3][0:1, t0:t0 + tb], rows[2][0:1, t0:t0 + tb]],
                                        [bc0, bc1], ones1, tb, s_v.h, s_v.n)
                        nc.vector.wait_ge(s_pe.h, s_pe.n)
                        for dk in range(CD):
                            sl = slice(dk * T + t0, dk * T + t0 + tb)
                            nc.vector.tensor_tensor(out=scr_f[:, :], in0=x_sb[:, sl],
                                                    in1=bc0[:, :], op=OP.mult)
                            last = nc.vector.tensor_tensor(out=xhat[:, dk * tb:(dk + 1) * tb],
                                                           in0=scr_f[:, :], in1=bc1[:, :], op=OP.add)
                        s_v.inc(last)
                    apply_done = s_v.n
                    if DEBUG_FFN and tbi == 0:
                        nc.sync.wait_ge(s_v.h, apply_done)
                        for dk in range(CD):
                            s_out.inc(nc.sync.dma_start(out=xh_dbg[dk], in_=xhat[:, dk * tb:(dk + 1) * tb]), 16)
                    with nc.psum_tensor(f"g0_{tbi}", [128, tb], F32) as g0, \
                         nc.psum_tensor(f"g1_{tbi}", [128, tb], F32) as g1, \
                         nc.psum_tensor(f"u0_{tbi}", [128, tb], F32) as u0, \
                         nc.psum_tensor(f"u1_{tbi}", [128, tb], F32) as u1:
                        g_ps, u_ps = [g0, g1], [u0, u1]
                        silu_base = s_act.n
                        mul_base = s_v.n
                        peA_base = s_pe.n
                        for c in range(C):
                            p = c % 2
                            if c >= 2:
                                nc.sync.wait_ge(s_pe.h, peA_base + 2 * c - 2)
                            s_wA[p].inc(nc.sync.dma_start(out=wg_sb[p][:, :], in_=wg[c]), 16)
                            s_wA[p].inc(nc.sync.dma_start(out=wu_sb[p][:, :], in_=wu[c]), 16)
                            nc.tensor.wait_ge(s_wA[p].h, s_wA[p].n)
                            nc.tensor.wait_ge(s_v.h, apply_done)
                            if c >= 2:
                                nc.tensor.wait_ge(s_act.h, silu_base + c - 1)
                                nc.tensor.wait_ge(s_v.h, mul_base + c - 1)
                            last = None
                            for dk in range(CD):
                                last = nc.tensor.matmul(out=g_ps[p][:, :],
                                                        lhsT=wg_sb[p][:, dk * 128:(dk + 1) * 128],
                                                        rhs=xhat[:, dk * tb:(dk + 1) * tb],
                                                        start=(dk == 0), stop=(dk == CD - 1))
                            s_pe.inc(last)
                            for dk in range(CD):
                                last = nc.tensor.matmul(out=u_ps[p][:, :],
                                                        lhsT=wu_sb[p][:, dk * 128:(dk + 1) * 128],
                                                        rhs=xhat[:, dk * tb:(dk + 1) * tb],
                                                        start=(dk == 0), stop=(dk == CD - 1))
                            s_pe.inc(last)
                            nc.scalar.wait_ge(s_pe.h, s_pe.n - 1)
                            if c >= 2:
                                nc.scalar.wait_ge(s_v.h, mul_base + c - 1)
                            s_act.inc(nc.scalar.activation(out=sg[p][:, :], in_=g_ps[p][:, :],
                                                           func=AF.Silu))
                            nc.vector.wait_ge(s_pe.h, s_pe.n)
                            nc.vector.wait_ge(s_act.h, silu_base + c + 1)
                            s_v.inc(nc.vector.tensor_tensor(out=h_sb[:, c * tb:(c + 1) * tb],
                                                            in0=sg[p][:, :], in1=u_ps[p][:, :],
                                                            op=OP.mult))
                    mulA_end = s_v.n
                    if DEBUG_FFN and tbi == 0:
                        nc.sync.wait_ge(s_v.h, mulA_end)
                        for c in range(C):
                            s_out.inc(nc.sync.dma_start(out=h_dbg[c], in_=h_sb[:, c * tb:(c + 1) * tb]), 16)
                        nc.sync.wait_ge(s_out.h, s_out.n)
                    with nc.psum_tensor(f"d0_{tbi}", [128, tb], F32) as d0, \
                         nc.psum_tensor(f"d1_{tbi}", [128, tb], F32) as d1:
                        d_ps = [d0, d1]
                        wd_sb = [xhat[:, 0:C * 128], wd1[:, :]]
                        evac_base = s_v.n
                        peB_base = s_pe.n
                        for o in range(CD):
                            p = o % 2
                            if o >= 2:
                                nc.sync.wait_ge(s_pe.h, peB_base + o - 1)
                            else:
                                nc.sync.wait_ge(s_pe.h, peB_base)
                            s_wD[p].inc(nc.sync.dma_start(out=wd_sb[p], in_=wd[o]), 16)
                            nc.tensor.wait_ge(s_wD[p].h, s_wD[p].n)
                            nc.tensor.wait_ge(s_v.h, mulA_end)
                            if o >= 2:
                                nc.tensor.wait_ge(s_v.h, evac_base + o - 1)
                            last = None
                            for c in range(C):
                                last = nc.tensor.matmul(out=d_ps[p][:, :],
                                                        lhsT=wd_sb[p][:, c * 128:(c + 1) * 128] if p else wd_sb[0][:, c * 128:(c + 1) * 128],
                                                        rhs=h_sb[:, c * tb:(c + 1) * tb],
                                                        start=(c == 0), stop=(c == C - 1))
                            s_pe.inc(last)
                            nc.vector.wait_ge(s_pe.h, s_pe.n)
                            s_v.inc(nc.vector.scalar_tensor_tensor(
                                out=x_sb[:, o * T + t0:o * T + t0 + tb], in0=d_ps[p][:, :],
                                scalar=0.5, in1=x_sb[:, o * T + t0:o * T + t0 + tb],
                                op0=OP.mult, op1=OP.add))
                nc.sync.wait_ge(s_v.h, s_v.n)
                for dk in range(CD):
                    s_out.inc(nc.sync.dma_start(out=x_out[dk], in_=x_sb[:, dk * T:(dk + 1) * T]), 16)
                nc.sync.wait_ge(s_out.h, s_out.n)
    return nc




def _soft_barrier(nc, ctrs):
    for eng in (nc.tensor, nc.scalar, nc.vector, nc.gpsimd, nc.sync):
        for c in ctrs:
            if c.n > 0:
                eng.wait_ge(c.h, c.n)



def build_qkv0():
    nc = bass.Bass("TRN2", target_bir_lowering=False)
    x_in = nc.dram_tensor("x", [CD, 128, T], F32, kind="ExternalInput").ap()
    wqk = nc.dram_tensor("wqk", [2 * H, 128, CD * 128], BF16, kind="ExternalInput").ap()
    wv = nc.dram_tensor("wv", [CD, 128, D], BF16, kind="ExternalInput").ap()
    bqk = nc.dram_tensor("bqk", [128, 2 * H], F32, kind="ExternalInput").ap()
    bv = nc.dram_tensor("bv", [1, D], F32, kind="ExternalInput").ap()
    q_out = nc.dram_tensor("q", [H, 128, T], BF16, kind="ExternalOutput").ap()
    k_out = nc.dram_tensor("k", [H, 128, T], BF16, kind="ExternalOutput").ap()
    v_out = nc.dram_tensor("v", [T // 128, 128, D], BF16, kind="ExternalOutput").ap()
    with ExitStack() as st:
        x_sb = st.enter_context(nc.sbuf_tensor("x_sb", [128, CD * T], F32))
        xhat = st.enter_context(nc.sbuf_tensor("xhat", [128, CD * T], BF16))
        scr = [st.enter_context(nc.sbuf_tensor(f"scr{p}", [128, T], BF16)) for p in range(2)]
        scrq = [st.enter_context(nc.sbuf_tensor(f"scrq{p}", [128, T], BF16)) for p in range(2)]
        scr_f = st.enter_context(nc.sbuf_tensor("scr_f", [128, T], F32))
        rows = [st.enter_context(nc.sbuf_tensor(f"rows{i}", [1, T], F32)) for i in range(4)]
        ones_bf = st.enter_context(nc.sbuf_tensor("ones_bf", [128, 1], BF16))
        ones1 = st.enter_context(nc.sbuf_tensor("ones1", [1, 128], F32))
        ones1b = st.enter_context(nc.sbuf_tensor("ones1b", [1, 128], BF16))
        eps_sb = st.enter_context(nc.sbuf_tensor("eps_sb", [1, 1], F32))
        bqk_sb = st.enter_context(nc.sbuf_tensor("bqk_sb", [128, 2 * H], F32))
        bv_sb = st.enter_context(nc.sbuf_tensor("bv_sb", [1, D], BF16))
        wqk_sb = [st.enter_context(nc.sbuf_tensor(f"wqk{p}", [128, CD * 128], BF16)) for p in range(2)]
        wv_sb = [st.enter_context(nc.sbuf_tensor(f"wv{dk}", [128, 512], BF16)) for dk in range(CD)]
        obuf = [st.enter_context(nc.sbuf_tensor(f"obuf{p}", [128, T], BF16)) for p in range(2)]
        vbuf = [st.enter_context(nc.sbuf_tensor(f"vbuf{p}", [128, 512], BF16)) for p in range(2)]
        s_x = Ctr(st, nc, "s_x")
        s_wA = [Ctr(st, nc, f"s_wA{p}") for p in range(2)]
        s_wV = Ctr(st, nc, "s_wV")
        s_pe = Ctr(st, nc, "s_pe")
        s_act = Ctr(st, nc, "s_act")
        s_v = Ctr(st, nc, "s_v")
        s_out = Ctr(st, nc, "s_out")
        with nc.Block() as block:
            @block.sync
            def _(sync):
                for dk in range(CD):
                    s_x.inc(nc.sync.dma_start(out=x_sb[:, dk * T:(dk + 1) * T], in_=x_in[dk]), 16)
                s_x.inc(nc.sync.dma_start(out=bqk_sb[:, :], in_=bqk), 16)
                s_x.inc(nc.gpsimd.dma_start(out=bv_sb[:, :], in_=bv), 16)
                s_v.inc(nc.vector.memset(ones_bf[:, :], 1.0))
                s_v.inc(nc.vector.memset(ones1[:, :], 1.0))
                s_v.inc(nc.vector.memset(ones1b[:, :], 1.0))
                s_v.inc(nc.vector.memset(eps_sb[:, :], LN_EPS))
                nc.scalar.wait_ge(s_x.h, s_x.n)
                nc.scalar.wait_ge(s_v.h, 4)
                nc.tensor.wait_ge(s_v.h, 4)
                nc.vector.wait_ge(s_x.h, s_x.n)
                with nc.psum_tensor("stat0", [1, T], F32) as st0, \
                     nc.psum_tensor("stat1", [1, T], F32) as st1:
                    emit_ln_stats(nc, s_pe, s_act, s_v, x_sb, scr, scrq, [st0, st1], rows,
                                  ones_bf, eps_sb[0:1, 0:1], 1.0 / D)
                with nc.psum_tensor("bc0", [128, T], F32) as bc0, \
                     nc.psum_tensor("bc1", [128, T], F32) as bc1:
                    emit_bcast_rows(nc, s_pe, [rows[3][0:1, :T], rows[2][0:1, :T]],
                                    [bc0, bc1], ones1, T, s_v.h, s_v.n)
                    nc.vector.wait_ge(s_pe.h, s_pe.n)
                    for dk in range(CD):
                        sl = slice(dk * T, (dk + 1) * T)
                        nc.vector.tensor_tensor(out=scr_f[:, :], in0=x_sb[:, sl], in1=bc0[:, :], op=OP.mult)
                        last = nc.vector.tensor_tensor(out=xhat[:, sl], in0=scr_f[:, :], in1=bc1[:, :], op=OP.add)
                    s_v.inc(last)
                apply_done = s_v.n
                with nc.psum_tensor("qk0", [128, T], F32) as qk0, \
                     nc.psum_tensor("qk1", [128, T], F32) as qk1:
                    qk_ps = [qk0, qk1]
                    peQ = s_pe.n
                    actQ = s_act.n
                    for oc in range(2 * H):
                        p = oc % 2
                        if oc >= 2:
                            nc.sync.wait_ge(s_pe.h, peQ + oc - 1)
                        s_wA[p].inc(nc.sync.dma_start(out=wqk_sb[p][:, :], in_=wqk[oc]), 16)
                        nc.tensor.wait_ge(s_wA[p].h, s_wA[p].n)
                        nc.tensor.wait_ge(s_v.h, apply_done)
                        if oc >= 2:
                            nc.tensor.wait_ge(s_act.h, actQ + oc - 1)
                        last = None
                        for hh in range(2):
                            ts = slice(hh * 512, (hh + 1) * 512)
                            for dk in range(CD):
                                last = nc.tensor.matmul(out=qk_ps[p][:, ts],
                                                        lhsT=wqk_sb[p][:, dk * 128:(dk + 1) * 128],
                                                        rhs=xhat[:, dk * T:(dk + 1) * T][:, ts],
                                                        start=(dk == 0), stop=(dk == CD - 1))
                        s_pe.inc(last)
                        nc.scalar.wait_ge(s_pe.h, s_pe.n)
                        if oc >= 2:
                            nc.scalar.wait_ge(s_out.h, 16 * (oc - 1))
                        s_act.inc(nc.scalar.activation(out=obuf[p][:, :], in_=qk_ps[p][:, :],
                                                       func=AF.Identity, bias=bqk_sb[:, oc:oc + 1]))
                        dst = q_out[oc] if oc < H else k_out[oc - H]
                        s_out.inc(nc.scalar.dma_start(out=dst, in_=obuf[p][:, :]), 16)
                _soft_barrier(nc, [s_pe, s_act, s_v, s_out])
                with nc.psum_tensor("v0", [128, 512], F32) as v0, \
                     nc.psum_tensor("v1", [128, 512], F32) as v1:
                    v_ps = [v0, v1]
                    outV = s_out.n
                    for vg in range(4):
                        vs = slice(vg * 512, (vg + 1) * 512)
                        if vg >= 1:
                            nc.sync.wait_ge(s_pe.h, s_pe.n)
                        for dk in range(CD):
                            s_wV.inc(nc.sync.dma_start(out=wv_sb[dk][:, :], in_=wv[dk][:, vs]), 16)
                        nc.tensor.wait_ge(s_wV.h, s_wV.n)
                        for tt in range(T // 128):
                            u = vg * 8 + tt
                            p = u % 2
                            nc.tensor.wait_ge(s_v.h, apply_done)
                            if u >= 2:
                                nc.tensor.wait_ge(s_act.h, s_act.n - 1)
                            for dk in range(CD):
                                nc.tensor.matmul(out=v_ps[p][:, :], lhsT=xhat[:, dk * T + tt * 128:dk * T + (tt + 1) * 128],
                                                 rhs=wv_sb[dk][:, :], start=(dk == 0), stop=False)
                            last = nc.tensor.matmul(out=v_ps[p][:, :], lhsT=ones1b[:, :],
                                                    rhs=bv_sb[:, vs], start=False, stop=True)
                            s_pe.inc(last)
                            nc.scalar.wait_ge(s_pe.h, s_pe.n)
                            if u >= 2:
                                nc.scalar.wait_ge(s_out.h, outV + 16 * (u - 1))
                            s_act.inc(nc.scalar.activation(out=vbuf[p][:, :], in_=v_ps[p][:, :],
                                                           func=AF.Identity))
                            s_out.inc(nc.scalar.dma_start(out=v_out[tt][:, vs], in_=vbuf[p][:, :]), 16)
                nc.sync.wait_ge(s_out.h, s_out.n)
    return nc


def build_qkv1():
    nc = bass.Bass("TRN2", target_bir_lowering=False)
    x_in = nc.dram_tensor("x", [CD, 128, T], F32, kind="ExternalInput").ap()
    wq = nc.dram_tensor("wq", [H, 128, CD * 128], BF16, kind="ExternalInput").ap()
    wk = nc.dram_tensor("wk", [H, 128, CD * 128], BF16, kind="ExternalInput").ap()
    wv = nc.dram_tensor("wv", [CD, 128, D], BF16, kind="ExternalInput").ap()
    bq = nc.dram_tensor("bq", [128, H], F32, kind="ExternalInput").ap()
    bk = nc.dram_tensor("bk", [128, H], F32, kind="ExternalInput").ap()
    bv = nc.dram_tensor("bv", [1, D], F32, kind="ExternalInput").ap()
    gq = nc.dram_tensor("gq", [128, CD], F32, kind="ExternalInput").ap()
    gkv = nc.dram_tensor("gkv", [128, CD], F32, kind="ExternalInput").ap()
    ghead = nc.dram_tensor("ghead", [128, 1], F32, kind="ExternalInput").ap()
    q_out = nc.dram_tensor("q", [H, 128, T], F16, kind="ExternalOutput").ap()
    k_out = nc.dram_tensor("k", [H, 128, T], F16, kind="ExternalOutput").ap()
    v_out = nc.dram_tensor("v", [T // 128, 128, D], BF16, kind="ExternalOutput").ap()
    with ExitStack() as st:
        x_sb = st.enter_context(nc.sbuf_tensor("x_sb", [128, CD * T], F32))
        xq = st.enter_context(nc.sbuf_tensor("xq", [128, CD * T], BF16))
        xkv = st.enter_context(nc.sbuf_tensor("xkv", [128, CD * T], BF16))
        scr = [st.enter_context(nc.sbuf_tensor(f"scr{p}", [128, T], BF16)) for p in range(2)]
        scrq = [st.enter_context(nc.sbuf_tensor(f"scrq{p}", [128, T], BF16)) for p in range(2)]
        scr_f = st.enter_context(nc.sbuf_tensor("scr_f", [128, T], F32))
        qf32 = [st.enter_context(nc.sbuf_tensor(f"qf32{p}", [128, T], F32)) for p in range(2)]
        rows = [st.enter_context(nc.sbuf_tensor(f"rows{i}", [1, T], F32)) for i in range(4)]
        ones_bf = st.enter_context(nc.sbuf_tensor("ones_bf", [128, 1], BF16))
        ones1 = st.enter_context(nc.sbuf_tensor("ones1", [1, 128], F32))
        ones1b = st.enter_context(nc.sbuf_tensor("ones1b", [1, 128], BF16))
        eps_sb = st.enter_context(nc.sbuf_tensor("eps_sb", [1, 1], F32))
        bq_sb = st.enter_context(nc.sbuf_tensor("bq_sb", [128, H], F32))
        bk_sb = st.enter_context(nc.sbuf_tensor("bk_sb", [128, H], F32))
        bv_sb = st.enter_context(nc.sbuf_tensor("bv_sb", [1, D], BF16))
        gq_sb = st.enter_context(nc.sbuf_tensor("gq_sb", [128, CD], F32))
        gkv_sb = st.enter_context(nc.sbuf_tensor("gkv_sb", [128, CD], F32))
        gh_sb = st.enter_context(nc.sbuf_tensor("gh_sb", [128, 1], F32))
        w_sb = [st.enter_context(nc.sbuf_tensor(f"w{p}", [128, CD * 128], BF16)) for p in range(2)]
        wv_sb = [st.enter_context(nc.sbuf_tensor(f"wv{dk}", [128, 512], BF16)) for dk in range(CD)]
        obuf = [st.enter_context(nc.sbuf_tensor(f"obuf{p}", [128, T], F16)) for p in range(2)]
        vbuf = [st.enter_context(nc.sbuf_tensor(f"vbuf{p}", [128, 512], BF16)) for p in range(2)]
        s_x = Ctr(st, nc, "s_x")
        s_wA = [Ctr(st, nc, f"s_wA{p}") for p in range(2)]
        s_wV = Ctr(st, nc, "s_wV")
        s_pe = Ctr(st, nc, "s_pe")
        s_act = Ctr(st, nc, "s_act")
        s_v = Ctr(st, nc, "s_v")
        s_out = Ctr(st, nc, "s_out")
        with nc.Block() as block:
            @block.sync
            def _(sync):
                for dk in range(CD):
                    s_x.inc(nc.sync.dma_start(out=x_sb[:, dk * T:(dk + 1) * T], in_=x_in[dk]), 16)
                for name, dst in ((bq, bq_sb), (bk, bk_sb), (gq, gq_sb), (gkv, gkv_sb), (ghead, gh_sb)):
                    s_x.inc(nc.sync.dma_start(out=dst[:, :], in_=name), 16)
                s_x.inc(nc.gpsimd.dma_start(out=bv_sb[:, :], in_=bv), 16)
                s_v.inc(nc.vector.memset(ones_bf[:, :], 1.0))
                s_v.inc(nc.vector.memset(ones1[:, :], 1.0))
                s_v.inc(nc.vector.memset(ones1b[:, :], 1.0))
                s_v.inc(nc.vector.memset(eps_sb[:, :], RMS_EPS))
                nc.scalar.wait_ge(s_x.h, s_x.n)
                nc.scalar.wait_ge(s_v.h, 4)
                nc.tensor.wait_ge(s_v.h, 4)
                nc.vector.wait_ge(s_x.h, s_x.n)
                with nc.psum_tensor("stat0", [1, T], F32) as st0, \
                     nc.psum_tensor("stat1", [1, T], F32) as st1:
                    emit_ln_stats(nc, s_pe, s_act, s_v, x_sb, scr, scrq, [st0, st1], rows,
                                  ones_bf, eps_sb[0:1, 0:1], 1.0 / D, rms=True)
                with nc.psum_tensor("bc0", [128, T], F32) as bc0:
                    emit_bcast_rows(nc, s_pe, [rows[3][0:1, :T]], [bc0], ones1, T, s_act.h, s_act.n)
                    nc.vector.wait_ge(s_pe.h, s_pe.n)
                    for dk in range(CD):
                        sl = slice(dk * T, (dk + 1) * T)
                        nc.vector.tensor_tensor(out=scr_f[:, :], in0=x_sb[:, sl], in1=bc0[:, :], op=OP.mult)
                        nc.vector.tensor_scalar(out=xq[:, sl], in0=scr_f[:, :], scalar1=gq_sb[:, dk:dk + 1],
                                                scalar2=None, op0=OP.mult)
                        last = nc.vector.tensor_scalar(out=xkv[:, sl], in0=scr_f[:, :],
                                                       scalar1=gkv_sb[:, dk:dk + 1], scalar2=None, op0=OP.mult)
                    s_v.inc(last)
                apply_done = s_v.n
                for which, (wsrc, bias_sb, xin, dst_t) in enumerate(
                        ((wq, bq_sb, xq, q_out), (wk, bk_sb, xkv, k_out))):
                    _soft_barrier(nc, [s_pe, s_act, s_v, s_out])
                    with nc.psum_tensor(f"qk0_{which}", [128, T], F32) as qk0, \
                         nc.psum_tensor(f"qk1_{which}", [128, T], F32) as qk1, \
                         nc.psum_tensor(f"hs_{which}", [1, T], F32) as hs_ps, \
                         nc.psum_tensor(f"hb_{which}", [128, T], F32) as hb_ps:
                        qk_ps = [qk0, qk1]
                        peQ = s_pe.n
                        evQ = s_v.n
                        actQ = s_act.n
                        outQ = s_out.n
                        for oc in range(H):
                            p = oc % 2
                            if oc >= 2:
                                nc.sync.wait_ge(s_pe.h, peQ + 5 * (oc - 2) + 1)
                            s_wA[p].inc(nc.sync.dma_start(out=w_sb[p][:, :], in_=wsrc[oc]), 16)
                            nc.tensor.wait_ge(s_wA[p].h, s_wA[p].n)
                            nc.tensor.wait_ge(s_v.h, apply_done)
                            if oc >= 2:
                                nc.tensor.wait_ge(s_act.h, actQ + 4 * (oc - 2) + 1)
                            last = None
                            for hh in range(2):
                                ts = slice(hh * 512, (hh + 1) * 512)
                                for dk in range(CD):
                                    last = nc.tensor.matmul(out=qk_ps[p][:, ts],
                                                            lhsT=w_sb[p][:, dk * 128:(dk + 1) * 128],
                                                            rhs=xin[:, dk * T:(dk + 1) * T][:, ts],
                                                            start=(dk == 0), stop=(dk == CD - 1))
                            s_pe.inc(last)
                            nc.scalar.wait_ge(s_pe.h, s_pe.n)
                            if oc >= 2:
                                nc.scalar.wait_ge(s_v.h, evQ + (oc - 2) + 1)
                            s_act.inc(nc.scalar.activation(out=qf32[p][:, :], in_=qk_ps[p][:, :],
                                                           func=AF.Identity, bias=bias_sb[:, oc:oc + 1]))
                            s_act.inc(nc.scalar.activation(out=scr[p][:, :], in_=qf32[p][:, :], func=AF.Square))
                            nc.tensor.wait_ge(s_act.h, s_act.n)
                            for hh in range(2):
                                ts = slice(hh * 512, (hh + 1) * 512)
                                s_pe.inc(nc.tensor.matmul(out=hs_ps[:, ts], lhsT=ones_bf[:, :],
                                                          rhs=scr[p][:, ts], start=True, stop=True))
                            nc.scalar.wait_ge(s_pe.h, s_pe.n)
                            s_act.inc(nc.scalar.activation(out=rows[0][0:1, :T], in_=hs_ps[0:1, :T],
                                                           func=AF.Ln, bias=eps_sb[0:1, 0:1], scale=1.0 / HD))
                            s_act.inc(nc.scalar.activation(out=rows[0][0:1, :T], in_=rows[0][0:1, :T],
                                                           func=AF.Exp, scale=-0.5))
                            nc.tensor.wait_ge(s_act.h, s_act.n)
                            if oc >= 1:
                                nc.tensor.wait_ge(s_v.h, evQ + oc)
                            for hh in range(2):
                                ts = slice(hh * 512, (hh + 1) * 512)
                                s_pe.inc(nc.tensor.matmul(out=hb_ps[:, ts], lhsT=ones1[:, :],
                                                          rhs=rows[0][0:1, :T][:, ts], start=True, stop=True))
                            nc.vector.wait_ge(s_pe.h, s_pe.n)
                            if oc >= 2:
                                nc.vector.wait_ge(s_out.h, outQ + 16 * (oc - 1))
                            s_v.inc(nc.vector.scalar_tensor_tensor(out=obuf[p][:, :], in0=qf32[p][:, :],
                                                                   scalar=gh_sb[:, 0:1], in1=hb_ps[:, :],
                                                                   op0=OP.mult, op1=OP.mult))
                            nc.scalar.wait_ge(s_v.h, s_v.n)
                            s_out.inc(nc.scalar.dma_start(out=dst_t[oc], in_=obuf[p][:, :]), 16)
                _soft_barrier(nc, [s_pe, s_act, s_v, s_out])
                with nc.psum_tensor("v0", [128, 512], F32) as v0, \
                     nc.psum_tensor("v1", [128, 512], F32) as v1:
                    v_ps = [v0, v1]
                    outV = s_out.n
                    for vg in range(4):
                        vs = slice(vg * 512, (vg + 1) * 512)
                        if vg >= 1:
                            nc.sync.wait_ge(s_pe.h, s_pe.n)
                        for dk in range(CD):
                            s_wV.inc(nc.sync.dma_start(out=wv_sb[dk][:, :], in_=wv[dk][:, vs]), 16)
                        nc.tensor.wait_ge(s_wV.h, s_wV.n)
                        for tt in range(T // 128):
                            u = vg * 8 + tt
                            p = u % 2
                            nc.tensor.wait_ge(s_v.h, apply_done)
                            if u >= 2:
                                nc.tensor.wait_ge(s_act.h, s_act.n - 1)
                            for dk in range(CD):
                                nc.tensor.matmul(out=v_ps[p][:, :], lhsT=xkv[:, dk * T + tt * 128:dk * T + (tt + 1) * 128],
                                                 rhs=wv_sb[dk][:, :], start=(dk == 0), stop=False)
                            last = nc.tensor.matmul(out=v_ps[p][:, :], lhsT=ones1b[:, :],
                                                    rhs=bv_sb[:, vs], start=False, stop=True)
                            s_pe.inc(last)
                            nc.scalar.wait_ge(s_pe.h, s_pe.n)
                            if u >= 2:
                                nc.scalar.wait_ge(s_out.h, outV + 16 * (u - 1))
                            s_act.inc(nc.scalar.activation(out=vbuf[p][:, :], in_=v_ps[p][:, :],
                                                           func=AF.Identity))
                            s_out.inc(nc.scalar.dma_start(out=v_out[tt][:, vs], in_=vbuf[p][:, :]), 16)
                nc.sync.wait_ge(s_out.h, s_out.n)
    return nc



def build_attn(causal, qk_dt=BF16):
    nc = bass.Bass("TRN2", target_bir_lowering=False)
    TK = 2048
    x_in = nc.dram_tensor("x", [CD, 128, T], F32, kind="ExternalInput").ap()
    q_in = nc.dram_tensor("q", [H, 128, T], qk_dt, kind="ExternalInput").ap()
    k_in = nc.dram_tensor("k", [H, 128, TK], qk_dt, kind="ExternalInput").ap()
    v_in = nc.dram_tensor("v", [H, 128, TK], BF16, kind="ExternalInput").ap()
    wo = nc.dram_tensor("wo", [CD, 128, CD * 128], BF16, kind="ExternalInput").ap()
    bo = nc.dram_tensor("bo", [128, CD], F32, kind="ExternalInput").ap()
    qpos = nc.dram_tensor("qpos", [1, T], F32, kind="ExternalInput").ap()
    x_out = nc.dram_tensor("x_out", [CD, 128, T], F32, kind="ExternalOutput").ap()
    NKC = TK // 128
    with ExitStack() as st:
        x_sb = st.enter_context(nc.sbuf_tensor("x_sb", [128, CD * T], F32))
        q_sb = st.enter_context(nc.sbuf_tensor("q_sb", [128, H * T], qk_dt))
        pbuf = st.enter_context(nc.sbuf_tensor("pbuf", [128, NKC * T], BF16))
        attn_sb = st.enter_context(nc.sbuf_tensor("attn_sb", [128, H * T], BF16))
        kbuf = [st.enter_context(nc.sbuf_tensor(f"kbuf{p}", [128, TK], qk_dt)) for p in range(2)]
        vbuf = [st.enter_context(nc.sbuf_tensor(f"vbuf{p}", [128, TK], BF16)) for p in range(2)]
        av_sb = [st.enter_context(nc.sbuf_tensor(f"av_sb{p}", [128, T], BF16)) for p in range(2)]
        mq2row = st.enter_context(nc.sbuf_tensor("mq2row", [1, H], F32))
        mk2row = st.enter_context(nc.sbuf_tensor("mk2row", [1, H], F32))
        negb_cols = st.enter_context(nc.sbuf_tensor("negb_cols", [128, H], F32))
        qgb = st.enter_context(nc.sbuf_tensor("qgb", [128, T], F16))
        kcols = st.enter_context(nc.sbuf_tensor("kcols", [128, NKC], F32))
        kcols_i = st.enter_context(nc.sbuf_tensor("kcols_i", [128, NKC], I32))
        rows = [st.enter_context(nc.sbuf_tensor(f"rows{i}", [1, T], F32)) for i in range(2)]
        r2 = st.enter_context(nc.sbuf_tensor("r2", [1, 512], F32))
        sc1 = [st.enter_context(nc.sbuf_tensor(f"sc1_{i}", [1, 1], F32)) for i in range(4)]
        ones_bf = st.enter_context(nc.sbuf_tensor("ones_bf", [128, 1], BF16))
        ones1 = st.enter_context(nc.sbuf_tensor("ones1", [1, 128], F32))
        bo_sb = st.enter_context(nc.sbuf_tensor("bo_sb", [128, CD], F32))
        wo_sb = [st.enter_context(nc.sbuf_tensor(f"wo{p}", [128, CD * 128], BF16)) for p in range(2)]
        s_x = Ctr(st, nc, "s_x")
        s_k = [Ctr(st, nc, f"s_k{p}") for p in range(2)]
        s_vv = [Ctr(st, nc, f"s_vv{p}") for p in range(2)]
        s_wO = [Ctr(st, nc, f"s_wO{p}") for p in range(2)]
        s_pe = Ctr(st, nc, "s_pe")
        s_act = Ctr(st, nc, "s_act")
        s_v = Ctr(st, nc, "s_v")
        s_out = Ctr(st, nc, "s_out")
        s_hd = Ctr(st, nc, "s_hd")
        with nc.Block() as block:
            @block.sync
            def _(sync):
                for dk in range(CD):
                    s_x.inc(nc.sync.dma_start(out=x_sb[:, dk * T:(dk + 1) * T], in_=x_in[dk]), 16)
                for hh in range(H):
                    s_x.inc(nc.sync.dma_start(out=q_sb[:, hh * T:(hh + 1) * T], in_=q_in[hh]), 16)
                s_x.inc(nc.sync.dma_start(out=bo_sb[:, :], in_=bo), 16)
                s_x.inc(nc.sync.dma_start(out=rows[0][0:1, :], in_=qpos), 16)
                s_v.inc(nc.vector.memset(ones_bf[:, :], 1.0))
                s_v.inc(nc.vector.memset(ones1[:, :], 1.0))
                nc.gpsimd.iota(out=kcols_i[:, :], pattern=[[128, NKC]], base=0,
                               channel_multiplier=1).then_inc(s_v.h, 1)
                s_v.n += 1
                nc.vector.wait_ge(s_v.h, 3)
                nc.vector.wait_ge(s_x.h, s_x.n)
                s_v.inc(nc.vector.tensor_copy(out=kcols[:, :], in_=kcols_i[:, :]))
                with nc.psum_tensor("qg_ps", [128, T], F32) as qg_ps:
                    nc.tensor.wait_ge(s_v.h, 2)
                    nc.tensor.wait_ge(s_x.h, s_x.n)
                    for hh in range(2):
                        ts = slice(hh * 512, (hh + 1) * 512)
                        s_pe.inc(nc.tensor.matmul(out=qg_ps[:, ts], lhsT=ones1[:, :],
                                                  rhs=rows[0][0:1, :][:, ts], start=True, stop=True))
                    nc.scalar.wait_ge(s_pe.h, s_pe.n)
                    s_act.inc(nc.scalar.activation(out=qgb[:, :], in_=qg_ps[:, :], func=AF.Identity))
                with nc.psum_tensor("sc0", [128, 512], F32) as sc0, \
                     nc.psum_tensor("sc1p", [128, 512], F32) as sc1p, \
                     nc.psum_tensor("av", [128, T], F32) as av_ps, \
                     nc.psum_tensor("sums", [1, T], F32) as sum_ps, \
                     nc.psum_tensor("rb", [128, T], F32) as rb_ps:
                    sc_ps = [sc0, sc1p]
                    nc.scalar.wait_ge(s_x.h, s_x.n)
                    s_act.inc(nc.scalar.activation(out=pbuf[:, :H * T], in_=q_sb[:, :], func=AF.Square))
                    red_marks = []
                    for h in range(H):
                        reg = sum_ps[0:1, :T] if h % 2 == 0 else rb_ps[0:1, :T]
                        nc.tensor.wait_ge(s_act.h, s_act.n)
                        if h >= 2:
                            nc.tensor.wait_ge(s_v.h, red_marks[h - 2])
                        for hh in range(2):
                            cs = slice(hh * 512, (hh + 1) * 512)
                            s_pe.inc(nc.tensor.matmul(out=reg[:, cs], lhsT=ones_bf[:, :],
                                                      rhs=pbuf[:, h * T:(h + 1) * T][:, cs],
                                                      start=True, stop=True))
                        nc.vector.wait_ge(s_pe.h, s_pe.n)
                        s_v.inc(nc.vector.reduce_max(out=mq2row[0:1, h:h + 1], in_=reg, axis=AX.X))
                        red_marks.append(s_v.n)
                    nc.scalar.wait_ge(s_pe.h, s_pe.n)
                    for h in range(H):
                        p = h % 2
                        if h >= 2:
                            nc.sync.wait_ge(s_v.h, s_v.n - 4)
                        s_k[p].inc(nc.sync.dma_start(out=kbuf[p][:, :], in_=k_in[h]), 16)
                        nc.scalar.wait_ge(s_k[p].h, s_k[p].n)
                        s_act.inc(nc.scalar.activation(out=pbuf[:, p * TK:(p + 1) * TK],
                                                       in_=kbuf[p][:, :], func=AF.Square))
                        nc.tensor.wait_ge(s_act.h, s_act.n)
                        if h >= 1:
                            nc.tensor.wait_ge(s_v.h, s_v.n)
                        for ks in range(4):
                            dst = sum_ps if ks < 2 else rb_ps
                            cs = slice((ks % 2) * 512, (ks % 2) * 512 + 512)
                            s_pe.inc(nc.tensor.matmul(out=dst[0:1, cs], lhsT=ones_bf[:, :],
                                                      rhs=pbuf[:, p * TK + ks * 512:p * TK + (ks + 1) * 512],
                                                      start=True, stop=True))
                        nc.vector.wait_ge(s_pe.h, s_pe.n)
                        s_v.inc(nc.vector.reduce_max(out=sc1[0][0:1, :], in_=sum_ps[0:1, :], axis=AX.X))
                        s_v.inc(nc.vector.reduce_max(out=sc1[3][0:1, :], in_=rb_ps[0:1, :T], axis=AX.X))
                        s_v.inc(nc.vector.tensor_tensor(out=mk2row[0:1, h:h + 1], in0=sc1[0][0:1, :],
                                                        in1=sc1[3][0:1, :], op=OP.max))
                    nc.scalar.wait_ge(s_v.h, s_v.n)
                    s_act.inc(nc.scalar.activation(out=mq2row[0:1, :], in_=mq2row[0:1, :], func=AF.Sqrt))
                    s_act.inc(nc.scalar.activation(out=mk2row[0:1, :], in_=mk2row[0:1, :], func=AF.Sqrt))
                    nc.vector.wait_ge(s_act.h, s_act.n)
                    s_v.inc(nc.vector.tensor_tensor(out=mq2row[0:1, :], in0=mq2row[0:1, :],
                                                    in1=mk2row[0:1, :], op=OP.mult))
                    s_v.inc(nc.vector.tensor_scalar(out=mq2row[0:1, :], in0=mq2row[0:1, :],
                                                    scalar1=-INVSQRT_HD, scalar2=None, op0=OP.mult))
                    nc.tensor.wait_ge(s_v.h, s_v.n)
                    s_pe.inc(nc.tensor.matmul(out=rb_ps[:, 0:H], lhsT=ones1[:, :],
                                              rhs=mq2row[0:1, :], start=True, stop=True))
                    nc.scalar.wait_ge(s_pe.h, s_pe.n)
                    s_act.inc(nc.scalar.activation(out=negb_cols[:, :], in_=rb_ps[:, 0:H], func=AF.Identity))
                    nc.tensor.wait_ge(s_act.h, s_act.n)
                    nc.tensor.wait_ge(s_v.h, s_v.n)
                    for h in range(H):
                        p = h % 2
                        if h >= 2:
                            nc.sync.wait_ge(s_hd.h, h - 1)
                        s_k[p].inc(nc.sync.dma_start(out=kbuf[p][:, :], in_=k_in[h]), 16)
                        s_vv[p].inc(nc.sync.dma_start(out=vbuf[p][:, :], in_=v_in[h]), 16)
                        negb = negb_cols[:, h:h + 1]
                        bound_act = s_act.n
                        nc.tensor.wait_ge(s_vv[p].h, s_vv[p].n)
                        nc.tensor.wait_ge(s_k[p].h, s_k[p].n)
                        units = [(kc, qh) for kc in range(NKC) for qh in range(2)]

                        def _sums_av(u):
                            kc, qh = units[u]
                            ts = slice(qh * 512, (qh + 1) * 512)
                            psl = slice(kc * T + qh * 512, kc * T + (qh + 1) * 512)
                            s_pe.inc(nc.tensor.matmul(out=sum_ps[0:1, ts], lhsT=ones_bf[:, :],
                                                      rhs=pbuf[:, psl], start=(kc == 0),
                                                      stop=(kc == NKC - 1)))
                            s_pe.inc(nc.tensor.matmul(out=av_ps[:, ts],
                                                      lhsT=vbuf[p][:, kc * 128:(kc + 1) * 128],
                                                      rhs=pbuf[:, psl], start=(kc == 0),
                                                      stop=(kc == NKC - 1)))

                        mask_marks = []
                        for u, (kc, qh) in enumerate(units):
                            up = u % 2
                            ts = slice(qh * 512, (qh + 1) * 512)
                            psl = slice(kc * T + qh * 512, kc * T + (qh + 1) * 512)
                            if u >= 2:
                                nc.tensor.wait_ge(s_act.h, bound_act + (u - 1))
                            s_pe.inc(nc.tensor.matmul(out=sc_ps[up][:, :],
                                                      lhsT=kbuf[p][:, kc * 128:(kc + 1) * 128],
                                                      rhs=q_sb[:, h * T:(h + 1) * T][:, ts],
                                                      start=True, stop=True))
                            nc.scalar.wait_ge(s_pe.h, s_pe.n)
                            s_act.inc(nc.scalar.activation(out=pbuf[:, psl], in_=sc_ps[up][:, :],
                                                           func=AF.Exp, bias=negb,
                                                           scale=INVSQRT_HD))
                            if causal:
                                nc.vector.wait_ge(s_act.h, s_act.n)
                                s_v.inc(nc.vector.scalar_tensor_tensor(
                                    out=pbuf[:, psl], in0=qgb[:, ts], scalar=kcols[:, kc:kc + 1],
                                    in1=pbuf[:, psl], op0=OP.is_ge, op1=OP.mult))
                            mask_marks.append(s_v.n if causal else s_act.n)
                            if u >= 1:
                                if causal:
                                    nc.tensor.wait_ge(s_v.h, mask_marks[u - 1])
                                else:
                                    nc.tensor.wait_ge(s_act.h, mask_marks[u - 1])
                                _sums_av(u - 1)
                        if causal:
                            nc.tensor.wait_ge(s_v.h, mask_marks[-1])
                        else:
                            nc.tensor.wait_ge(s_act.h, mask_marks[-1])
                        _sums_av(len(units) - 1)
                        nc.vector.wait_ge(s_pe.h, s_pe.n)
                        s_v.inc(nc.vector.tensor_copy(out=rows[1][0:1, :], in_=sum_ps[0:1, :]))
                        s_v.inc(nc.vector.reciprocal(out=rows[1][0:1, :], in_=rows[1][0:1, :]))
                        nc.tensor.wait_ge(s_v.h, s_v.n)
                        for hh in range(2):
                            ts = slice(hh * 512, (hh + 1) * 512)
                            s_pe.inc(nc.tensor.matmul(out=rb_ps[:, ts], lhsT=ones1[:, :],
                                                      rhs=rows[1][0:1, :][:, ts], start=True, stop=True))
                        nc.scalar.wait_ge(s_pe.h, s_pe.n)
                        s_act.inc(nc.scalar.activation(out=av_sb[p][:, :], in_=av_ps[:, :], func=AF.Identity))
                        nc.vector.wait_ge(s_act.h, s_act.n)
                        s_v.inc(nc.vector.tensor_tensor(out=attn_sb[:, h * T:(h + 1) * T],
                                                        in0=av_sb[p][:, :], in1=rb_ps[:, :], op=OP.mult))
                        s_hd.inc(nc.vector.engine_nop())
                _soft_barrier(nc, [s_pe, s_act, s_v, s_out])
                with nc.psum_tensor("o0", [128, T], F32) as o0, \
                     nc.psum_tensor("o1", [128, T], F32) as o1:
                    o_ps = [o0, o1]
                    attn_done = s_v.n
                    peO = s_pe.n
                    evO_base = s_v.n
                    for oc in range(CD):
                        p = oc % 2
                        if oc >= 2:
                            nc.sync.wait_ge(s_pe.h, peO + oc - 1)
                        s_wO[p].inc(nc.sync.dma_start(out=wo_sb[p][:, :], in_=wo[oc]), 16)
                        nc.tensor.wait_ge(s_wO[p].h, s_wO[p].n)
                        nc.tensor.wait_ge(s_v.h, attn_done)
                        if oc >= 2:
                            nc.tensor.wait_ge(s_v.h, evO_base + oc - 1)
                        last = None
                        for hh in range(2):
                            ts = slice(hh * 512, (hh + 1) * 512)
                            for dk in range(CD):
                                last = nc.tensor.matmul(out=o_ps[p][:, ts],
                                                        lhsT=wo_sb[p][:, dk * 128:(dk + 1) * 128],
                                                        rhs=attn_sb[:, dk * T:(dk + 1) * T][:, ts],
                                                        start=(dk == 0), stop=(dk == CD - 1))
                        s_pe.inc(last)
                        nc.vector.wait_ge(s_pe.h, s_pe.n)
                        s_v.inc(nc.vector.scalar_tensor_tensor(
                            out=x_sb[:, oc * T:(oc + 1) * T], in0=o_ps[p][:, :], scalar=bo_sb[:, oc:oc + 1],
                            in1=x_sb[:, oc * T:(oc + 1) * T], op0=OP.add, op1=OP.add))
                nc.sync.wait_ge(s_v.h, s_v.n)
                for dk in range(CD):
                    s_out.inc(nc.sync.dma_start(out=x_out[dk], in_=x_sb[:, dk * T:(dk + 1) * T]), 16)
                nc.sync.wait_ge(s_out.h, s_out.n)
    return nc



def build_wcast():
    nc = bass.Bass("TRN2", target_bir_lowering=False)
    wc_in = nc.dram_tensor("wc_in", [32, 128, CD * 128], F32, kind="ExternalInput").ap()
    wc_out = nc.dram_tensor("wc_out", [32, 128, CD * 128], BF16, kind="ExternalOutput").ap()
    with ExitStack() as st:
        buf = [st.enter_context(nc.sbuf_tensor(f"b{p}", [128, CD * 128], BF16)) for p in range(2)]
        s_in = [Ctr(st, nc, f"s_in{p}") for p in range(2)]
        s_out = [Ctr(st, nc, f"s_out{p}") for p in range(2)]
        with nc.Block() as block:
            @block.sync
            def _(sync):
                for u in range(32):
                    p = u % 2
                    if u >= 2:
                        nc.gpsimd.wait_ge(s_out[p].h, s_out[p].n)
                    s_in[p].inc(nc.gpsimd.dma_start(out=buf[p][:, :], in_=wc_in[u]), 16)
                    nc.sync.wait_ge(s_in[p].h, s_in[p].n)
                    s_out[p].inc(nc.sync.dma_start(out=wc_out[u], in_=buf[p][:, :]), 16)
                nc.sync.wait_ge(s_out[0].h, s_out[0].n)
                nc.sync.wait_ge(s_out[1].h, s_out[1].n)
    return nc


def _wcast_host(arrs):
    cat = np.ascontiguousarray(np.concatenate(arrs, axis=0), dtype=np.float32)
    n = cat.shape[0]
    assert n == 256, n
    per = n // NCORES
    in_maps = [{"wc_in": np.ascontiguousarray(cat[c * per:(c + 1) * per])} for c in range(NCORES)]
    res = _run("wcast", build_wcast, in_maps)
    out = np.concatenate([r["wc_out"] for r in res], axis=0)
    outs = []
    off = 0
    for a in arrs:
        outs.append(np.ascontiguousarray(out[off:off + a.shape[0]]))
        off += a.shape[0]
    return outs



def build_prep():
    nc = bass.Bass("TRN2", target_bir_lowering=False)
    gu_idx = nc.dram_tensor("gu_idx", [12, 128, CD * 128], I32, kind="ExternalInput").ap()
    dn_idx = nc.dram_tensor("dn_idx", [2, 128, C0 * 128], I32, kind="ExternalInput").ap()
    lut = nc.dram_tensor("lut", [1, 16], F32, kind="ExternalInput").ap()
    w2t = nc.dram_tensor("w2t", [C1, 128, 256], F32, kind="ExternalInput").ap()
    h128 = nc.dram_tensor("h128", [128, 128], F32, kind="ExternalInput").ap()
    wc_in = nc.dram_tensor("wc_in", [32, 128, CD * 128], F32, kind="ExternalInput").ap()
    gu_w = nc.dram_tensor("gu_w", [12, 128, CD * 128], BF16, kind="ExternalOutput").ap()
    dn_w = nc.dram_tensor("dn_w", [2, 128, C0 * 128], BF16, kind="ExternalOutput").ap()
    w2f = nc.dram_tensor("w2f", [C1, 128, 256], BF16, kind="ExternalOutput").ap()
    wc_out = nc.dram_tensor("wc_out", [32, 128, CD * 128], BF16, kind="ExternalOutput").ap()
    WMAX = C0 * 128
    W = 256
    with ExitStack() as st:
        idx_sb = [st.enter_context(nc.sbuf_tensor(f"idx{p}", [128, WMAX], I32)) for p in range(2)]
        idxb = st.enter_context(nc.sbuf_tensor("idxb", [128, WMAX], BF16))
        w_bf = [st.enter_context(nc.sbuf_tensor(f"wbf{p}", [128, WMAX], BF16)) for p in range(2)]
        msk = st.enter_context(nc.sbuf_tensor("msk", [128, WMAX], BF16))
        lut_sb = st.enter_context(nc.sbuf_tensor("lut_sb", [1, 16], F32))
        lut_col = st.enter_context(nc.sbuf_tensor("lut_col", [128, 16], F32))
        ones1 = st.enter_context(nc.sbuf_tensor("ones1", [1, 128], F32))
        lut_ps = st.enter_context(nc.psum_tensor("lut_ps", [128, 16], F32))
        s_lut = Ctr(st, nc, "s_lut")
        s_in = [Ctr(st, nc, f"s_in{p}") for p in range(2)]
        s_pe = Ctr(st, nc, "s_pe")
        s_v = Ctr(st, nc, "s_v")
        s_out = Ctr(st, nc, "s_out")
        f_abf = st.enter_context(nc.sbuf_tensor("f_abf", [128, C1 * 128], BF16))
        f_a = st.enter_context(nc.sbuf_tensor("f_a", [128, C1 * 128], F32))
        f_b = st.enter_context(nc.sbuf_tensor("f_b", [128, C1 * 128], F32))
        h_sb = st.enter_context(nc.sbuf_tensor("h_sb", [128, 128], BF16))
        f_ps = [st.enter_context(nc.psum_tensor(f"f_ps{p}", [128, 128], F32)) for p in range(2)]
        f_in = Ctr(st, nc, "f_in")
        f_pe = Ctr(st, nc, "f_pe")
        f_a_c = Ctr(st, nc, "f_a_c")
        f_out = Ctr(st, nc, "f_out")
        c_buf = [st.enter_context(nc.sbuf_tensor(f"c_b{p}", [128, CD * 128], BF16)) for p in range(2)]
        c_in = [Ctr(st, nc, f"c_in{p}") for p in range(2)]
        c_out = [Ctr(st, nc, f"c_out{p}") for p in range(2)]
        with nc.Block() as block:
            @block.sync
            def _(sync):
                for u in range(32):
                    p = u % 2
                    if u >= 2:
                        nc.gpsimd.wait_ge(c_out[p].h, c_out[p].n)
                    c_in[p].inc(nc.gpsimd.dma_start(out=c_buf[p][:, :], in_=wc_in[u]), 16)
                    nc.scalar.wait_ge(c_in[p].h, c_in[p].n)
                    c_out[p].inc(nc.scalar.dma_start(out=wc_out[u], in_=c_buf[p][:, :]), 16)
                f_in.inc(nc.gpsimd.dma_start(out=h_sb[:, :], in_=h128), 16)
                FW = 128
                for sl in range(2):
                    co = sl * FW
                    if sl >= 1:
                        nc.gpsimd.wait_ge(f_pe.h, f_pe.n)
                    for c in range(C1):
                        f_in.inc(nc.gpsimd.dma_start(out=f_abf[:, c * FW:(c + 1) * FW],
                                                     in_=w2t[c][:, co:co + FW]), 16)
                    nc.tensor.wait_ge(f_in.h, f_in.n)
                    pe0 = f_pe.n
                    ac0 = f_a_c.n
                    for c in range(C1):
                        p = c % 2
                        if c >= 2:
                            nc.tensor.wait_ge(f_a_c.h, ac0 + c - 1)
                        f_pe.inc(nc.tensor.matmul(out=f_ps[p][:, 0:FW], lhsT=h_sb[:, :],
                                                  rhs=f_abf[:, c * FW:(c + 1) * FW], start=True, stop=True))
                        nc.scalar.wait_ge(f_pe.h, f_pe.n)
                        f_a_c.inc(nc.scalar.activation(out=f_b[:, c * FW:(c + 1) * FW], in_=f_ps[p][:, 0:FW],
                                                       func=AF.Identity, scale=1.0 / math.sqrt(I1)))
                    nc.vector.wait_ge(f_a_c.h, f_a_c.n)
                    if sl >= 1:
                        nc.vector.wait_ge(f_out.h, 16 * C1)
                    fsrc, fdst = f_b, f_a
                    last = None
                    for s in range(6):
                        step = 1 << s
                        for c in range(C1):
                            if c & step:
                                continue
                            cc = c + step
                            nc.vector.tensor_tensor(out=fdst[:, c * FW:(c + 1) * FW],
                                                    in0=fsrc[:, c * FW:(c + 1) * FW],
                                                    in1=fsrc[:, cc * FW:(cc + 1) * FW], op=OP.add)
                            last = nc.vector.tensor_tensor(out=fdst[:, cc * FW:(cc + 1) * FW],
                                                           in0=fsrc[:, c * FW:(c + 1) * FW],
                                                           in1=fsrc[:, cc * FW:(cc + 1) * FW],
                                                           op=OP.subtract)
                        fsrc, fdst = fdst, fsrc
                    f_a_c.inc(last)
                    nc.gpsimd.wait_ge(f_a_c.h, f_a_c.n)
                    for c in range(C1):
                        f_out.inc(nc.gpsimd.dma_start(out=w2f[c][:, co:co + FW],
                                                      in_=fsrc[:, c * FW:(c + 1) * FW]), 16)
                s_lut.inc(nc.sync.dma_start(out=lut_sb[:, :], in_=lut), 16)
                nc.vector.wait_ge(s_lut.h, 16)
                s_v.inc(nc.vector.memset(ones1[:, :], 1.0))
                nc.tensor.wait_ge(s_v.h, 1)
                s_pe.inc(nc.tensor.matmul(out=lut_ps[:, :], lhsT=ones1[:, :], rhs=lut_sb[:, :],
                                          start=True, stop=True))
                nc.scalar.wait_ge(s_pe.h, s_pe.n)
                s_v.inc(nc.scalar.copy(out=lut_col[:, :], in_=lut_ps[:, :]))
                units = [(gu_idx, gu_w, CD * 128)] * 12 + [(dn_idx, dn_w, C0 * 128)] * 2
                for ui, (usrc, udst, width) in enumerate(units):
                    u = ui if ui < 12 else ui - 12
                    p = ui % 2
                    if ui >= 2:
                        nc.sync.wait_ge(s_v.h, 2 + (ui - 2) + 1)
                    s_in[p].inc(nc.sync.dma_start(out=idx_sb[p][:, :width], in_=usrc[u]), 16)
                    nc.vector.wait_ge(s_in[p].h, s_in[p].n)
                    nc.vector.wait_ge(s_v.h, 2)
                    if ui >= 2:
                        nc.vector.wait_ge(s_out.h, 16 * (ui - 1))
                    nc.vector.tensor_copy(out=idxb[:, :width], in_=idx_sb[p][:, :width])
                    last = None
                    for k in range(16):
                        if k == 0:
                            nc.vector.tensor_scalar(out=w_bf[p][:, :width], in0=idxb[:, :width],
                                                    scalar1=0.0, scalar2=lut_col[:, 0:1],
                                                    op0=OP.is_equal, op1=OP.mult)
                        else:
                            nc.vector.tensor_scalar(out=msk[:, :width], in0=idxb[:, :width],
                                                    scalar1=float(k), scalar2=lut_col[:, k:k + 1],
                                                    op0=OP.is_equal, op1=OP.mult)
                            last = nc.vector.tensor_tensor(out=w_bf[p][:, :width],
                                                           in0=w_bf[p][:, :width],
                                                           in1=msk[:, :width], op=OP.add)
                    s_v.inc(last)
                    nc.gpsimd.wait_ge(s_v.h, 2 + ui + 1)
                    s_out.inc(nc.gpsimd.dma_start(out=udst[u], in_=w_bf[p][:, :width]), 16)
                nc.gpsimd.wait_ge(s_out.h, 16 * 14)
                nc.gpsimd.wait_ge(f_out.h, 16 * 2 * C1)
                nc.gpsimd.wait_ge(c_out[0].h, c_out[0].n)
                nc.gpsimd.wait_ge(c_out[1].h, c_out[1].n)
    return nc


def _prep_host(lut, gate_idx, up_idx, down_idx, W2, cast_arrs):
    gidxT = np.ascontiguousarray(gate_idx.T)
    uidxT = np.ascontiguousarray(up_idx.T)
    didxT = np.ascontiguousarray(down_idx.T)
    gu = np.concatenate([_pw(gidxT, pad_oc=C0P), _pw(uidxT, pad_oc=C0P)], axis=0)
    dn = _pw(didxT)
    lut2 = np.ascontiguousarray(np.asarray(lut, dtype=np.float32).reshape(1, 16))
    w2t = np.ascontiguousarray(np.asarray(W2, dtype=np.float32).T)
    h128 = _hadamard(128)
    cat = np.ascontiguousarray(np.concatenate(cast_arrs, axis=0), dtype=np.float32)
    per = cat.shape[0] // NCORES
    in_maps = []
    for c in range(NCORES):
        in_maps.append({
            "gu_idx": np.ascontiguousarray(gu[c * 12:(c + 1) * 12]),
            "dn_idx": np.ascontiguousarray(dn[c * 2:(c + 1) * 2]),
            "lut": lut2,
            "w2t": np.ascontiguousarray(w2t[:, c * 256:(c + 1) * 256].reshape(C1, 128, 256)),
            "h128": h128,
            "wc_in": np.ascontiguousarray(cat[c * per:(c + 1) * per]),
        })
    res = _run("prep", build_prep, in_maps)
    gu_w = np.concatenate([r["gu_w"] for r in res], axis=0)
    dn_w = np.concatenate([r["dn_w"] for r in res], axis=0)
    wg0 = np.ascontiguousarray(gu_w[:C0])
    wu0 = np.ascontiguousarray(gu_w[C0P:C0P + C0])
    wd0 = np.ascontiguousarray(dn_w)
    w2f_full = np.empty((I1, D), dtype=NPBF16)
    for c in range(NCORES):
        w2f_full[:, c * 256:(c + 1) * 256] = res[c]["w2f"].reshape(I1, 256)
    w2f_t = _pw(w2f_full)
    cast_out = np.concatenate([r["wc_out"] for r in res], axis=0)
    outs = []
    off = 0
    for a in cast_arrs:
        outs.append(np.ascontiguousarray(cast_out[off:off + a.shape[0]]))
        off += a.shape[0]
    return wg0, wu0, wd0, w2f_t, outs



def _shard_tokens(x):
    shards = []
    for c in range(NCORES):
        b, hh = c // 2, c % 2
        shards.append(_fm(x[b, hh * T:(hh + 1) * T, :]).reshape(CD, 128, T))
    return shards


def _unshard_tokens(shards):
    out = np.empty((4, 2048, D), dtype=np.float32)
    for c in range(NCORES):
        b, hh = c // 2, c % 2
        out[b, hh * T:(hh + 1) * T, :] = shards[c].reshape(D, T).T
    return out


def _run(key, build_fn, in_maps):
    if key not in _PROG_CACHE:
        _PROG_CACHE[key] = build_fn()
    res = run_bass_kernel_spmd(_PROG_CACHE[key], in_maps, core_ids=list(range(NCORES)))
    return res.results


def _ghost_host(lut, gate_idx, up_idx, down_idx):
    gidxT = np.ascontiguousarray(gate_idx.T)
    uidxT = np.ascontiguousarray(up_idx.T)
    didxT = np.ascontiguousarray(down_idx.T)
    gu = np.concatenate([_pw(gidxT, pad_oc=C0P), _pw(uidxT, pad_oc=C0P)], axis=0)
    dn = _pw(didxT)
    lut2 = np.ascontiguousarray(np.asarray(lut, dtype=np.float32).reshape(1, 16))
    in_maps = [{"gu_idx": np.ascontiguousarray(gu[c * 12:(c + 1) * 12]),
                "dn_idx": np.ascontiguousarray(dn[c * 2:(c + 1) * 2]),
                "lut": lut2} for c in range(NCORES)]
    res = _run("ghost", build_ghost, in_maps)
    gu_w = np.concatenate([r["gu_w"] for r in res], axis=0)
    dn_w = np.concatenate([r["dn_w"] for r in res], axis=0)
    return (np.ascontiguousarray(gu_w[:C0]), np.ascontiguousarray(gu_w[C0P:C0P + C0]),
            np.ascontiguousarray(dn_w))


def _w2fold_host(W2):
    w2t = np.ascontiguousarray(np.asarray(W2, dtype=np.float32).T)
    h128 = _hadamard(128)
    in_maps = [{"w2t": np.ascontiguousarray(w2t[:, c * 256:(c + 1) * 256].reshape(C1, 128, 256)),
                "h128": h128} for c in range(NCORES)]
    res = _run("w2fold", build_w2fold, in_maps)
    w2f = np.empty((I1, D), dtype=NPBF16)
    for c in range(NCORES):
        w2f[:, c * 256:(c + 1) * 256] = res[c]["w2f"].reshape(I1, 256)
    return _pw(w2f)


def _ffn_host(x_shards, wg_t, wu_t, wd_t, C):
    wdtype = BF16 if wg_t.dtype == NPBF16 else F32
    wdtype_d = BF16 if wd_t.dtype == NPBF16 else F32
    key = ("ffn", C, str(wg_t.dtype), str(wd_t.dtype))
    in_maps = [{"x": x_shards[c], "wg": wg_t, "wu": wu_t, "wd": wd_t} for c in range(NCORES)]
    res = _run(key, lambda: build_ffn(C, wdtype, wdtype_d), in_maps)
    return [np.ascontiguousarray(r["x_out"]) for r in res]


def _attn_host(x_shards, qs, ks, vs, wo_t, bo_cols, causal, qk_dt):
    in_maps = []
    for c in range(NCORES):
        b, hh = c // 2, c % 2
        k_full = np.ascontiguousarray(np.concatenate([ks[2 * b], ks[2 * b + 1]], axis=2))
        v_full = np.concatenate([vs[2 * b], vs[2 * b + 1]], axis=0)
        v_h = np.ascontiguousarray(
            v_full.reshape(16, 128, 16, 128).transpose(2, 1, 0, 3).reshape(H, 128, 2048))
        qpos = np.arange(hh * T, (hh + 1) * T, dtype=np.float32).reshape(1, T)
        in_maps.append({"x": x_shards[c], "q": qs[c], "k": k_full, "v": v_h,
                        "wo": wo_t, "bo": bo_cols, "qpos": qpos})
    res = _run(("attn", causal, str(qk_dt)), lambda: build_attn(causal, qk_dt), in_maps)
    return [np.ascontiguousarray(r["x_out"]) for r in res]


def kernel(L0_attn_x, L0_attn_qkv_w, L0_attn_qkv_b, L0_attn_out_w, L0_attn_out_b,
           L0_ffn_lut, L0_ffn_gate_idx, L0_ffn_up_idx, L0_ffn_down_idx,
           L1_attn_g_q, L1_attn_Wq, L1_attn_bq, L1_attn_g_kv, L1_attn_Wkv, L1_attn_bkv,
           L1_attn_g_head, L1_attn_Wo, L1_attn_bo, L1_ffn_W1, L1_ffn_W2, L1_ffn_W3):
    f32 = lambda a: np.asarray(a, dtype=np.float32)
    qkvT = np.ascontiguousarray(f32(L0_attn_qkv_w).T)
    wqk0_t = _pw(np.ascontiguousarray(qkvT[:, :2 * D]))
    wv0_r = np.ascontiguousarray(qkvT[:, 2 * D:].reshape(CD, 128, D))
    bqk0 = _pb(f32(L0_attn_qkv_b)[:2 * D])
    bv0 = np.ascontiguousarray(f32(L0_attn_qkv_b)[2 * D:].reshape(1, D))
    wo0_t = _pw(np.ascontiguousarray(f32(L0_attn_out_w).T))
    bo0 = _pb(f32(L0_attn_out_b))
    wq1_t = _pw(f32(L1_attn_Wq))
    wk1_t = _pw(np.ascontiguousarray(f32(L1_attn_Wkv)[:, :D]))
    wv1_r = np.ascontiguousarray(f32(L1_attn_Wkv)[:, D:].reshape(CD, 128, D))
    wg1_t = _pw(np.ascontiguousarray(f32(L1_ffn_W1).T))
    wu1_t = _pw(np.ascontiguousarray(f32(L1_ffn_W3).T))
    wg0, wu0, wd0, w2f, casts = _prep_host(
        f32(L0_ffn_lut), np.asarray(L0_ffn_gate_idx), np.asarray(L0_ffn_up_idx),
        np.asarray(L0_ffn_down_idx), f32(L1_ffn_W2),
        [wqk0_t, wv0_r.reshape(CD, 128, D), wo0_t, wq1_t, wk1_t, wv1_r.reshape(CD, 128, D),
         _pw(f32(L1_attn_Wo)), wg1_t, wu1_t])
    (wqk0_t, wv0_r, wo0_t, wq1_t, wk1_t, wv1_r, wo1_t, wg1_t, wu1_t) = casts
    bq1 = _pb(f32(L1_attn_bq))
    bk1 = _pb(f32(L1_attn_bkv)[:D])
    bv1 = np.ascontiguousarray(f32(L1_attn_bkv)[D:].reshape(1, D))
    gq1 = _pb(f32(L1_attn_g_q))
    gkv1 = _pb(f32(L1_attn_g_kv))
    gh1 = np.ascontiguousarray(f32(L1_attn_g_head).reshape(128, 1))
    bo1 = _pb(f32(L1_attn_bo))

    x = _shard_tokens(f32(L0_attn_x))
    x = _ffn_host(x, wg0, wu0, wd0, C0)
    in_maps = [{"x": x[c], "wqk": wqk0_t, "wv": wv0_r, "bqk": bqk0, "bv": bv0}
               for c in range(NCORES)]
    res = _run("qkv0", build_qkv0, in_maps)
    x = _attn_host(x, [r["q"] for r in res], [r["k"] for r in res], [r["v"] for r in res],
                   wo0_t, bo0, True, BF16)
    x = _ffn_host(x, wg0, wu0, wd0, C0)
    x = _ffn_host(x, wg1_t, wu1_t, w2f, C1)
    in_maps = [{"x": x[c], "wq": wq1_t, "wk": wk1_t, "wv": wv1_r, "bq": bq1, "bk": bk1,
                "bv": bv1, "gq": gq1, "gkv": gkv1, "ghead": gh1} for c in range(NCORES)]
    res = _run("qkv1", build_qkv1, in_maps)
    x = _attn_host(x, [r["q"] for r in res], [r["k"] for r in res], [r["v"] for r in res],
                   wo1_t, bo1, False, F16)
    x = _ffn_host(x, wg1_t, wu1_t, w2f, C1)
    return _unshard_tokens(x)


# version 08
# speedup vs baseline: 1.0203x; 1.0203x over previous
import math
from contextlib import ExitStack

import numpy as np
import ml_dtypes

import concourse.bass as bass
import concourse.mybir as mybir
from concourse.bass_utils import run_bass_kernel_spmd

F32 = mybir.dt.float32
BF16 = mybir.dt.bfloat16
F16 = mybir.dt.float16
I32 = mybir.dt.int32
AF = mybir.ActivationFunctionType
OP = mybir.AluOpType
AX = mybir.AxisListType

NPBF16 = ml_dtypes.bfloat16

D = 2048
CD = D // 128
T = 1024
NCORES = 8
H = 16
HD = 128
I0 = 5504
C0 = I0 // 128
C0P = 48
I1 = 8192
C1 = I1 // 128
LN_EPS = 1e-5
RMS_EPS = float(np.finfo(np.float32).eps)
INVSQRT_HD = 1.0 / math.sqrt(HD)

_PROG_CACHE = {}



def _pw(w, pad_oc=None):
    K, O = w.shape
    KC, OC = K // 128, O // 128
    t = np.ascontiguousarray(
        w.reshape(KC, 128, OC, 128).transpose(2, 1, 0, 3).reshape(OC, 128, KC * 128))
    if pad_oc is not None and pad_oc > OC:
        t = np.concatenate([t, np.zeros((pad_oc - OC, 128, KC * 128), dtype=t.dtype)], axis=0)
    return t


def _pb(b):
    O = b.shape[0]
    return np.ascontiguousarray(np.asarray(b, dtype=np.float32).reshape(O // 128, 128).T)


def _fm(x):
    return np.ascontiguousarray(np.asarray(x, dtype=np.float32).T)


def _hadamard(n):
    h = np.array([[1.0]], dtype=np.float32)
    while h.shape[0] < n:
        h = np.block([[h, h], [h, -h]])
    return np.ascontiguousarray(h.astype(np.float32))


class Ctr:

    def __init__(self, st, nc, name):
        self.h = st.enter_context(nc.semaphore(name))
        self.n = 0

    def inc(self, instr, k=1):
        instr.then_inc(self.h, k)
        self.n += k
        return instr



def build_ghost():
    nc = bass.Bass("TRN2", target_bir_lowering=False)
    gu_idx = nc.dram_tensor("gu_idx", [12, 128, CD * 128], I32, kind="ExternalInput").ap()
    dn_idx = nc.dram_tensor("dn_idx", [2, 128, C0 * 128], I32, kind="ExternalInput").ap()
    lut = nc.dram_tensor("lut", [1, 16], F32, kind="ExternalInput").ap()
    gu_w = nc.dram_tensor("gu_w", [12, 128, CD * 128], BF16, kind="ExternalOutput").ap()
    dn_w = nc.dram_tensor("dn_w", [2, 128, C0 * 128], BF16, kind="ExternalOutput").ap()
    WMAX = C0 * 128
    with ExitStack() as st:
        idx_sb = [st.enter_context(nc.sbuf_tensor(f"idx{p}", [128, WMAX], I32)) for p in range(2)]
        idxb = st.enter_context(nc.sbuf_tensor("idxb", [128, WMAX], BF16))
        w_bf = [st.enter_context(nc.sbuf_tensor(f"wbf{p}", [128, WMAX], BF16)) for p in range(2)]
        msk = st.enter_context(nc.sbuf_tensor("msk", [128, WMAX], BF16))
        lut_sb = st.enter_context(nc.sbuf_tensor("lut_sb", [1, 16], F32))
        lut_col = st.enter_context(nc.sbuf_tensor("lut_col", [128, 16], F32))
        ones1 = st.enter_context(nc.sbuf_tensor("ones1", [1, 128], F32))
        lut_ps = st.enter_context(nc.psum_tensor("lut_ps", [128, 16], F32))
        s_lut = Ctr(st, nc, "s_lut")
        s_in = [Ctr(st, nc, f"s_in{p}") for p in range(2)]
        s_pe = Ctr(st, nc, "s_pe")
        s_v = Ctr(st, nc, "s_v")
        s_out = Ctr(st, nc, "s_out")
        with nc.Block() as block:
            @block.sync
            def _(sync):
                s_lut.inc(nc.sync.dma_start(out=lut_sb[:, :], in_=lut), 16)
                nc.vector.wait_ge(s_lut.h, 16)
                s_v.inc(nc.vector.memset(ones1[:, :], 1.0))
                nc.tensor.wait_ge(s_v.h, 1)
                s_pe.inc(nc.tensor.matmul(out=lut_ps[:, :], lhsT=ones1[:, :], rhs=lut_sb[:, :],
                                          start=True, stop=True))
                nc.scalar.wait_ge(s_pe.h, 1)
                s_v.inc(nc.scalar.copy(out=lut_col[:, :], in_=lut_ps[:, :]))

                units = [(gu_idx, gu_w, CD * 128)] * 12 + [(dn_idx, dn_w, C0 * 128)] * 2
                for ui, (src, dst, width) in enumerate(units):
                    u = ui if ui < 12 else ui - 12
                    p = ui % 2
                    if ui >= 2:
                        nc.sync.wait_ge(s_v.h, 2 + (ui - 2) + 1)
                    s_in[p].inc(nc.sync.dma_start(out=idx_sb[p][:, :width], in_=src[u]), 16)
                    nc.vector.wait_ge(s_in[p].h, s_in[p].n)
                    nc.vector.wait_ge(s_v.h, 2)
                    if ui >= 2:
                        nc.vector.wait_ge(s_out.h, 16 * (ui - 1))
                    nc.vector.tensor_copy(out=idxb[:, :width], in_=idx_sb[p][:, :width])
                    last = None
                    for k in range(16):
                        if k == 0:
                            nc.vector.tensor_scalar(out=w_bf[p][:, :width], in0=idxb[:, :width],
                                                    scalar1=0.0, scalar2=lut_col[:, 0:1],
                                                    op0=OP.is_equal, op1=OP.mult)
                        else:
                            nc.vector.tensor_scalar(out=msk[:, :width], in0=idxb[:, :width],
                                                    scalar1=float(k), scalar2=lut_col[:, k:k + 1],
                                                    op0=OP.is_equal, op1=OP.mult)
                            last = nc.vector.tensor_tensor(out=w_bf[p][:, :width],
                                                           in0=w_bf[p][:, :width],
                                                           in1=msk[:, :width], op=OP.add)
                    s_v.inc(last)
                    nc.gpsimd.wait_ge(s_v.h, 2 + ui + 1)
                    s_out.inc(nc.gpsimd.dma_start(out=dst[u], in_=w_bf[p][:, :width]), 16)
                nc.gpsimd.wait_ge(s_out.h, 16 * 14)
    return nc



def build_w2fold():
    nc = bass.Bass("TRN2", target_bir_lowering=False)
    w2t = nc.dram_tensor("w2t", [C1, 128, 256], F32, kind="ExternalInput").ap()
    h128 = nc.dram_tensor("h128", [128, 128], F32, kind="ExternalInput").ap()
    w2f = nc.dram_tensor("w2f", [C1, 128, 256], BF16, kind="ExternalOutput").ap()
    W = 256
    with ExitStack() as st:
        a_bf = st.enter_context(nc.sbuf_tensor("arena_abf", [128, C1 * W], BF16))
        a = st.enter_context(nc.sbuf_tensor("arena_a", [128, C1 * W], F32))
        b = st.enter_context(nc.sbuf_tensor("arena_b", [128, C1 * W], F32))
        h_sb = st.enter_context(nc.sbuf_tensor("h_sb", [128, 128], BF16))
        ps = [st.enter_context(nc.psum_tensor(f"ps{p}", [128, W], F32)) for p in range(2)]
        s_in = Ctr(st, nc, "s_in")
        s_pe = Ctr(st, nc, "s_pe")
        s_a = Ctr(st, nc, "s_a")
        s_out = Ctr(st, nc, "s_out")
        with nc.Block() as block:
            @block.sync
            def _(sync):
                s_in.inc(nc.gpsimd.dma_start(out=h_sb[:, :], in_=h128), 16)
                for c in range(C1):
                    s_in.inc(nc.gpsimd.dma_start(out=a_bf[:, c * W:(c + 1) * W], in_=w2t[c]), 16)
                nc.tensor.wait_ge(s_in.h, 16 * (C1 + 1))
                for c in range(C1):
                    p = c % 2
                    if c >= 2:
                        nc.tensor.wait_ge(s_a.h, c - 1)
                    s_pe.inc(nc.tensor.matmul(out=ps[p][:, :], lhsT=h_sb[:, :],
                                              rhs=a_bf[:, c * W:(c + 1) * W], start=True, stop=True))
                    nc.scalar.wait_ge(s_pe.h, c + 1)
                    s_a.inc(nc.scalar.activation(out=b[:, c * W:(c + 1) * W], in_=ps[p][:, :],
                                                 func=AF.Identity, scale=1.0 / math.sqrt(I1)))
                nc.vector.wait_ge(s_a.h, C1)
                src, dst = b, a
                last = None
                for s in range(6):
                    step = 1 << s
                    for c in range(C1):
                        if c & step:
                            continue
                        cc = c + step
                        nc.vector.tensor_tensor(out=dst[:, c * W:(c + 1) * W],
                                                in0=src[:, c * W:(c + 1) * W],
                                                in1=src[:, cc * W:(cc + 1) * W], op=OP.add)
                        last = nc.vector.tensor_tensor(out=dst[:, cc * W:(cc + 1) * W],
                                                       in0=src[:, c * W:(c + 1) * W],
                                                       in1=src[:, cc * W:(cc + 1) * W],
                                                       op=OP.subtract)
                    src, dst = dst, src
                s_a.inc(last)
                nc.gpsimd.wait_ge(s_a.h, C1 + 1)
                for c in range(C1):
                    s_out.inc(nc.gpsimd.dma_start(out=w2f[c], in_=src[:, c * W:(c + 1) * W]), 16)
                nc.gpsimd.wait_ge(s_out.h, 16 * C1)
    return nc



def emit_ln_stats(nc, s_pe, s_act, s_v, x_sb, scr_a, scr_b, stat_ps, rows, ones_bf,
                  eps_ap, inv_n, rms=False, n_tok=T):
    nh = n_tok // 512
    for dk in range(CD):
        sl = slice(dk * n_tok, (dk + 1) * n_tok)
        pa = dk % 2
        if dk >= 2:
            nc.scalar.wait_ge(s_pe.h, s_pe.n)
        if not rms:
            s_act.inc(nc.scalar.activation(out=scr_a[pa][:, :n_tok], in_=x_sb[:, sl], func=AF.Identity))
        s_act.inc(nc.scalar.activation(out=scr_b[pa][:, :n_tok], in_=x_sb[:, sl], func=AF.Square))
        nc.tensor.wait_ge(s_act.h, s_act.n)
        for hh in range(nh):
            ts = slice(hh * 512, (hh + 1) * 512)
            if not rms:
                s_pe.inc(nc.tensor.matmul(out=stat_ps[0][:, ts], lhsT=ones_bf[:, :],
                                          rhs=scr_a[pa][:, ts], start=(dk == 0), stop=(dk == CD - 1)))
            s_pe.inc(nc.tensor.matmul(out=stat_ps[1][:, ts], lhsT=ones_bf[:, :],
                                      rhs=scr_b[pa][:, ts], start=(dk == 0), stop=(dk == CD - 1)))
    nc.scalar.wait_ge(s_pe.h, s_pe.n)
    if not rms:
        s_act.inc(nc.scalar.activation(out=rows[0][0:1, :n_tok], in_=stat_ps[0][0:1, :n_tok],
                                       func=AF.Identity, scale=inv_n))
    s_act.inc(nc.scalar.activation(out=rows[1][0:1, :n_tok], in_=stat_ps[1][0:1, :n_tok],
                                   func=AF.Identity, scale=inv_n))
    nc.vector.wait_ge(s_act.h, s_act.n)
    if not rms:
        s_v.inc(nc.vector.scalar_tensor_tensor(out=rows[2][0:1, :n_tok], in0=rows[0][0:1, :n_tok],
                                               scalar=-1.0, in1=rows[0][0:1, :n_tok],
                                               op0=OP.mult, op1=OP.mult))
        s_v.inc(nc.vector.tensor_tensor(out=rows[2][0:1, :n_tok], in0=rows[1][0:1, :n_tok],
                                        in1=rows[2][0:1, :n_tok], op=OP.add))
        var_row = rows[2][0:1, :n_tok]
        nc.scalar.wait_ge(s_v.h, s_v.n)
    else:
        var_row = rows[1][0:1, :n_tok]
    s_act.inc(nc.scalar.activation(out=rows[3][0:1, :n_tok], in_=var_row, func=AF.Ln, bias=eps_ap))
    s_act.inc(nc.scalar.activation(out=rows[3][0:1, :n_tok], in_=rows[3][0:1, :n_tok], func=AF.Exp,
                                   scale=-0.5))
    if not rms:
        nc.vector.wait_ge(s_act.h, s_act.n)
        s_v.inc(nc.vector.scalar_tensor_tensor(out=rows[2][0:1, :n_tok], in0=rows[0][0:1, :n_tok],
                                               scalar=-1.0, in1=rows[3][0:1, :n_tok],
                                               op0=OP.mult, op1=OP.mult))


def emit_bcast_rows(nc, s_pe, srcs, dsts, ones1, n_tok, wait_sem=None, wait_val=None):
    if wait_sem is not None:
        nc.tensor.wait_ge(wait_sem, wait_val)
    for src, dst in zip(srcs, dsts):
        for hh in range(n_tok // 512):
            ts = slice(hh * 512, (hh + 1) * 512)
            s_pe.inc(nc.tensor.matmul(out=dst[:, ts], lhsT=ones1[:, :], rhs=src[:, ts],
                                      start=True, stop=True))



DEBUG_FFN = False


def build_ffn(C, wdtype, wdtype_d=None):
    if wdtype_d is None:
        wdtype_d = wdtype
    nc = bass.Bass("TRN2", target_bir_lowering=False)
    tb = 512
    if DEBUG_FFN:
        xh_dbg = nc.dram_tensor("xh_dbg", [CD, 128, 512], BF16, kind="ExternalOutput").ap()
        h_dbg = nc.dram_tensor("h_dbg", [C, 128, 512], BF16, kind="ExternalOutput").ap()
        wd_dbg = nc.dram_tensor("wd_dbg", [2, 128, C * 128], BF16, kind="ExternalOutput").ap()
    x_in = nc.dram_tensor("x", [CD, 128, T], F32, kind="ExternalInput").ap()
    wg = nc.dram_tensor("wg", [C, 128, CD * 128], wdtype, kind="ExternalInput").ap()
    wu = nc.dram_tensor("wu", [C, 128, CD * 128], wdtype, kind="ExternalInput").ap()
    wd = nc.dram_tensor("wd", [CD, 128, C * 128], wdtype_d, kind="ExternalInput").ap()
    x_out = nc.dram_tensor("x_out", [CD, 128, T], F32, kind="ExternalOutput").ap()
    n_tb = T // tb
    with ExitStack() as st:
        x_sb = st.enter_context(nc.sbuf_tensor("x_sb", [128, CD * T], F32))
        xhat = st.enter_context(nc.sbuf_tensor("xhat", [128, CD * tb], BF16))
        h_sb = st.enter_context(nc.sbuf_tensor("h_sb", [128, C * tb], BF16))
        scr = [st.enter_context(nc.sbuf_tensor(f"scr{p}", [128, T], BF16)) for p in range(2)]
        scrq = [st.enter_context(nc.sbuf_tensor(f"scrq{p}", [128, T], BF16)) for p in range(2)]
        scr_f = st.enter_context(nc.sbuf_tensor("scr_f", [128, tb], F32))
        sg = [st.enter_context(nc.sbuf_tensor(f"sg{p}", [128, tb], BF16)) for p in range(2)]
        rows = [st.enter_context(nc.sbuf_tensor(f"rows{i}", [1, T], F32)) for i in range(4)]
        ones_bf = st.enter_context(nc.sbuf_tensor("ones_bf", [128, 1], BF16))
        ones1 = st.enter_context(nc.sbuf_tensor("ones1", [1, 128], F32))
        eps_sb = st.enter_context(nc.sbuf_tensor("eps_sb", [1, 1], F32))
        wg_sb = [st.enter_context(nc.sbuf_tensor(f"wg{p}", [128, CD * 128], BF16)) for p in range(2)]
        wu_sb = [st.enter_context(nc.sbuf_tensor(f"wu{p}", [128, CD * 128], BF16)) for p in range(2)]
        wd1 = st.enter_context(nc.sbuf_tensor("wd1", [128, C * 128], BF16))
        s_x = Ctr(st, nc, "s_x")
        s_wA = [Ctr(st, nc, f"s_wA{p}") for p in range(2)]
        s_wD = [Ctr(st, nc, f"s_wD{p}") for p in range(2)]
        s_pe = Ctr(st, nc, "s_pe")
        s_act = Ctr(st, nc, "s_act")
        s_v = Ctr(st, nc, "s_v")
        s_out = Ctr(st, nc, "s_out")
        with nc.Block() as block:
            @block.sync
            def _(sync):
                for dk in range(CD):
                    s_x.inc(nc.sync.dma_start(out=x_sb[:, dk * T:(dk + 1) * T], in_=x_in[dk]), 16)
                s_v.inc(nc.vector.memset(ones_bf[:, :], 1.0))
                s_v.inc(nc.vector.memset(ones1[:, :], 1.0))
                s_v.inc(nc.vector.memset(eps_sb[:, :], LN_EPS))
                nc.scalar.wait_ge(s_x.h, 16 * CD)
                nc.scalar.wait_ge(s_v.h, 3)
                nc.tensor.wait_ge(s_v.h, 3)
                nc.vector.wait_ge(s_x.h, 16 * CD)
                with nc.psum_tensor("stat0", [1, T], F32) as st0, \
                     nc.psum_tensor("stat1", [1, T], F32) as st1:
                    emit_ln_stats(nc, s_pe, s_act, s_v, x_sb, scr, scrq, [st0, st1], rows,
                                  ones_bf, eps_sb[0:1, 0:1], 1.0 / D)
                for tbi in range(n_tb):
                    t0 = tbi * tb
                    with nc.psum_tensor(f"bc0_{tbi}", [128, tb], F32) as bc0, \
                         nc.psum_tensor(f"bc1_{tbi}", [128, tb], F32) as bc1:
                        emit_bcast_rows(nc, s_pe, [rows[3][0:1, t0:t0 + tb], rows[2][0:1, t0:t0 + tb]],
                                        [bc0, bc1], ones1, tb, s_v.h, s_v.n)
                        nc.vector.wait_ge(s_pe.h, s_pe.n)
                        for dk in range(CD):
                            sl = slice(dk * T + t0, dk * T + t0 + tb)
                            nc.vector.tensor_tensor(out=scr_f[:, :], in0=x_sb[:, sl],
                                                    in1=bc0[:, :], op=OP.mult)
                            last = nc.vector.tensor_tensor(out=xhat[:, dk * tb:(dk + 1) * tb],
                                                           in0=scr_f[:, :], in1=bc1[:, :], op=OP.add)
                        s_v.inc(last)
                    apply_done = s_v.n
                    if DEBUG_FFN and tbi == 0:
                        nc.sync.wait_ge(s_v.h, apply_done)
                        for dk in range(CD):
                            s_out.inc(nc.sync.dma_start(out=xh_dbg[dk], in_=xhat[:, dk * tb:(dk + 1) * tb]), 16)
                    with nc.psum_tensor(f"g0_{tbi}", [128, tb], F32) as g0, \
                         nc.psum_tensor(f"g1_{tbi}", [128, tb], F32) as g1, \
                         nc.psum_tensor(f"u0_{tbi}", [128, tb], F32) as u0, \
                         nc.psum_tensor(f"u1_{tbi}", [128, tb], F32) as u1:
                        g_ps, u_ps = [g0, g1], [u0, u1]
                        silu_base = s_act.n
                        mul_base = s_v.n
                        peA_base = s_pe.n
                        for c in range(C):
                            p = c % 2
                            if c >= 2:
                                nc.sync.wait_ge(s_pe.h, peA_base + 2 * c - 2)
                            s_wA[p].inc(nc.sync.dma_start(out=wg_sb[p][:, :], in_=wg[c]), 16)
                            s_wA[p].inc(nc.sync.dma_start(out=wu_sb[p][:, :], in_=wu[c]), 16)
                            nc.tensor.wait_ge(s_wA[p].h, s_wA[p].n)
                            nc.tensor.wait_ge(s_v.h, apply_done)
                            if c >= 2:
                                nc.tensor.wait_ge(s_act.h, silu_base + c - 1)
                                nc.tensor.wait_ge(s_v.h, mul_base + c - 1)
                            last = None
                            for dk in range(CD):
                                last = nc.tensor.matmul(out=g_ps[p][:, :],
                                                        lhsT=wg_sb[p][:, dk * 128:(dk + 1) * 128],
                                                        rhs=xhat[:, dk * tb:(dk + 1) * tb],
                                                        start=(dk == 0), stop=(dk == CD - 1))
                            s_pe.inc(last)
                            for dk in range(CD):
                                last = nc.tensor.matmul(out=u_ps[p][:, :],
                                                        lhsT=wu_sb[p][:, dk * 128:(dk + 1) * 128],
                                                        rhs=xhat[:, dk * tb:(dk + 1) * tb],
                                                        start=(dk == 0), stop=(dk == CD - 1))
                            s_pe.inc(last)
                            nc.scalar.wait_ge(s_pe.h, s_pe.n - 1)
                            if c >= 2:
                                nc.scalar.wait_ge(s_v.h, mul_base + c - 1)
                            s_act.inc(nc.scalar.activation(out=sg[p][:, :], in_=g_ps[p][:, :],
                                                           func=AF.Silu))
                            nc.vector.wait_ge(s_pe.h, s_pe.n)
                            nc.vector.wait_ge(s_act.h, silu_base + c + 1)
                            s_v.inc(nc.vector.tensor_tensor(out=h_sb[:, c * tb:(c + 1) * tb],
                                                            in0=sg[p][:, :], in1=u_ps[p][:, :],
                                                            op=OP.mult))
                    mulA_end = s_v.n
                    if DEBUG_FFN and tbi == 0:
                        nc.sync.wait_ge(s_v.h, mulA_end)
                        for c in range(C):
                            s_out.inc(nc.sync.dma_start(out=h_dbg[c], in_=h_sb[:, c * tb:(c + 1) * tb]), 16)
                        nc.sync.wait_ge(s_out.h, s_out.n)
                    with nc.psum_tensor(f"d0_{tbi}", [128, tb], F32) as d0, \
                         nc.psum_tensor(f"d1_{tbi}", [128, tb], F32) as d1:
                        d_ps = [d0, d1]
                        wd_sb = [xhat[:, 0:C * 128], wd1[:, :]]
                        evac_base = s_v.n
                        peB_base = s_pe.n
                        for o in range(CD):
                            p = o % 2
                            if o >= 2:
                                nc.sync.wait_ge(s_pe.h, peB_base + o - 1)
                            else:
                                nc.sync.wait_ge(s_pe.h, peB_base)
                            s_wD[p].inc(nc.sync.dma_start(out=wd_sb[p], in_=wd[o]), 16)
                            nc.tensor.wait_ge(s_wD[p].h, s_wD[p].n)
                            nc.tensor.wait_ge(s_v.h, mulA_end)
                            if o >= 2:
                                nc.tensor.wait_ge(s_v.h, evac_base + o - 1)
                            last = None
                            for c in range(C):
                                last = nc.tensor.matmul(out=d_ps[p][:, :],
                                                        lhsT=wd_sb[p][:, c * 128:(c + 1) * 128] if p else wd_sb[0][:, c * 128:(c + 1) * 128],
                                                        rhs=h_sb[:, c * tb:(c + 1) * tb],
                                                        start=(c == 0), stop=(c == C - 1))
                            s_pe.inc(last)
                            nc.vector.wait_ge(s_pe.h, s_pe.n)
                            s_v.inc(nc.vector.scalar_tensor_tensor(
                                out=x_sb[:, o * T + t0:o * T + t0 + tb], in0=d_ps[p][:, :],
                                scalar=0.5, in1=x_sb[:, o * T + t0:o * T + t0 + tb],
                                op0=OP.mult, op1=OP.add))
                nc.sync.wait_ge(s_v.h, s_v.n)
                for dk in range(CD):
                    s_out.inc(nc.sync.dma_start(out=x_out[dk], in_=x_sb[:, dk * T:(dk + 1) * T]), 16)
                nc.sync.wait_ge(s_out.h, s_out.n)
    return nc




def _soft_barrier(nc, ctrs):
    for eng in (nc.tensor, nc.scalar, nc.vector, nc.gpsimd, nc.sync):
        for c in ctrs:
            if c.n > 0:
                eng.wait_ge(c.h, c.n)



def build_qkv0():
    nc = bass.Bass("TRN2", target_bir_lowering=False)
    x_in = nc.dram_tensor("x", [CD, 128, T], F32, kind="ExternalInput").ap()
    wqk = nc.dram_tensor("wqk", [2 * H, 128, CD * 128], BF16, kind="ExternalInput").ap()
    wv = nc.dram_tensor("wv", [CD, 128, D], BF16, kind="ExternalInput").ap()
    bqk = nc.dram_tensor("bqk", [128, 2 * H], F32, kind="ExternalInput").ap()
    bv = nc.dram_tensor("bv", [1, D], F32, kind="ExternalInput").ap()
    q_out = nc.dram_tensor("q", [H, 128, T], BF16, kind="ExternalOutput").ap()
    k_out = nc.dram_tensor("k", [H, 128, T], BF16, kind="ExternalOutput").ap()
    v_out = nc.dram_tensor("v", [T // 128, 128, D], BF16, kind="ExternalOutput").ap()
    with ExitStack() as st:
        x_sb = st.enter_context(nc.sbuf_tensor("x_sb", [128, CD * T], F32))
        xhat = st.enter_context(nc.sbuf_tensor("xhat", [128, CD * T], BF16))
        scr = [st.enter_context(nc.sbuf_tensor(f"scr{p}", [128, T], BF16)) for p in range(2)]
        scrq = [st.enter_context(nc.sbuf_tensor(f"scrq{p}", [128, T], BF16)) for p in range(2)]
        scr_f = st.enter_context(nc.sbuf_tensor("scr_f", [128, T], F32))
        rows = [st.enter_context(nc.sbuf_tensor(f"rows{i}", [1, T], F32)) for i in range(4)]
        ones_bf = st.enter_context(nc.sbuf_tensor("ones_bf", [128, 1], BF16))
        ones1 = st.enter_context(nc.sbuf_tensor("ones1", [1, 128], F32))
        ones1b = st.enter_context(nc.sbuf_tensor("ones1b", [1, 128], BF16))
        eps_sb = st.enter_context(nc.sbuf_tensor("eps_sb", [1, 1], F32))
        bqk_sb = st.enter_context(nc.sbuf_tensor("bqk_sb", [128, 2 * H], F32))
        bv_sb = st.enter_context(nc.sbuf_tensor("bv_sb", [1, D], BF16))
        wqk_sb = [st.enter_context(nc.sbuf_tensor(f"wqk{p}", [128, CD * 128], BF16)) for p in range(2)]
        wv_sb = [st.enter_context(nc.sbuf_tensor(f"wv{dk}", [128, 512], BF16)) for dk in range(CD)]
        obuf = [st.enter_context(nc.sbuf_tensor(f"obuf{p}", [128, T], BF16)) for p in range(2)]
        vbuf = [st.enter_context(nc.sbuf_tensor(f"vbuf{p}", [128, 512], BF16)) for p in range(2)]
        s_x = Ctr(st, nc, "s_x")
        s_wA = [Ctr(st, nc, f"s_wA{p}") for p in range(2)]
        s_wV = Ctr(st, nc, "s_wV")
        s_pe = Ctr(st, nc, "s_pe")
        s_act = Ctr(st, nc, "s_act")
        s_v = Ctr(st, nc, "s_v")
        s_out = Ctr(st, nc, "s_out")
        with nc.Block() as block:
            @block.sync
            def _(sync):
                for dk in range(CD):
                    s_x.inc(nc.sync.dma_start(out=x_sb[:, dk * T:(dk + 1) * T], in_=x_in[dk]), 16)
                s_x.inc(nc.sync.dma_start(out=bqk_sb[:, :], in_=bqk), 16)
                s_x.inc(nc.gpsimd.dma_start(out=bv_sb[:, :], in_=bv), 16)
                s_v.inc(nc.vector.memset(ones_bf[:, :], 1.0))
                s_v.inc(nc.vector.memset(ones1[:, :], 1.0))
                s_v.inc(nc.vector.memset(ones1b[:, :], 1.0))
                s_v.inc(nc.vector.memset(eps_sb[:, :], LN_EPS))
                nc.scalar.wait_ge(s_x.h, s_x.n)
                nc.scalar.wait_ge(s_v.h, 4)
                nc.tensor.wait_ge(s_v.h, 4)
                nc.vector.wait_ge(s_x.h, s_x.n)
                with nc.psum_tensor("stat0", [1, T], F32) as st0, \
                     nc.psum_tensor("stat1", [1, T], F32) as st1:
                    emit_ln_stats(nc, s_pe, s_act, s_v, x_sb, scr, scrq, [st0, st1], rows,
                                  ones_bf, eps_sb[0:1, 0:1], 1.0 / D)
                with nc.psum_tensor("bc0", [128, T], F32) as bc0, \
                     nc.psum_tensor("bc1", [128, T], F32) as bc1:
                    emit_bcast_rows(nc, s_pe, [rows[3][0:1, :T], rows[2][0:1, :T]],
                                    [bc0, bc1], ones1, T, s_v.h, s_v.n)
                    nc.vector.wait_ge(s_pe.h, s_pe.n)
                    for dk in range(CD):
                        sl = slice(dk * T, (dk + 1) * T)
                        nc.vector.tensor_tensor(out=scr_f[:, :], in0=x_sb[:, sl], in1=bc0[:, :], op=OP.mult)
                        last = nc.vector.tensor_tensor(out=xhat[:, sl], in0=scr_f[:, :], in1=bc1[:, :], op=OP.add)
                    s_v.inc(last)
                apply_done = s_v.n
                with nc.psum_tensor("qk0", [128, T], F32) as qk0, \
                     nc.psum_tensor("qk1", [128, T], F32) as qk1:
                    qk_ps = [qk0, qk1]
                    peQ = s_pe.n
                    actQ = s_act.n
                    for oc in range(2 * H):
                        p = oc % 2
                        if oc >= 2:
                            nc.sync.wait_ge(s_pe.h, peQ + oc - 1)
                        s_wA[p].inc(nc.sync.dma_start(out=wqk_sb[p][:, :], in_=wqk[oc]), 16)
                        nc.tensor.wait_ge(s_wA[p].h, s_wA[p].n)
                        nc.tensor.wait_ge(s_v.h, apply_done)
                        if oc >= 2:
                            nc.tensor.wait_ge(s_act.h, actQ + oc - 1)
                        last = None
                        for hh in range(2):
                            ts = slice(hh * 512, (hh + 1) * 512)
                            for dk in range(CD):
                                last = nc.tensor.matmul(out=qk_ps[p][:, ts],
                                                        lhsT=wqk_sb[p][:, dk * 128:(dk + 1) * 128],
                                                        rhs=xhat[:, dk * T:(dk + 1) * T][:, ts],
                                                        start=(dk == 0), stop=(dk == CD - 1))
                        s_pe.inc(last)
                        nc.scalar.wait_ge(s_pe.h, s_pe.n)
                        if oc >= 2:
                            nc.scalar.wait_ge(s_out.h, 16 * (oc - 1))
                        s_act.inc(nc.scalar.activation(out=obuf[p][:, :], in_=qk_ps[p][:, :],
                                                       func=AF.Identity, bias=bqk_sb[:, oc:oc + 1]))
                        dst = q_out[oc] if oc < H else k_out[oc - H]
                        s_out.inc(nc.scalar.dma_start(out=dst, in_=obuf[p][:, :]), 16)
                _soft_barrier(nc, [s_pe, s_act, s_v, s_out])
                with nc.psum_tensor("v0", [128, 512], F32) as v0, \
                     nc.psum_tensor("v1", [128, 512], F32) as v1:
                    v_ps = [v0, v1]
                    outV = s_out.n
                    for vg in range(4):
                        vs = slice(vg * 512, (vg + 1) * 512)
                        if vg >= 1:
                            nc.sync.wait_ge(s_pe.h, s_pe.n)
                        for dk in range(CD):
                            s_wV.inc(nc.sync.dma_start(out=wv_sb[dk][:, :], in_=wv[dk][:, vs]), 16)
                        nc.tensor.wait_ge(s_wV.h, s_wV.n)
                        for tt in range(T // 128):
                            u = vg * 8 + tt
                            p = u % 2
                            nc.tensor.wait_ge(s_v.h, apply_done)
                            if u >= 2:
                                nc.tensor.wait_ge(s_act.h, s_act.n - 1)
                            for dk in range(CD):
                                nc.tensor.matmul(out=v_ps[p][:, :], lhsT=xhat[:, dk * T + tt * 128:dk * T + (tt + 1) * 128],
                                                 rhs=wv_sb[dk][:, :], start=(dk == 0), stop=False)
                            last = nc.tensor.matmul(out=v_ps[p][:, :], lhsT=ones1b[:, :],
                                                    rhs=bv_sb[:, vs], start=False, stop=True)
                            s_pe.inc(last)
                            nc.scalar.wait_ge(s_pe.h, s_pe.n)
                            if u >= 2:
                                nc.scalar.wait_ge(s_out.h, outV + 16 * (u - 1))
                            s_act.inc(nc.scalar.activation(out=vbuf[p][:, :], in_=v_ps[p][:, :],
                                                           func=AF.Identity))
                            s_out.inc(nc.scalar.dma_start(out=v_out[tt][:, vs], in_=vbuf[p][:, :]), 16)
                nc.sync.wait_ge(s_out.h, s_out.n)
    return nc


def build_qkv1():
    nc = bass.Bass("TRN2", target_bir_lowering=False)
    x_in = nc.dram_tensor("x", [CD, 128, T], F32, kind="ExternalInput").ap()
    wq = nc.dram_tensor("wq", [H, 128, CD * 128], BF16, kind="ExternalInput").ap()
    wk = nc.dram_tensor("wk", [H, 128, CD * 128], BF16, kind="ExternalInput").ap()
    wv = nc.dram_tensor("wv", [CD, 128, D], BF16, kind="ExternalInput").ap()
    bq = nc.dram_tensor("bq", [128, H], F32, kind="ExternalInput").ap()
    bk = nc.dram_tensor("bk", [128, H], F32, kind="ExternalInput").ap()
    bv = nc.dram_tensor("bv", [1, D], F32, kind="ExternalInput").ap()
    gq = nc.dram_tensor("gq", [128, CD], F32, kind="ExternalInput").ap()
    gkv = nc.dram_tensor("gkv", [128, CD], F32, kind="ExternalInput").ap()
    ghead = nc.dram_tensor("ghead", [128, 1], F32, kind="ExternalInput").ap()
    q_out = nc.dram_tensor("q", [H, 128, T], F16, kind="ExternalOutput").ap()
    k_out = nc.dram_tensor("k", [H, 128, T], F16, kind="ExternalOutput").ap()
    v_out = nc.dram_tensor("v", [T // 128, 128, D], BF16, kind="ExternalOutput").ap()
    with ExitStack() as st:
        x_sb = st.enter_context(nc.sbuf_tensor("x_sb", [128, CD * T], F32))
        xq = st.enter_context(nc.sbuf_tensor("xq", [128, CD * T], BF16))
        xkv = st.enter_context(nc.sbuf_tensor("xkv", [128, CD * T], BF16))
        scr = [st.enter_context(nc.sbuf_tensor(f"scr{p}", [128, T], BF16)) for p in range(2)]
        scrq = [st.enter_context(nc.sbuf_tensor(f"scrq{p}", [128, T], BF16)) for p in range(2)]
        scr_f = st.enter_context(nc.sbuf_tensor("scr_f", [128, T], F32))
        qf32 = [st.enter_context(nc.sbuf_tensor(f"qf32{p}", [128, T], F32)) for p in range(2)]
        rows = [st.enter_context(nc.sbuf_tensor(f"rows{i}", [1, T], F32)) for i in range(4)]
        ones_bf = st.enter_context(nc.sbuf_tensor("ones_bf", [128, 1], BF16))
        ones1 = st.enter_context(nc.sbuf_tensor("ones1", [1, 128], F32))
        ones1b = st.enter_context(nc.sbuf_tensor("ones1b", [1, 128], BF16))
        eps_sb = st.enter_context(nc.sbuf_tensor("eps_sb", [1, 1], F32))
        bq_sb = st.enter_context(nc.sbuf_tensor("bq_sb", [128, H], F32))
        bk_sb = st.enter_context(nc.sbuf_tensor("bk_sb", [128, H], F32))
        bv_sb = st.enter_context(nc.sbuf_tensor("bv_sb", [1, D], BF16))
        gq_sb = st.enter_context(nc.sbuf_tensor("gq_sb", [128, CD], F32))
        gkv_sb = st.enter_context(nc.sbuf_tensor("gkv_sb", [128, CD], F32))
        gh_sb = st.enter_context(nc.sbuf_tensor("gh_sb", [128, 1], F32))
        w_sb = [st.enter_context(nc.sbuf_tensor(f"w{p}", [128, CD * 128], BF16)) for p in range(2)]
        wv_sb = [st.enter_context(nc.sbuf_tensor(f"wv{dk}", [128, 512], BF16)) for dk in range(CD)]
        obuf = [st.enter_context(nc.sbuf_tensor(f"obuf{p}", [128, T], F16)) for p in range(2)]
        vbuf = [st.enter_context(nc.sbuf_tensor(f"vbuf{p}", [128, 512], BF16)) for p in range(2)]
        s_x = Ctr(st, nc, "s_x")
        s_wA = [Ctr(st, nc, f"s_wA{p}") for p in range(2)]
        s_wV = Ctr(st, nc, "s_wV")
        s_pe = Ctr(st, nc, "s_pe")
        s_act = Ctr(st, nc, "s_act")
        s_v = Ctr(st, nc, "s_v")
        s_out = Ctr(st, nc, "s_out")
        with nc.Block() as block:
            @block.sync
            def _(sync):
                for dk in range(CD):
                    s_x.inc(nc.sync.dma_start(out=x_sb[:, dk * T:(dk + 1) * T], in_=x_in[dk]), 16)
                for name, dst in ((bq, bq_sb), (bk, bk_sb), (gq, gq_sb), (gkv, gkv_sb), (ghead, gh_sb)):
                    s_x.inc(nc.sync.dma_start(out=dst[:, :], in_=name), 16)
                s_x.inc(nc.gpsimd.dma_start(out=bv_sb[:, :], in_=bv), 16)
                s_v.inc(nc.vector.memset(ones_bf[:, :], 1.0))
                s_v.inc(nc.vector.memset(ones1[:, :], 1.0))
                s_v.inc(nc.vector.memset(ones1b[:, :], 1.0))
                s_v.inc(nc.vector.memset(eps_sb[:, :], RMS_EPS))
                nc.scalar.wait_ge(s_x.h, s_x.n)
                nc.scalar.wait_ge(s_v.h, 4)
                nc.tensor.wait_ge(s_v.h, 4)
                nc.vector.wait_ge(s_x.h, s_x.n)
                with nc.psum_tensor("stat0", [1, T], F32) as st0, \
                     nc.psum_tensor("stat1", [1, T], F32) as st1:
                    emit_ln_stats(nc, s_pe, s_act, s_v, x_sb, scr, scrq, [st0, st1], rows,
                                  ones_bf, eps_sb[0:1, 0:1], 1.0 / D, rms=True)
                with nc.psum_tensor("bc0", [128, T], F32) as bc0:
                    emit_bcast_rows(nc, s_pe, [rows[3][0:1, :T]], [bc0], ones1, T, s_act.h, s_act.n)
                    nc.vector.wait_ge(s_pe.h, s_pe.n)
                    for dk in range(CD):
                        sl = slice(dk * T, (dk + 1) * T)
                        nc.vector.tensor_tensor(out=scr_f[:, :], in0=x_sb[:, sl], in1=bc0[:, :], op=OP.mult)
                        nc.vector.tensor_scalar(out=xq[:, sl], in0=scr_f[:, :], scalar1=gq_sb[:, dk:dk + 1],
                                                scalar2=None, op0=OP.mult)
                        last = nc.vector.tensor_scalar(out=xkv[:, sl], in0=scr_f[:, :],
                                                       scalar1=gkv_sb[:, dk:dk + 1], scalar2=None, op0=OP.mult)
                    s_v.inc(last)
                apply_done = s_v.n
                for which, (wsrc, bias_sb, xin, dst_t) in enumerate(
                        ((wq, bq_sb, xq, q_out), (wk, bk_sb, xkv, k_out))):
                    _soft_barrier(nc, [s_pe, s_act, s_v, s_out])
                    with nc.psum_tensor(f"qk0_{which}", [128, T], F32) as qk0, \
                         nc.psum_tensor(f"qk1_{which}", [128, T], F32) as qk1, \
                         nc.psum_tensor(f"hs_{which}", [1, T], F32) as hs_ps, \
                         nc.psum_tensor(f"hb_{which}", [128, T], F32) as hb_ps:
                        qk_ps = [qk0, qk1]
                        evQ = s_v.n
                        outQ = s_out.n
                        proj_marks, evac_marks, sq_marks, exp_marks, stt_marks, bc_marks = \
                            {}, {}, {}, {}, {}, {}

                        def _proj(oc):
                            p = oc % 2
                            if oc >= 2:
                                nc.sync.wait_ge(s_pe.h, proj_marks[oc - 2])
                            s_wA[p].inc(nc.sync.dma_start(out=w_sb[p][:, :], in_=wsrc[oc]), 16)
                            nc.tensor.wait_ge(s_wA[p].h, s_wA[p].n)
                            nc.tensor.wait_ge(s_v.h, apply_done)
                            if oc >= 2:
                                nc.tensor.wait_ge(s_act.h, evac_marks[oc - 2])
                            last = None
                            for hh in range(2):
                                ts = slice(hh * 512, (hh + 1) * 512)
                                for dk in range(CD):
                                    last = nc.tensor.matmul(out=qk_ps[p][:, ts],
                                                            lhsT=w_sb[p][:, dk * 128:(dk + 1) * 128],
                                                            rhs=xin[:, dk * T:(dk + 1) * T][:, ts],
                                                            start=(dk == 0), stop=(dk == CD - 1))
                            s_pe.inc(last)
                            proj_marks[oc] = s_pe.n

                        def _chain(oc):
                            p = oc % 2
                            nc.scalar.wait_ge(s_pe.h, proj_marks[oc])
                            if oc >= 2:
                                nc.scalar.wait_ge(s_v.h, stt_marks[oc - 2])
                            s_act.inc(nc.scalar.activation(out=qf32[p][:, :], in_=qk_ps[p][:, :],
                                                           func=AF.Identity, bias=bias_sb[:, oc:oc + 1]))
                            evac_marks[oc] = s_act.n
                            s_act.inc(nc.scalar.activation(out=scr[p][:, :], in_=qf32[p][:, :],
                                                           func=AF.Square))
                            sq_marks[oc] = s_act.n
                            nc.tensor.wait_ge(s_act.h, sq_marks[oc])
                            for hh in range(2):
                                ts = slice(hh * 512, (hh + 1) * 512)
                                s_pe.inc(nc.tensor.matmul(out=hs_ps[:, ts], lhsT=ones_bf[:, :],
                                                          rhs=scr[p][:, ts], start=True, stop=True))
                            hs_mark = s_pe.n
                            nc.scalar.wait_ge(s_pe.h, hs_mark)
                            s_act.inc(nc.scalar.activation(out=rows[0][0:1, :T], in_=hs_ps[0:1, :T],
                                                           func=AF.Ln, bias=eps_sb[0:1, 0:1],
                                                           scale=1.0 / HD))
                            s_act.inc(nc.scalar.activation(out=rows[0][0:1, :T], in_=rows[0][0:1, :T],
                                                           func=AF.Exp, scale=-0.5))
                            exp_marks[oc] = s_act.n
                            nc.tensor.wait_ge(s_act.h, exp_marks[oc])
                            if oc >= 1:
                                nc.tensor.wait_ge(s_v.h, stt_marks[oc - 1])
                            for hh in range(2):
                                ts = slice(hh * 512, (hh + 1) * 512)
                                s_pe.inc(nc.tensor.matmul(out=hb_ps[:, ts], lhsT=ones1[:, :],
                                                          rhs=rows[0][0:1, :T][:, ts],
                                                          start=True, stop=True))
                            bc_marks[oc] = s_pe.n
                            nc.vector.wait_ge(s_pe.h, bc_marks[oc])
                            if oc >= 2:
                                nc.vector.wait_ge(s_out.h, outQ + 16 * (oc - 1))
                            s_v.inc(nc.vector.scalar_tensor_tensor(out=obuf[p][:, :], in0=qf32[p][:, :],
                                                                   scalar=gh_sb[:, 0:1], in1=hb_ps[:, :],
                                                                   op0=OP.mult, op1=OP.mult))
                            stt_marks[oc] = s_v.n
                            nc.scalar.wait_ge(s_v.h, stt_marks[oc])
                            s_out.inc(nc.scalar.dma_start(out=dst_t[oc], in_=obuf[p][:, :]), 16)

                        for j in range(H + 1):
                            if j < H:
                                _proj(j)
                            if j >= 1:
                                _chain(j - 1)
                _soft_barrier(nc, [s_pe, s_act, s_v, s_out])
                with nc.psum_tensor("v0", [128, 512], F32) as v0, \
                     nc.psum_tensor("v1", [128, 512], F32) as v1:
                    v_ps = [v0, v1]
                    outV = s_out.n
                    for vg in range(4):
                        vs = slice(vg * 512, (vg + 1) * 512)
                        if vg >= 1:
                            nc.sync.wait_ge(s_pe.h, s_pe.n)
                        for dk in range(CD):
                            s_wV.inc(nc.sync.dma_start(out=wv_sb[dk][:, :], in_=wv[dk][:, vs]), 16)
                        nc.tensor.wait_ge(s_wV.h, s_wV.n)
                        for tt in range(T // 128):
                            u = vg * 8 + tt
                            p = u % 2
                            nc.tensor.wait_ge(s_v.h, apply_done)
                            if u >= 2:
                                nc.tensor.wait_ge(s_act.h, s_act.n - 1)
                            for dk in range(CD):
                                nc.tensor.matmul(out=v_ps[p][:, :], lhsT=xkv[:, dk * T + tt * 128:dk * T + (tt + 1) * 128],
                                                 rhs=wv_sb[dk][:, :], start=(dk == 0), stop=False)
                            last = nc.tensor.matmul(out=v_ps[p][:, :], lhsT=ones1b[:, :],
                                                    rhs=bv_sb[:, vs], start=False, stop=True)
                            s_pe.inc(last)
                            nc.scalar.wait_ge(s_pe.h, s_pe.n)
                            if u >= 2:
                                nc.scalar.wait_ge(s_out.h, outV + 16 * (u - 1))
                            s_act.inc(nc.scalar.activation(out=vbuf[p][:, :], in_=v_ps[p][:, :],
                                                           func=AF.Identity))
                            s_out.inc(nc.scalar.dma_start(out=v_out[tt][:, vs], in_=vbuf[p][:, :]), 16)
                nc.sync.wait_ge(s_out.h, s_out.n)
    return nc



def build_attn(causal, qk_dt=BF16):
    nc = bass.Bass("TRN2", target_bir_lowering=False)
    TK = 2048
    x_in = nc.dram_tensor("x", [CD, 128, T], F32, kind="ExternalInput").ap()
    q_in = nc.dram_tensor("q", [H, 128, T], qk_dt, kind="ExternalInput").ap()
    k_in = nc.dram_tensor("k", [H, 128, TK], qk_dt, kind="ExternalInput").ap()
    v_in = nc.dram_tensor("v", [H, 128, TK], BF16, kind="ExternalInput").ap()
    wo = nc.dram_tensor("wo", [CD, 128, CD * 128], BF16, kind="ExternalInput").ap()
    bo = nc.dram_tensor("bo", [128, CD], F32, kind="ExternalInput").ap()
    qpos = nc.dram_tensor("qpos", [1, T], F32, kind="ExternalInput").ap()
    x_out = nc.dram_tensor("x_out", [CD, 128, T], F32, kind="ExternalOutput").ap()
    NKC = TK // 128
    with ExitStack() as st:
        x_sb = st.enter_context(nc.sbuf_tensor("x_sb", [128, CD * T], F32))
        q_sb = st.enter_context(nc.sbuf_tensor("q_sb", [128, H * T], qk_dt))
        pbuf = st.enter_context(nc.sbuf_tensor("pbuf", [128, NKC * T], BF16))
        attn_sb = st.enter_context(nc.sbuf_tensor("attn_sb", [128, H * T], BF16))
        kbuf = [st.enter_context(nc.sbuf_tensor(f"kbuf{p}", [128, TK], qk_dt)) for p in range(2)]
        vbuf = [st.enter_context(nc.sbuf_tensor(f"vbuf{p}", [128, TK], BF16)) for p in range(2)]
        av_sb = [st.enter_context(nc.sbuf_tensor(f"av_sb{p}", [128, T], BF16)) for p in range(2)]
        mq2row = st.enter_context(nc.sbuf_tensor("mq2row", [1, H], F32))
        mk2row = st.enter_context(nc.sbuf_tensor("mk2row", [1, H], F32))
        negb_cols = st.enter_context(nc.sbuf_tensor("negb_cols", [128, H], F32))
        qgb = st.enter_context(nc.sbuf_tensor("qgb", [128, T], F16))
        kcols = st.enter_context(nc.sbuf_tensor("kcols", [128, NKC], F32))
        kcols_i = st.enter_context(nc.sbuf_tensor("kcols_i", [128, NKC], I32))
        rows = [st.enter_context(nc.sbuf_tensor(f"rows{i}", [1, T], F32)) for i in range(2)]
        r2 = st.enter_context(nc.sbuf_tensor("r2", [1, 512], F32))
        sc1 = [st.enter_context(nc.sbuf_tensor(f"sc1_{i}", [1, 1], F32)) for i in range(4)]
        ones_bf = st.enter_context(nc.sbuf_tensor("ones_bf", [128, 1], BF16))
        ones1 = st.enter_context(nc.sbuf_tensor("ones1", [1, 128], F32))
        bo_sb = st.enter_context(nc.sbuf_tensor("bo_sb", [128, CD], F32))
        wo_sb = [st.enter_context(nc.sbuf_tensor(f"wo{p}", [128, CD * 128], BF16)) for p in range(2)]
        s_x = Ctr(st, nc, "s_x")
        s_k = [Ctr(st, nc, f"s_k{p}") for p in range(2)]
        s_vv = [Ctr(st, nc, f"s_vv{p}") for p in range(2)]
        s_wO = [Ctr(st, nc, f"s_wO{p}") for p in range(2)]
        s_pe = Ctr(st, nc, "s_pe")
        s_act = Ctr(st, nc, "s_act")
        s_v = Ctr(st, nc, "s_v")
        s_out = Ctr(st, nc, "s_out")
        s_hd = Ctr(st, nc, "s_hd")
        with nc.Block() as block:
            @block.sync
            def _(sync):
                for dk in range(CD):
                    s_x.inc(nc.sync.dma_start(out=x_sb[:, dk * T:(dk + 1) * T], in_=x_in[dk]), 16)
                for hh in range(H):
                    s_x.inc(nc.sync.dma_start(out=q_sb[:, hh * T:(hh + 1) * T], in_=q_in[hh]), 16)
                s_x.inc(nc.sync.dma_start(out=bo_sb[:, :], in_=bo), 16)
                s_x.inc(nc.sync.dma_start(out=rows[0][0:1, :], in_=qpos), 16)
                s_v.inc(nc.vector.memset(ones_bf[:, :], 1.0))
                s_v.inc(nc.vector.memset(ones1[:, :], 1.0))
                nc.gpsimd.iota(out=kcols_i[:, :], pattern=[[128, NKC]], base=0,
                               channel_multiplier=1).then_inc(s_v.h, 1)
                s_v.n += 1
                nc.vector.wait_ge(s_v.h, 3)
                nc.vector.wait_ge(s_x.h, s_x.n)
                s_v.inc(nc.vector.tensor_copy(out=kcols[:, :], in_=kcols_i[:, :]))
                with nc.psum_tensor("qg_ps", [128, T], F32) as qg_ps:
                    nc.tensor.wait_ge(s_v.h, 2)
                    nc.tensor.wait_ge(s_x.h, s_x.n)
                    for hh in range(2):
                        ts = slice(hh * 512, (hh + 1) * 512)
                        s_pe.inc(nc.tensor.matmul(out=qg_ps[:, ts], lhsT=ones1[:, :],
                                                  rhs=rows[0][0:1, :][:, ts], start=True, stop=True))
                    nc.scalar.wait_ge(s_pe.h, s_pe.n)
                    s_act.inc(nc.scalar.activation(out=qgb[:, :], in_=qg_ps[:, :], func=AF.Identity))
                with nc.psum_tensor("sc0", [128, 512], F32) as sc0, \
                     nc.psum_tensor("sc1p", [128, 512], F32) as sc1p, \
                     nc.psum_tensor("av", [128, T], F32) as av_ps, \
                     nc.psum_tensor("sums", [1, T], F32) as sum_ps, \
                     nc.psum_tensor("rb", [128, T], F32) as rb_ps:
                    sc_ps = [sc0, sc1p]
                    nc.scalar.wait_ge(s_x.h, s_x.n)
                    s_act.inc(nc.scalar.activation(out=pbuf[:, :H * T], in_=q_sb[:, :], func=AF.Square))
                    red_marks = []
                    for h in range(H):
                        reg = sum_ps[0:1, :T] if h % 2 == 0 else rb_ps[0:1, :T]
                        nc.tensor.wait_ge(s_act.h, s_act.n)
                        if h >= 2:
                            nc.tensor.wait_ge(s_v.h, red_marks[h - 2])
                        for hh in range(2):
                            cs = slice(hh * 512, (hh + 1) * 512)
                            s_pe.inc(nc.tensor.matmul(out=reg[:, cs], lhsT=ones_bf[:, :],
                                                      rhs=pbuf[:, h * T:(h + 1) * T][:, cs],
                                                      start=True, stop=True))
                        nc.vector.wait_ge(s_pe.h, s_pe.n)
                        s_v.inc(nc.vector.reduce_max(out=mq2row[0:1, h:h + 1], in_=reg, axis=AX.X))
                        red_marks.append(s_v.n)
                    nc.scalar.wait_ge(s_pe.h, s_pe.n)
                    for h in range(H):
                        p = h % 2
                        if h >= 2:
                            nc.sync.wait_ge(s_v.h, s_v.n - 4)
                        s_k[p].inc(nc.sync.dma_start(out=kbuf[p][:, :], in_=k_in[h]), 16)
                        nc.scalar.wait_ge(s_k[p].h, s_k[p].n)
                        s_act.inc(nc.scalar.activation(out=pbuf[:, p * TK:(p + 1) * TK],
                                                       in_=kbuf[p][:, :], func=AF.Square))
                        nc.tensor.wait_ge(s_act.h, s_act.n)
                        if h >= 1:
                            nc.tensor.wait_ge(s_v.h, s_v.n)
                        for ks in range(4):
                            dst = sum_ps if ks < 2 else rb_ps
                            cs = slice((ks % 2) * 512, (ks % 2) * 512 + 512)
                            s_pe.inc(nc.tensor.matmul(out=dst[0:1, cs], lhsT=ones_bf[:, :],
                                                      rhs=pbuf[:, p * TK + ks * 512:p * TK + (ks + 1) * 512],
                                                      start=True, stop=True))
                        nc.vector.wait_ge(s_pe.h, s_pe.n)
                        s_v.inc(nc.vector.reduce_max(out=sc1[0][0:1, :], in_=sum_ps[0:1, :], axis=AX.X))
                        s_v.inc(nc.vector.reduce_max(out=sc1[3][0:1, :], in_=rb_ps[0:1, :T], axis=AX.X))
                        s_v.inc(nc.vector.tensor_tensor(out=mk2row[0:1, h:h + 1], in0=sc1[0][0:1, :],
                                                        in1=sc1[3][0:1, :], op=OP.max))
                    nc.scalar.wait_ge(s_v.h, s_v.n)
                    s_act.inc(nc.scalar.activation(out=mq2row[0:1, :], in_=mq2row[0:1, :], func=AF.Sqrt))
                    s_act.inc(nc.scalar.activation(out=mk2row[0:1, :], in_=mk2row[0:1, :], func=AF.Sqrt))
                    nc.vector.wait_ge(s_act.h, s_act.n)
                    s_v.inc(nc.vector.tensor_tensor(out=mq2row[0:1, :], in0=mq2row[0:1, :],
                                                    in1=mk2row[0:1, :], op=OP.mult))
                    s_v.inc(nc.vector.tensor_scalar(out=mq2row[0:1, :], in0=mq2row[0:1, :],
                                                    scalar1=-INVSQRT_HD, scalar2=None, op0=OP.mult))
                    nc.tensor.wait_ge(s_v.h, s_v.n)
                    s_pe.inc(nc.tensor.matmul(out=rb_ps[:, 0:H], lhsT=ones1[:, :],
                                              rhs=mq2row[0:1, :], start=True, stop=True))
                    nc.scalar.wait_ge(s_pe.h, s_pe.n)
                    s_act.inc(nc.scalar.activation(out=negb_cols[:, :], in_=rb_ps[:, 0:H], func=AF.Identity))
                    nc.tensor.wait_ge(s_act.h, s_act.n)
                    nc.tensor.wait_ge(s_v.h, s_v.n)
                    for h in range(H):
                        p = h % 2
                        if h >= 2:
                            nc.sync.wait_ge(s_hd.h, h - 1)
                        s_k[p].inc(nc.sync.dma_start(out=kbuf[p][:, :], in_=k_in[h]), 16)
                        s_vv[p].inc(nc.sync.dma_start(out=vbuf[p][:, :], in_=v_in[h]), 16)
                        negb = negb_cols[:, h:h + 1]
                        bound_act = s_act.n
                        nc.tensor.wait_ge(s_vv[p].h, s_vv[p].n)
                        nc.tensor.wait_ge(s_k[p].h, s_k[p].n)
                        units = [(kc, qh) for kc in range(NKC) for qh in range(2)]

                        def _sums_av(u):
                            kc, qh = units[u]
                            ts = slice(qh * 512, (qh + 1) * 512)
                            psl = slice(kc * T + qh * 512, kc * T + (qh + 1) * 512)
                            s_pe.inc(nc.tensor.matmul(out=sum_ps[0:1, ts], lhsT=ones_bf[:, :],
                                                      rhs=pbuf[:, psl], start=(kc == 0),
                                                      stop=(kc == NKC - 1)))
                            s_pe.inc(nc.tensor.matmul(out=av_ps[:, ts],
                                                      lhsT=vbuf[p][:, kc * 128:(kc + 1) * 128],
                                                      rhs=pbuf[:, psl], start=(kc == 0),
                                                      stop=(kc == NKC - 1)))

                        mask_marks = []
                        for u, (kc, qh) in enumerate(units):
                            up = u % 2
                            ts = slice(qh * 512, (qh + 1) * 512)
                            psl = slice(kc * T + qh * 512, kc * T + (qh + 1) * 512)
                            if u >= 2:
                                nc.tensor.wait_ge(s_act.h, bound_act + (u - 1))
                            s_pe.inc(nc.tensor.matmul(out=sc_ps[up][:, :],
                                                      lhsT=kbuf[p][:, kc * 128:(kc + 1) * 128],
                                                      rhs=q_sb[:, h * T:(h + 1) * T][:, ts],
                                                      start=True, stop=True))
                            nc.scalar.wait_ge(s_pe.h, s_pe.n)
                            s_act.inc(nc.scalar.activation(out=pbuf[:, psl], in_=sc_ps[up][:, :],
                                                           func=AF.Exp, bias=negb,
                                                           scale=INVSQRT_HD))
                            if causal:
                                nc.vector.wait_ge(s_act.h, s_act.n)
                                s_v.inc(nc.vector.scalar_tensor_tensor(
                                    out=pbuf[:, psl], in0=qgb[:, ts], scalar=kcols[:, kc:kc + 1],
                                    in1=pbuf[:, psl], op0=OP.is_ge, op1=OP.mult))
                            mask_marks.append(s_v.n if causal else s_act.n)
                            if u >= 1:
                                if causal:
                                    nc.tensor.wait_ge(s_v.h, mask_marks[u - 1])
                                else:
                                    nc.tensor.wait_ge(s_act.h, mask_marks[u - 1])
                                _sums_av(u - 1)
                        if causal:
                            nc.tensor.wait_ge(s_v.h, mask_marks[-1])
                        else:
                            nc.tensor.wait_ge(s_act.h, mask_marks[-1])
                        _sums_av(len(units) - 1)
                        nc.vector.wait_ge(s_pe.h, s_pe.n)
                        s_v.inc(nc.vector.tensor_copy(out=rows[1][0:1, :], in_=sum_ps[0:1, :]))
                        s_v.inc(nc.vector.reciprocal(out=rows[1][0:1, :], in_=rows[1][0:1, :]))
                        nc.tensor.wait_ge(s_v.h, s_v.n)
                        for hh in range(2):
                            ts = slice(hh * 512, (hh + 1) * 512)
                            s_pe.inc(nc.tensor.matmul(out=rb_ps[:, ts], lhsT=ones1[:, :],
                                                      rhs=rows[1][0:1, :][:, ts], start=True, stop=True))
                        nc.scalar.wait_ge(s_pe.h, s_pe.n)
                        s_act.inc(nc.scalar.activation(out=av_sb[p][:, :], in_=av_ps[:, :], func=AF.Identity))
                        nc.vector.wait_ge(s_act.h, s_act.n)
                        s_v.inc(nc.vector.tensor_tensor(out=attn_sb[:, h * T:(h + 1) * T],
                                                        in0=av_sb[p][:, :], in1=rb_ps[:, :], op=OP.mult))
                        s_hd.inc(nc.vector.engine_nop())
                _soft_barrier(nc, [s_pe, s_act, s_v, s_out])
                with nc.psum_tensor("o0", [128, T], F32) as o0, \
                     nc.psum_tensor("o1", [128, T], F32) as o1:
                    o_ps = [o0, o1]
                    attn_done = s_v.n
                    peO = s_pe.n
                    evO_base = s_v.n
                    for oc in range(CD):
                        p = oc % 2
                        if oc >= 2:
                            nc.sync.wait_ge(s_pe.h, peO + oc - 1)
                        s_wO[p].inc(nc.sync.dma_start(out=wo_sb[p][:, :], in_=wo[oc]), 16)
                        nc.tensor.wait_ge(s_wO[p].h, s_wO[p].n)
                        nc.tensor.wait_ge(s_v.h, attn_done)
                        if oc >= 2:
                            nc.tensor.wait_ge(s_v.h, evO_base + oc - 1)
                        last = None
                        for hh in range(2):
                            ts = slice(hh * 512, (hh + 1) * 512)
                            for dk in range(CD):
                                last = nc.tensor.matmul(out=o_ps[p][:, ts],
                                                        lhsT=wo_sb[p][:, dk * 128:(dk + 1) * 128],
                                                        rhs=attn_sb[:, dk * T:(dk + 1) * T][:, ts],
                                                        start=(dk == 0), stop=(dk == CD - 1))
                        s_pe.inc(last)
                        nc.vector.wait_ge(s_pe.h, s_pe.n)
                        s_v.inc(nc.vector.scalar_tensor_tensor(
                            out=x_sb[:, oc * T:(oc + 1) * T], in0=o_ps[p][:, :], scalar=bo_sb[:, oc:oc + 1],
                            in1=x_sb[:, oc * T:(oc + 1) * T], op0=OP.add, op1=OP.add))
                nc.sync.wait_ge(s_v.h, s_v.n)
                for dk in range(CD):
                    s_out.inc(nc.sync.dma_start(out=x_out[dk], in_=x_sb[:, dk * T:(dk + 1) * T]), 16)
                nc.sync.wait_ge(s_out.h, s_out.n)
    return nc



def build_wcast():
    nc = bass.Bass("TRN2", target_bir_lowering=False)
    wc_in = nc.dram_tensor("wc_in", [32, 128, CD * 128], F32, kind="ExternalInput").ap()
    wc_out = nc.dram_tensor("wc_out", [32, 128, CD * 128], BF16, kind="ExternalOutput").ap()
    with ExitStack() as st:
        buf = [st.enter_context(nc.sbuf_tensor(f"b{p}", [128, CD * 128], BF16)) for p in range(2)]
        s_in = [Ctr(st, nc, f"s_in{p}") for p in range(2)]
        s_out = [Ctr(st, nc, f"s_out{p}") for p in range(2)]
        with nc.Block() as block:
            @block.sync
            def _(sync):
                for u in range(32):
                    p = u % 2
                    if u >= 2:
                        nc.gpsimd.wait_ge(s_out[p].h, s_out[p].n)
                    s_in[p].inc(nc.gpsimd.dma_start(out=buf[p][:, :], in_=wc_in[u]), 16)
                    nc.sync.wait_ge(s_in[p].h, s_in[p].n)
                    s_out[p].inc(nc.sync.dma_start(out=wc_out[u], in_=buf[p][:, :]), 16)
                nc.sync.wait_ge(s_out[0].h, s_out[0].n)
                nc.sync.wait_ge(s_out[1].h, s_out[1].n)
    return nc


def _wcast_host(arrs):
    cat = np.ascontiguousarray(np.concatenate(arrs, axis=0), dtype=np.float32)
    n = cat.shape[0]
    assert n == 256, n
    per = n // NCORES
    in_maps = [{"wc_in": np.ascontiguousarray(cat[c * per:(c + 1) * per])} for c in range(NCORES)]
    res = _run("wcast", build_wcast, in_maps)
    out = np.concatenate([r["wc_out"] for r in res], axis=0)
    outs = []
    off = 0
    for a in arrs:
        outs.append(np.ascontiguousarray(out[off:off + a.shape[0]]))
        off += a.shape[0]
    return outs



def build_prep():
    nc = bass.Bass("TRN2", target_bir_lowering=False)
    gu_idx = nc.dram_tensor("gu_idx", [12, 128, CD * 128], I32, kind="ExternalInput").ap()
    dn_idx = nc.dram_tensor("dn_idx", [2, 128, C0 * 128], I32, kind="ExternalInput").ap()
    lut = nc.dram_tensor("lut", [1, 16], F32, kind="ExternalInput").ap()
    w2t = nc.dram_tensor("w2t", [C1, 128, 256], F32, kind="ExternalInput").ap()
    h128 = nc.dram_tensor("h128", [128, 128], F32, kind="ExternalInput").ap()
    wc_in = nc.dram_tensor("wc_in", [32, 128, CD * 128], F32, kind="ExternalInput").ap()
    gu_w = nc.dram_tensor("gu_w", [12, 128, CD * 128], BF16, kind="ExternalOutput").ap()
    dn_w = nc.dram_tensor("dn_w", [2, 128, C0 * 128], BF16, kind="ExternalOutput").ap()
    w2f = nc.dram_tensor("w2f", [C1, 128, 256], BF16, kind="ExternalOutput").ap()
    wc_out = nc.dram_tensor("wc_out", [32, 128, CD * 128], BF16, kind="ExternalOutput").ap()
    WMAX = C0 * 128
    W = 256
    with ExitStack() as st:
        idx_sb = [st.enter_context(nc.sbuf_tensor(f"idx{p}", [128, WMAX], I32)) for p in range(2)]
        idxb = st.enter_context(nc.sbuf_tensor("idxb", [128, WMAX], BF16))
        w_bf = [st.enter_context(nc.sbuf_tensor(f"wbf{p}", [128, WMAX], BF16)) for p in range(2)]
        msk = st.enter_context(nc.sbuf_tensor("msk", [128, WMAX], BF16))
        lut_sb = st.enter_context(nc.sbuf_tensor("lut_sb", [1, 16], F32))
        lut_col = st.enter_context(nc.sbuf_tensor("lut_col", [128, 16], F32))
        ones1 = st.enter_context(nc.sbuf_tensor("ones1", [1, 128], F32))
        lut_ps = st.enter_context(nc.psum_tensor("lut_ps", [128, 16], F32))
        s_lut = Ctr(st, nc, "s_lut")
        s_in = [Ctr(st, nc, f"s_in{p}") for p in range(2)]
        s_pe = Ctr(st, nc, "s_pe")
        s_v = Ctr(st, nc, "s_v")
        s_out = Ctr(st, nc, "s_out")
        f_abf = st.enter_context(nc.sbuf_tensor("f_abf", [128, C1 * 128], BF16))
        f_a = st.enter_context(nc.sbuf_tensor("f_a", [128, C1 * 128], F32))
        f_b = st.enter_context(nc.sbuf_tensor("f_b", [128, C1 * 128], F32))
        h_sb = st.enter_context(nc.sbuf_tensor("h_sb", [128, 128], BF16))
        f_ps = [st.enter_context(nc.psum_tensor(f"f_ps{p}", [128, 128], F32)) for p in range(2)]
        f_in = Ctr(st, nc, "f_in")
        f_pe = Ctr(st, nc, "f_pe")
        f_a_c = Ctr(st, nc, "f_a_c")
        f_out = Ctr(st, nc, "f_out")
        c_buf = [st.enter_context(nc.sbuf_tensor(f"c_b{p}", [128, CD * 128], BF16)) for p in range(2)]
        c_in = [Ctr(st, nc, f"c_in{p}") for p in range(2)]
        c_out = [Ctr(st, nc, f"c_out{p}") for p in range(2)]
        with nc.Block() as block:
            @block.sync
            def _(sync):
                for u in range(32):
                    p = u % 2
                    if u >= 2:
                        nc.gpsimd.wait_ge(c_out[p].h, c_out[p].n)
                    c_in[p].inc(nc.gpsimd.dma_start(out=c_buf[p][:, :], in_=wc_in[u]), 16)
                    nc.scalar.wait_ge(c_in[p].h, c_in[p].n)
                    c_out[p].inc(nc.scalar.dma_start(out=wc_out[u], in_=c_buf[p][:, :]), 16)
                f_in.inc(nc.gpsimd.dma_start(out=h_sb[:, :], in_=h128), 16)
                FW = 128
                for sl in range(2):
                    co = sl * FW
                    if sl >= 1:
                        nc.gpsimd.wait_ge(f_pe.h, f_pe.n)
                    for c in range(C1):
                        f_in.inc(nc.gpsimd.dma_start(out=f_abf[:, c * FW:(c + 1) * FW],
                                                     in_=w2t[c][:, co:co + FW]), 16)
                    nc.tensor.wait_ge(f_in.h, f_in.n)
                    pe0 = f_pe.n
                    ac0 = f_a_c.n
                    for c in range(C1):
                        p = c % 2
                        if c >= 2:
                            nc.tensor.wait_ge(f_a_c.h, ac0 + c - 1)
                        f_pe.inc(nc.tensor.matmul(out=f_ps[p][:, 0:FW], lhsT=h_sb[:, :],
                                                  rhs=f_abf[:, c * FW:(c + 1) * FW], start=True, stop=True))
                        nc.scalar.wait_ge(f_pe.h, f_pe.n)
                        f_a_c.inc(nc.scalar.activation(out=f_b[:, c * FW:(c + 1) * FW], in_=f_ps[p][:, 0:FW],
                                                       func=AF.Identity, scale=1.0 / math.sqrt(I1)))
                    nc.vector.wait_ge(f_a_c.h, f_a_c.n)
                    if sl >= 1:
                        nc.vector.wait_ge(f_out.h, 16 * C1)
                    fsrc, fdst = f_b, f_a
                    last = None
                    for s in range(6):
                        step = 1 << s
                        for c in range(C1):
                            if c & step:
                                continue
                            cc = c + step
                            nc.vector.tensor_tensor(out=fdst[:, c * FW:(c + 1) * FW],
                                                    in0=fsrc[:, c * FW:(c + 1) * FW],
                                                    in1=fsrc[:, cc * FW:(cc + 1) * FW], op=OP.add)
                            last = nc.vector.tensor_tensor(out=fdst[:, cc * FW:(cc + 1) * FW],
                                                           in0=fsrc[:, c * FW:(c + 1) * FW],
                                                           in1=fsrc[:, cc * FW:(cc + 1) * FW],
                                                           op=OP.subtract)
                        fsrc, fdst = fdst, fsrc
                    f_a_c.inc(last)
                    nc.gpsimd.wait_ge(f_a_c.h, f_a_c.n)
                    for c in range(C1):
                        f_out.inc(nc.gpsimd.dma_start(out=w2f[c][:, co:co + FW],
                                                      in_=fsrc[:, c * FW:(c + 1) * FW]), 16)
                s_lut.inc(nc.sync.dma_start(out=lut_sb[:, :], in_=lut), 16)
                nc.vector.wait_ge(s_lut.h, 16)
                s_v.inc(nc.vector.memset(ones1[:, :], 1.0))
                nc.tensor.wait_ge(s_v.h, 1)
                s_pe.inc(nc.tensor.matmul(out=lut_ps[:, :], lhsT=ones1[:, :], rhs=lut_sb[:, :],
                                          start=True, stop=True))
                nc.scalar.wait_ge(s_pe.h, s_pe.n)
                s_v.inc(nc.scalar.copy(out=lut_col[:, :], in_=lut_ps[:, :]))
                units = [(gu_idx, gu_w, CD * 128)] * 12 + [(dn_idx, dn_w, C0 * 128)] * 2
                for ui, (usrc, udst, width) in enumerate(units):
                    u = ui if ui < 12 else ui - 12
                    p = ui % 2
                    if ui >= 2:
                        nc.sync.wait_ge(s_v.h, 2 + (ui - 2) + 1)
                    s_in[p].inc(nc.sync.dma_start(out=idx_sb[p][:, :width], in_=usrc[u]), 16)
                    nc.vector.wait_ge(s_in[p].h, s_in[p].n)
                    nc.vector.wait_ge(s_v.h, 2)
                    if ui >= 2:
                        nc.vector.wait_ge(s_out.h, 16 * (ui - 1))
                    nc.vector.tensor_copy(out=idxb[:, :width], in_=idx_sb[p][:, :width])
                    last = None
                    for k in range(16):
                        if k == 0:
                            nc.vector.tensor_scalar(out=w_bf[p][:, :width], in0=idxb[:, :width],
                                                    scalar1=0.0, scalar2=lut_col[:, 0:1],
                                                    op0=OP.is_equal, op1=OP.mult)
                        else:
                            nc.vector.tensor_scalar(out=msk[:, :width], in0=idxb[:, :width],
                                                    scalar1=float(k), scalar2=lut_col[:, k:k + 1],
                                                    op0=OP.is_equal, op1=OP.mult)
                            last = nc.vector.tensor_tensor(out=w_bf[p][:, :width],
                                                           in0=w_bf[p][:, :width],
                                                           in1=msk[:, :width], op=OP.add)
                    s_v.inc(last)
                    nc.gpsimd.wait_ge(s_v.h, 2 + ui + 1)
                    s_out.inc(nc.gpsimd.dma_start(out=udst[u], in_=w_bf[p][:, :width]), 16)
                nc.gpsimd.wait_ge(s_out.h, 16 * 14)
                nc.gpsimd.wait_ge(f_out.h, 16 * 2 * C1)
                nc.gpsimd.wait_ge(c_out[0].h, c_out[0].n)
                nc.gpsimd.wait_ge(c_out[1].h, c_out[1].n)
    return nc


def _prep_host(lut, gate_idx, up_idx, down_idx, W2, cast_arrs):
    gidxT = np.ascontiguousarray(gate_idx.T)
    uidxT = np.ascontiguousarray(up_idx.T)
    didxT = np.ascontiguousarray(down_idx.T)
    gu = np.concatenate([_pw(gidxT, pad_oc=C0P), _pw(uidxT, pad_oc=C0P)], axis=0)
    dn = _pw(didxT)
    lut2 = np.ascontiguousarray(np.asarray(lut, dtype=np.float32).reshape(1, 16))
    w2t = np.ascontiguousarray(np.asarray(W2, dtype=np.float32).T)
    h128 = _hadamard(128)
    cat = np.ascontiguousarray(np.concatenate(cast_arrs, axis=0), dtype=np.float32)
    per = cat.shape[0] // NCORES
    in_maps = []
    for c in range(NCORES):
        in_maps.append({
            "gu_idx": np.ascontiguousarray(gu[c * 12:(c + 1) * 12]),
            "dn_idx": np.ascontiguousarray(dn[c * 2:(c + 1) * 2]),
            "lut": lut2,
            "w2t": np.ascontiguousarray(w2t[:, c * 256:(c + 1) * 256].reshape(C1, 128, 256)),
            "h128": h128,
            "wc_in": np.ascontiguousarray(cat[c * per:(c + 1) * per]),
        })
    res = _run("prep", build_prep, in_maps)
    gu_w = np.concatenate([r["gu_w"] for r in res], axis=0)
    dn_w = np.concatenate([r["dn_w"] for r in res], axis=0)
    wg0 = np.ascontiguousarray(gu_w[:C0])
    wu0 = np.ascontiguousarray(gu_w[C0P:C0P + C0])
    wd0 = np.ascontiguousarray(dn_w)
    w2f_full = np.empty((I1, D), dtype=NPBF16)
    for c in range(NCORES):
        w2f_full[:, c * 256:(c + 1) * 256] = res[c]["w2f"].reshape(I1, 256)
    w2f_t = _pw(w2f_full)
    cast_out = np.concatenate([r["wc_out"] for r in res], axis=0)
    outs = []
    off = 0
    for a in cast_arrs:
        outs.append(np.ascontiguousarray(cast_out[off:off + a.shape[0]]))
        off += a.shape[0]
    return wg0, wu0, wd0, w2f_t, outs



def _shard_tokens(x):
    shards = []
    for c in range(NCORES):
        b, hh = c // 2, c % 2
        shards.append(_fm(x[b, hh * T:(hh + 1) * T, :]).reshape(CD, 128, T))
    return shards


def _unshard_tokens(shards):
    out = np.empty((4, 2048, D), dtype=np.float32)
    for c in range(NCORES):
        b, hh = c // 2, c % 2
        out[b, hh * T:(hh + 1) * T, :] = shards[c].reshape(D, T).T
    return out


def _run(key, build_fn, in_maps):
    if key not in _PROG_CACHE:
        _PROG_CACHE[key] = build_fn()
    res = run_bass_kernel_spmd(_PROG_CACHE[key], in_maps, core_ids=list(range(NCORES)))
    return res.results


def _ghost_host(lut, gate_idx, up_idx, down_idx):
    gidxT = np.ascontiguousarray(gate_idx.T)
    uidxT = np.ascontiguousarray(up_idx.T)
    didxT = np.ascontiguousarray(down_idx.T)
    gu = np.concatenate([_pw(gidxT, pad_oc=C0P), _pw(uidxT, pad_oc=C0P)], axis=0)
    dn = _pw(didxT)
    lut2 = np.ascontiguousarray(np.asarray(lut, dtype=np.float32).reshape(1, 16))
    in_maps = [{"gu_idx": np.ascontiguousarray(gu[c * 12:(c + 1) * 12]),
                "dn_idx": np.ascontiguousarray(dn[c * 2:(c + 1) * 2]),
                "lut": lut2} for c in range(NCORES)]
    res = _run("ghost", build_ghost, in_maps)
    gu_w = np.concatenate([r["gu_w"] for r in res], axis=0)
    dn_w = np.concatenate([r["dn_w"] for r in res], axis=0)
    return (np.ascontiguousarray(gu_w[:C0]), np.ascontiguousarray(gu_w[C0P:C0P + C0]),
            np.ascontiguousarray(dn_w))


def _w2fold_host(W2):
    w2t = np.ascontiguousarray(np.asarray(W2, dtype=np.float32).T)
    h128 = _hadamard(128)
    in_maps = [{"w2t": np.ascontiguousarray(w2t[:, c * 256:(c + 1) * 256].reshape(C1, 128, 256)),
                "h128": h128} for c in range(NCORES)]
    res = _run("w2fold", build_w2fold, in_maps)
    w2f = np.empty((I1, D), dtype=NPBF16)
    for c in range(NCORES):
        w2f[:, c * 256:(c + 1) * 256] = res[c]["w2f"].reshape(I1, 256)
    return _pw(w2f)


def _ffn_host(x_shards, wg_t, wu_t, wd_t, C):
    wdtype = BF16 if wg_t.dtype == NPBF16 else F32
    wdtype_d = BF16 if wd_t.dtype == NPBF16 else F32
    key = ("ffn", C, str(wg_t.dtype), str(wd_t.dtype))
    in_maps = [{"x": x_shards[c], "wg": wg_t, "wu": wu_t, "wd": wd_t} for c in range(NCORES)]
    res = _run(key, lambda: build_ffn(C, wdtype, wdtype_d), in_maps)
    return [np.ascontiguousarray(r["x_out"]) for r in res]


def _attn_host(x_shards, qs, ks, vs, wo_t, bo_cols, causal, qk_dt):
    in_maps = []
    for c in range(NCORES):
        b, hh = c // 2, c % 2
        k_full = np.ascontiguousarray(np.concatenate([ks[2 * b], ks[2 * b + 1]], axis=2))
        v_full = np.concatenate([vs[2 * b], vs[2 * b + 1]], axis=0)
        v_h = np.ascontiguousarray(
            v_full.reshape(16, 128, 16, 128).transpose(2, 1, 0, 3).reshape(H, 128, 2048))
        qpos = np.arange(hh * T, (hh + 1) * T, dtype=np.float32).reshape(1, T)
        in_maps.append({"x": x_shards[c], "q": qs[c], "k": k_full, "v": v_h,
                        "wo": wo_t, "bo": bo_cols, "qpos": qpos})
    res = _run(("attn", causal, str(qk_dt)), lambda: build_attn(causal, qk_dt), in_maps)
    return [np.ascontiguousarray(r["x_out"]) for r in res]


def kernel(L0_attn_x, L0_attn_qkv_w, L0_attn_qkv_b, L0_attn_out_w, L0_attn_out_b,
           L0_ffn_lut, L0_ffn_gate_idx, L0_ffn_up_idx, L0_ffn_down_idx,
           L1_attn_g_q, L1_attn_Wq, L1_attn_bq, L1_attn_g_kv, L1_attn_Wkv, L1_attn_bkv,
           L1_attn_g_head, L1_attn_Wo, L1_attn_bo, L1_ffn_W1, L1_ffn_W2, L1_ffn_W3):
    f32 = lambda a: np.asarray(a, dtype=np.float32)
    qkvT = np.ascontiguousarray(f32(L0_attn_qkv_w).T)
    wqk0_t = _pw(np.ascontiguousarray(qkvT[:, :2 * D]))
    wv0_r = np.ascontiguousarray(qkvT[:, 2 * D:].reshape(CD, 128, D))
    bqk0 = _pb(f32(L0_attn_qkv_b)[:2 * D])
    bv0 = np.ascontiguousarray(f32(L0_attn_qkv_b)[2 * D:].reshape(1, D))
    wo0_t = _pw(np.ascontiguousarray(f32(L0_attn_out_w).T))
    bo0 = _pb(f32(L0_attn_out_b))
    wq1_t = _pw(f32(L1_attn_Wq))
    wk1_t = _pw(np.ascontiguousarray(f32(L1_attn_Wkv)[:, :D]))
    wv1_r = np.ascontiguousarray(f32(L1_attn_Wkv)[:, D:].reshape(CD, 128, D))
    wg1_t = _pw(np.ascontiguousarray(f32(L1_ffn_W1).T))
    wu1_t = _pw(np.ascontiguousarray(f32(L1_ffn_W3).T))
    wg0, wu0, wd0, w2f, casts = _prep_host(
        f32(L0_ffn_lut), np.asarray(L0_ffn_gate_idx), np.asarray(L0_ffn_up_idx),
        np.asarray(L0_ffn_down_idx), f32(L1_ffn_W2),
        [wqk0_t, wv0_r.reshape(CD, 128, D), wo0_t, wq1_t, wk1_t, wv1_r.reshape(CD, 128, D),
         _pw(f32(L1_attn_Wo)), wg1_t, wu1_t])
    (wqk0_t, wv0_r, wo0_t, wq1_t, wk1_t, wv1_r, wo1_t, wg1_t, wu1_t) = casts
    bq1 = _pb(f32(L1_attn_bq))
    bk1 = _pb(f32(L1_attn_bkv)[:D])
    bv1 = np.ascontiguousarray(f32(L1_attn_bkv)[D:].reshape(1, D))
    gq1 = _pb(f32(L1_attn_g_q))
    gkv1 = _pb(f32(L1_attn_g_kv))
    gh1 = np.ascontiguousarray(f32(L1_attn_g_head).reshape(128, 1))
    bo1 = _pb(f32(L1_attn_bo))

    x = _shard_tokens(f32(L0_attn_x))
    x = _ffn_host(x, wg0, wu0, wd0, C0)
    in_maps = [{"x": x[c], "wqk": wqk0_t, "wv": wv0_r, "bqk": bqk0, "bv": bv0}
               for c in range(NCORES)]
    res = _run("qkv0", build_qkv0, in_maps)
    x = _attn_host(x, [r["q"] for r in res], [r["k"] for r in res], [r["v"] for r in res],
                   wo0_t, bo0, True, BF16)
    x = _ffn_host(x, wg0, wu0, wd0, C0)
    x = _ffn_host(x, wg1_t, wu1_t, w2f, C1)
    in_maps = [{"x": x[c], "wq": wq1_t, "wk": wk1_t, "wv": wv1_r, "bq": bq1, "bk": bk1,
                "bv": bv1, "gq": gq1, "gkv": gkv1, "ghead": gh1} for c in range(NCORES)]
    res = _run("qkv1", build_qkv1, in_maps)
    x = _attn_host(x, [r["q"] for r in res], [r["k"] for r in res], [r["v"] for r in res],
                   wo1_t, bo1, False, F16)
    x = _ffn_host(x, wg1_t, wu1_t, w2f, C1)
    return _unshard_tokens(x)


# version 09
# speedup vs baseline: 1.0222x; 1.0018x over previous
import math
from contextlib import ExitStack

import numpy as np
import ml_dtypes

import concourse.bass as bass
import concourse.mybir as mybir
from concourse.bass_utils import run_bass_kernel_spmd

F32 = mybir.dt.float32
BF16 = mybir.dt.bfloat16
F16 = mybir.dt.float16
I32 = mybir.dt.int32
AF = mybir.ActivationFunctionType
OP = mybir.AluOpType
AX = mybir.AxisListType

NPBF16 = ml_dtypes.bfloat16

D = 2048
CD = D // 128
T = 1024
NCORES = 8
H = 16
HD = 128
I0 = 5504
C0 = I0 // 128
C0P = 48
I1 = 8192
C1 = I1 // 128
LN_EPS = 1e-5
RMS_EPS = float(np.finfo(np.float32).eps)
INVSQRT_HD = 1.0 / math.sqrt(HD)

_PROG_CACHE = {}



def _pw(w, pad_oc=None):
    K, O = w.shape
    KC, OC = K // 128, O // 128
    t = np.ascontiguousarray(
        w.reshape(KC, 128, OC, 128).transpose(2, 1, 0, 3).reshape(OC, 128, KC * 128))
    if pad_oc is not None and pad_oc > OC:
        t = np.concatenate([t, np.zeros((pad_oc - OC, 128, KC * 128), dtype=t.dtype)], axis=0)
    return t


def _pb(b):
    O = b.shape[0]
    return np.ascontiguousarray(np.asarray(b, dtype=np.float32).reshape(O // 128, 128).T)


def _fm(x):
    return np.ascontiguousarray(np.asarray(x, dtype=np.float32).T)


def _hadamard(n):
    h = np.array([[1.0]], dtype=np.float32)
    while h.shape[0] < n:
        h = np.block([[h, h], [h, -h]])
    return np.ascontiguousarray(h.astype(np.float32))


class Ctr:

    def __init__(self, st, nc, name):
        self.h = st.enter_context(nc.semaphore(name))
        self.n = 0

    def inc(self, instr, k=1):
        instr.then_inc(self.h, k)
        self.n += k
        return instr



def build_ghost():
    nc = bass.Bass("TRN2", target_bir_lowering=False)
    gu_idx = nc.dram_tensor("gu_idx", [12, 128, CD * 128], I32, kind="ExternalInput").ap()
    dn_idx = nc.dram_tensor("dn_idx", [2, 128, C0 * 128], I32, kind="ExternalInput").ap()
    lut = nc.dram_tensor("lut", [1, 16], F32, kind="ExternalInput").ap()
    gu_w = nc.dram_tensor("gu_w", [12, 128, CD * 128], BF16, kind="ExternalOutput").ap()
    dn_w = nc.dram_tensor("dn_w", [2, 128, C0 * 128], BF16, kind="ExternalOutput").ap()
    WMAX = C0 * 128
    with ExitStack() as st:
        idx_sb = [st.enter_context(nc.sbuf_tensor(f"idx{p}", [128, WMAX], I32)) for p in range(2)]
        idxb = st.enter_context(nc.sbuf_tensor("idxb", [128, WMAX], BF16))
        w_bf = [st.enter_context(nc.sbuf_tensor(f"wbf{p}", [128, WMAX], BF16)) for p in range(2)]
        msk = st.enter_context(nc.sbuf_tensor("msk", [128, WMAX], BF16))
        lut_sb = st.enter_context(nc.sbuf_tensor("lut_sb", [1, 16], F32))
        lut_col = st.enter_context(nc.sbuf_tensor("lut_col", [128, 16], F32))
        ones1 = st.enter_context(nc.sbuf_tensor("ones1", [1, 128], F32))
        lut_ps = st.enter_context(nc.psum_tensor("lut_ps", [128, 16], F32))
        s_lut = Ctr(st, nc, "s_lut")
        s_in = [Ctr(st, nc, f"s_in{p}") for p in range(2)]
        s_pe = Ctr(st, nc, "s_pe")
        s_v = Ctr(st, nc, "s_v")
        s_out = Ctr(st, nc, "s_out")
        with nc.Block() as block:
            @block.sync
            def _(sync):
                s_lut.inc(nc.sync.dma_start(out=lut_sb[:, :], in_=lut), 16)
                nc.vector.wait_ge(s_lut.h, 16)
                s_v.inc(nc.vector.memset(ones1[:, :], 1.0))
                nc.tensor.wait_ge(s_v.h, 1)
                s_pe.inc(nc.tensor.matmul(out=lut_ps[:, :], lhsT=ones1[:, :], rhs=lut_sb[:, :],
                                          start=True, stop=True))
                nc.scalar.wait_ge(s_pe.h, 1)
                s_v.inc(nc.scalar.copy(out=lut_col[:, :], in_=lut_ps[:, :]))

                units = [(gu_idx, gu_w, CD * 128)] * 12 + [(dn_idx, dn_w, C0 * 128)] * 2
                for ui, (src, dst, width) in enumerate(units):
                    u = ui if ui < 12 else ui - 12
                    p = ui % 2
                    if ui >= 2:
                        nc.sync.wait_ge(s_v.h, 2 + (ui - 2) + 1)
                    s_in[p].inc(nc.sync.dma_start(out=idx_sb[p][:, :width], in_=src[u]), 16)
                    nc.vector.wait_ge(s_in[p].h, s_in[p].n)
                    nc.vector.wait_ge(s_v.h, 2)
                    if ui >= 2:
                        nc.vector.wait_ge(s_out.h, 16 * (ui - 1))
                    nc.vector.tensor_copy(out=idxb[:, :width], in_=idx_sb[p][:, :width])
                    last = None
                    for k in range(16):
                        if k == 0:
                            nc.vector.tensor_scalar(out=w_bf[p][:, :width], in0=idxb[:, :width],
                                                    scalar1=0.0, scalar2=lut_col[:, 0:1],
                                                    op0=OP.is_equal, op1=OP.mult)
                        else:
                            nc.vector.tensor_scalar(out=msk[:, :width], in0=idxb[:, :width],
                                                    scalar1=float(k), scalar2=lut_col[:, k:k + 1],
                                                    op0=OP.is_equal, op1=OP.mult)
                            last = nc.vector.tensor_tensor(out=w_bf[p][:, :width],
                                                           in0=w_bf[p][:, :width],
                                                           in1=msk[:, :width], op=OP.add)
                    s_v.inc(last)
                    nc.gpsimd.wait_ge(s_v.h, 2 + ui + 1)
                    s_out.inc(nc.gpsimd.dma_start(out=dst[u], in_=w_bf[p][:, :width]), 16)
                nc.gpsimd.wait_ge(s_out.h, 16 * 14)
    return nc



def build_w2fold():
    nc = bass.Bass("TRN2", target_bir_lowering=False)
    w2t = nc.dram_tensor("w2t", [C1, 128, 256], F32, kind="ExternalInput").ap()
    h128 = nc.dram_tensor("h128", [128, 128], F32, kind="ExternalInput").ap()
    w2f = nc.dram_tensor("w2f", [C1, 128, 256], BF16, kind="ExternalOutput").ap()
    W = 256
    with ExitStack() as st:
        a_bf = st.enter_context(nc.sbuf_tensor("arena_abf", [128, C1 * W], BF16))
        a = st.enter_context(nc.sbuf_tensor("arena_a", [128, C1 * W], F32))
        b = st.enter_context(nc.sbuf_tensor("arena_b", [128, C1 * W], F32))
        h_sb = st.enter_context(nc.sbuf_tensor("h_sb", [128, 128], BF16))
        ps = [st.enter_context(nc.psum_tensor(f"ps{p}", [128, W], F32)) for p in range(2)]
        s_in = Ctr(st, nc, "s_in")
        s_pe = Ctr(st, nc, "s_pe")
        s_a = Ctr(st, nc, "s_a")
        s_out = Ctr(st, nc, "s_out")
        with nc.Block() as block:
            @block.sync
            def _(sync):
                s_in.inc(nc.gpsimd.dma_start(out=h_sb[:, :], in_=h128), 16)
                for c in range(C1):
                    s_in.inc(nc.gpsimd.dma_start(out=a_bf[:, c * W:(c + 1) * W], in_=w2t[c]), 16)
                nc.tensor.wait_ge(s_in.h, 16 * (C1 + 1))
                for c in range(C1):
                    p = c % 2
                    if c >= 2:
                        nc.tensor.wait_ge(s_a.h, c - 1)
                    s_pe.inc(nc.tensor.matmul(out=ps[p][:, :], lhsT=h_sb[:, :],
                                              rhs=a_bf[:, c * W:(c + 1) * W], start=True, stop=True))
                    nc.scalar.wait_ge(s_pe.h, c + 1)
                    s_a.inc(nc.scalar.activation(out=b[:, c * W:(c + 1) * W], in_=ps[p][:, :],
                                                 func=AF.Identity, scale=1.0 / math.sqrt(I1)))
                nc.vector.wait_ge(s_a.h, C1)
                src, dst = b, a
                last = None
                for s in range(6):
                    step = 1 << s
                    for c in range(C1):
                        if c & step:
                            continue
                        cc = c + step
                        nc.vector.tensor_tensor(out=dst[:, c * W:(c + 1) * W],
                                                in0=src[:, c * W:(c + 1) * W],
                                                in1=src[:, cc * W:(cc + 1) * W], op=OP.add)
                        last = nc.vector.tensor_tensor(out=dst[:, cc * W:(cc + 1) * W],
                                                       in0=src[:, c * W:(c + 1) * W],
                                                       in1=src[:, cc * W:(cc + 1) * W],
                                                       op=OP.subtract)
                    src, dst = dst, src
                s_a.inc(last)
                nc.gpsimd.wait_ge(s_a.h, C1 + 1)
                for c in range(C1):
                    s_out.inc(nc.gpsimd.dma_start(out=w2f[c], in_=src[:, c * W:(c + 1) * W]), 16)
                nc.gpsimd.wait_ge(s_out.h, 16 * C1)
    return nc



def emit_ln_stats(nc, s_pe, s_act, s_v, x_sb, scr_a, scr_b, stat_ps, rows, ones_bf,
                  eps_ap, inv_n, rms=False, n_tok=T):
    nh = n_tok // 512
    for dk in range(CD):
        sl = slice(dk * n_tok, (dk + 1) * n_tok)
        pa = dk % 2
        if dk >= 2:
            nc.scalar.wait_ge(s_pe.h, s_pe.n)
        if not rms:
            s_act.inc(nc.scalar.activation(out=scr_a[pa][:, :n_tok], in_=x_sb[:, sl], func=AF.Identity))
        s_act.inc(nc.scalar.activation(out=scr_b[pa][:, :n_tok], in_=x_sb[:, sl], func=AF.Square))
        nc.tensor.wait_ge(s_act.h, s_act.n)
        for hh in range(nh):
            ts = slice(hh * 512, (hh + 1) * 512)
            if not rms:
                s_pe.inc(nc.tensor.matmul(out=stat_ps[0][:, ts], lhsT=ones_bf[:, :],
                                          rhs=scr_a[pa][:, ts], start=(dk == 0), stop=(dk == CD - 1)))
            s_pe.inc(nc.tensor.matmul(out=stat_ps[1][:, ts], lhsT=ones_bf[:, :],
                                      rhs=scr_b[pa][:, ts], start=(dk == 0), stop=(dk == CD - 1)))
    nc.scalar.wait_ge(s_pe.h, s_pe.n)
    if not rms:
        s_act.inc(nc.scalar.activation(out=rows[0][0:1, :n_tok], in_=stat_ps[0][0:1, :n_tok],
                                       func=AF.Identity, scale=inv_n))
    s_act.inc(nc.scalar.activation(out=rows[1][0:1, :n_tok], in_=stat_ps[1][0:1, :n_tok],
                                   func=AF.Identity, scale=inv_n))
    nc.vector.wait_ge(s_act.h, s_act.n)
    if not rms:
        s_v.inc(nc.vector.scalar_tensor_tensor(out=rows[2][0:1, :n_tok], in0=rows[0][0:1, :n_tok],
                                               scalar=-1.0, in1=rows[0][0:1, :n_tok],
                                               op0=OP.mult, op1=OP.mult))
        s_v.inc(nc.vector.tensor_tensor(out=rows[2][0:1, :n_tok], in0=rows[1][0:1, :n_tok],
                                        in1=rows[2][0:1, :n_tok], op=OP.add))
        var_row = rows[2][0:1, :n_tok]
        nc.scalar.wait_ge(s_v.h, s_v.n)
    else:
        var_row = rows[1][0:1, :n_tok]
    s_act.inc(nc.scalar.activation(out=rows[3][0:1, :n_tok], in_=var_row, func=AF.Ln, bias=eps_ap))
    s_act.inc(nc.scalar.activation(out=rows[3][0:1, :n_tok], in_=rows[3][0:1, :n_tok], func=AF.Exp,
                                   scale=-0.5))
    if not rms:
        nc.vector.wait_ge(s_act.h, s_act.n)
        s_v.inc(nc.vector.scalar_tensor_tensor(out=rows[2][0:1, :n_tok], in0=rows[0][0:1, :n_tok],
                                               scalar=-1.0, in1=rows[3][0:1, :n_tok],
                                               op0=OP.mult, op1=OP.mult))


def emit_bcast_rows(nc, s_pe, srcs, dsts, ones1, n_tok, wait_sem=None, wait_val=None):
    if wait_sem is not None:
        nc.tensor.wait_ge(wait_sem, wait_val)
    for src, dst in zip(srcs, dsts):
        for hh in range(n_tok // 512):
            ts = slice(hh * 512, (hh + 1) * 512)
            s_pe.inc(nc.tensor.matmul(out=dst[:, ts], lhsT=ones1[:, :], rhs=src[:, ts],
                                      start=True, stop=True))



DEBUG_FFN = False


def build_ffn(C, wdtype, wdtype_d=None):
    if wdtype_d is None:
        wdtype_d = wdtype
    nc = bass.Bass("TRN2", target_bir_lowering=False)
    tb = 512
    if DEBUG_FFN:
        xh_dbg = nc.dram_tensor("xh_dbg", [CD, 128, 512], BF16, kind="ExternalOutput").ap()
        h_dbg = nc.dram_tensor("h_dbg", [C, 128, 512], BF16, kind="ExternalOutput").ap()
        wd_dbg = nc.dram_tensor("wd_dbg", [2, 128, C * 128], BF16, kind="ExternalOutput").ap()
    x_in = nc.dram_tensor("x", [CD, 128, T], F32, kind="ExternalInput").ap()
    wg = nc.dram_tensor("wg", [C, 128, CD * 128], wdtype, kind="ExternalInput").ap()
    wu = nc.dram_tensor("wu", [C, 128, CD * 128], wdtype, kind="ExternalInput").ap()
    wd = nc.dram_tensor("wd", [CD, 128, C * 128], wdtype_d, kind="ExternalInput").ap()
    x_out = nc.dram_tensor("x_out", [CD, 128, T], F32, kind="ExternalOutput").ap()
    n_tb = T // tb
    with ExitStack() as st:
        x_sb = st.enter_context(nc.sbuf_tensor("x_sb", [128, CD * T], F32))
        xhat = st.enter_context(nc.sbuf_tensor("xhat", [128, CD * tb], BF16))
        h_sb = st.enter_context(nc.sbuf_tensor("h_sb", [128, C * tb], BF16))
        scr = [st.enter_context(nc.sbuf_tensor(f"scr{p}", [128, T], BF16)) for p in range(2)]
        scrq = [st.enter_context(nc.sbuf_tensor(f"scrq{p}", [128, T], BF16)) for p in range(2)]
        scr_f = st.enter_context(nc.sbuf_tensor("scr_f", [128, tb], F32))
        sg = [st.enter_context(nc.sbuf_tensor(f"sg{p}", [128, tb], BF16)) for p in range(2)]
        rows = [st.enter_context(nc.sbuf_tensor(f"rows{i}", [1, T], F32)) for i in range(4)]
        ones_bf = st.enter_context(nc.sbuf_tensor("ones_bf", [128, 1], BF16))
        ones1 = st.enter_context(nc.sbuf_tensor("ones1", [1, 128], F32))
        eps_sb = st.enter_context(nc.sbuf_tensor("eps_sb", [1, 1], F32))
        wg_sb = [st.enter_context(nc.sbuf_tensor(f"wg{p}", [128, CD * 128], BF16)) for p in range(2)]
        wu_sb = [st.enter_context(nc.sbuf_tensor(f"wu{p}", [128, CD * 128], BF16)) for p in range(2)]
        wd1 = st.enter_context(nc.sbuf_tensor("wd1", [128, C * 128], BF16))
        s_x = Ctr(st, nc, "s_x")
        s_wA = [Ctr(st, nc, f"s_wA{p}") for p in range(2)]
        s_wD = [Ctr(st, nc, f"s_wD{p}") for p in range(2)]
        s_pe = Ctr(st, nc, "s_pe")
        s_act = Ctr(st, nc, "s_act")
        s_v = Ctr(st, nc, "s_v")
        s_out = Ctr(st, nc, "s_out")
        with nc.Block() as block:
            @block.sync
            def _(sync):
                for dk in range(CD):
                    s_x.inc(nc.sync.dma_start(out=x_sb[:, dk * T:(dk + 1) * T], in_=x_in[dk]), 16)
                s_v.inc(nc.vector.memset(ones_bf[:, :], 1.0))
                s_v.inc(nc.vector.memset(ones1[:, :], 1.0))
                s_v.inc(nc.vector.memset(eps_sb[:, :], LN_EPS))
                nc.scalar.wait_ge(s_x.h, 16 * CD)
                nc.scalar.wait_ge(s_v.h, 3)
                nc.tensor.wait_ge(s_v.h, 3)
                nc.vector.wait_ge(s_x.h, 16 * CD)
                with nc.psum_tensor("stat0", [1, T], F32) as st0, \
                     nc.psum_tensor("stat1", [1, T], F32) as st1:
                    emit_ln_stats(nc, s_pe, s_act, s_v, x_sb, scr, scrq, [st0, st1], rows,
                                  ones_bf, eps_sb[0:1, 0:1], 1.0 / D)
                for tbi in range(n_tb):
                    t0 = tbi * tb
                    with nc.psum_tensor(f"bc0_{tbi}", [128, tb], F32) as bc0, \
                         nc.psum_tensor(f"bc1_{tbi}", [128, tb], F32) as bc1:
                        emit_bcast_rows(nc, s_pe, [rows[3][0:1, t0:t0 + tb], rows[2][0:1, t0:t0 + tb]],
                                        [bc0, bc1], ones1, tb, s_v.h, s_v.n)
                        nc.vector.wait_ge(s_pe.h, s_pe.n)
                        for dk in range(CD):
                            sl = slice(dk * T + t0, dk * T + t0 + tb)
                            nc.vector.tensor_tensor(out=scr_f[:, :], in0=x_sb[:, sl],
                                                    in1=bc0[:, :], op=OP.mult)
                            last = nc.vector.tensor_tensor(out=xhat[:, dk * tb:(dk + 1) * tb],
                                                           in0=scr_f[:, :], in1=bc1[:, :], op=OP.add)
                        s_v.inc(last)
                    apply_done = s_v.n
                    if DEBUG_FFN and tbi == 0:
                        nc.sync.wait_ge(s_v.h, apply_done)
                        for dk in range(CD):
                            s_out.inc(nc.sync.dma_start(out=xh_dbg[dk], in_=xhat[:, dk * tb:(dk + 1) * tb]), 16)
                    with nc.psum_tensor(f"g0_{tbi}", [128, tb], F32) as g0, \
                         nc.psum_tensor(f"g1_{tbi}", [128, tb], F32) as g1, \
                         nc.psum_tensor(f"u0_{tbi}", [128, tb], F32) as u0, \
                         nc.psum_tensor(f"u1_{tbi}", [128, tb], F32) as u1:
                        g_ps, u_ps = [g0, g1], [u0, u1]
                        silu_base = s_act.n
                        mul_base = s_v.n
                        peA_base = s_pe.n
                        for c in range(C):
                            p = c % 2
                            if c >= 2:
                                nc.sync.wait_ge(s_pe.h, peA_base + 2 * c - 2)
                            s_wA[p].inc(nc.sync.dma_start(out=wg_sb[p][:, :], in_=wg[c]), 16)
                            s_wA[p].inc(nc.sync.dma_start(out=wu_sb[p][:, :], in_=wu[c]), 16)
                            nc.tensor.wait_ge(s_wA[p].h, s_wA[p].n)
                            nc.tensor.wait_ge(s_v.h, apply_done)
                            if c >= 2:
                                nc.tensor.wait_ge(s_act.h, silu_base + c - 1)
                                nc.tensor.wait_ge(s_v.h, mul_base + c - 1)
                            last = None
                            for dk in range(CD):
                                last = nc.tensor.matmul(out=g_ps[p][:, :],
                                                        lhsT=wg_sb[p][:, dk * 128:(dk + 1) * 128],
                                                        rhs=xhat[:, dk * tb:(dk + 1) * tb],
                                                        start=(dk == 0), stop=(dk == CD - 1))
                            s_pe.inc(last)
                            for dk in range(CD):
                                last = nc.tensor.matmul(out=u_ps[p][:, :],
                                                        lhsT=wu_sb[p][:, dk * 128:(dk + 1) * 128],
                                                        rhs=xhat[:, dk * tb:(dk + 1) * tb],
                                                        start=(dk == 0), stop=(dk == CD - 1))
                            s_pe.inc(last)
                            nc.scalar.wait_ge(s_pe.h, s_pe.n - 1)
                            if c >= 2:
                                nc.scalar.wait_ge(s_v.h, mul_base + c - 1)
                            s_act.inc(nc.scalar.activation(out=sg[p][:, :], in_=g_ps[p][:, :],
                                                           func=AF.Silu))
                            nc.vector.wait_ge(s_pe.h, s_pe.n)
                            nc.vector.wait_ge(s_act.h, silu_base + c + 1)
                            s_v.inc(nc.vector.tensor_tensor(out=h_sb[:, c * tb:(c + 1) * tb],
                                                            in0=sg[p][:, :], in1=u_ps[p][:, :],
                                                            op=OP.mult))
                    mulA_end = s_v.n
                    if DEBUG_FFN and tbi == 0:
                        nc.sync.wait_ge(s_v.h, mulA_end)
                        for c in range(C):
                            s_out.inc(nc.sync.dma_start(out=h_dbg[c], in_=h_sb[:, c * tb:(c + 1) * tb]), 16)
                        nc.sync.wait_ge(s_out.h, s_out.n)
                    with nc.psum_tensor(f"d0_{tbi}", [128, tb], F32) as d0, \
                         nc.psum_tensor(f"d1_{tbi}", [128, tb], F32) as d1:
                        d_ps = [d0, d1]
                        wd_sb = [xhat[:, 0:C * 128], wd1[:, :]]
                        evac_base = s_v.n
                        peB_base = s_pe.n
                        for o in range(CD):
                            p = o % 2
                            if o >= 2:
                                nc.sync.wait_ge(s_pe.h, peB_base + o - 1)
                            else:
                                nc.sync.wait_ge(s_pe.h, peB_base)
                            s_wD[p].inc(nc.sync.dma_start(out=wd_sb[p], in_=wd[o]), 16)
                            nc.tensor.wait_ge(s_wD[p].h, s_wD[p].n)
                            nc.tensor.wait_ge(s_v.h, mulA_end)
                            if o >= 2:
                                nc.tensor.wait_ge(s_v.h, evac_base + o - 1)
                            last = None
                            for c in range(C):
                                last = nc.tensor.matmul(out=d_ps[p][:, :],
                                                        lhsT=wd_sb[p][:, c * 128:(c + 1) * 128] if p else wd_sb[0][:, c * 128:(c + 1) * 128],
                                                        rhs=h_sb[:, c * tb:(c + 1) * tb],
                                                        start=(c == 0), stop=(c == C - 1))
                            s_pe.inc(last)
                            nc.vector.wait_ge(s_pe.h, s_pe.n)
                            s_v.inc(nc.vector.scalar_tensor_tensor(
                                out=x_sb[:, o * T + t0:o * T + t0 + tb], in0=d_ps[p][:, :],
                                scalar=0.5, in1=x_sb[:, o * T + t0:o * T + t0 + tb],
                                op0=OP.mult, op1=OP.add))
                nc.sync.wait_ge(s_v.h, s_v.n)
                for dk in range(CD):
                    s_out.inc(nc.sync.dma_start(out=x_out[dk], in_=x_sb[:, dk * T:(dk + 1) * T]), 16)
                nc.sync.wait_ge(s_out.h, s_out.n)
    return nc




def _soft_barrier(nc, ctrs):
    for eng in (nc.tensor, nc.scalar, nc.vector, nc.gpsimd, nc.sync):
        for c in ctrs:
            if c.n > 0:
                eng.wait_ge(c.h, c.n)



def build_qkv0():
    nc = bass.Bass("TRN2", target_bir_lowering=False)
    x_in = nc.dram_tensor("x", [CD, 128, T], F32, kind="ExternalInput").ap()
    wqk = nc.dram_tensor("wqk", [2 * H, 128, CD * 128], BF16, kind="ExternalInput").ap()
    wv = nc.dram_tensor("wv", [CD, 128, D], BF16, kind="ExternalInput").ap()
    bqk = nc.dram_tensor("bqk", [128, 2 * H], F32, kind="ExternalInput").ap()
    bv = nc.dram_tensor("bv", [1, D], F32, kind="ExternalInput").ap()
    q_out = nc.dram_tensor("q", [H, 128, T], BF16, kind="ExternalOutput").ap()
    k_out = nc.dram_tensor("k", [H, 128, T], BF16, kind="ExternalOutput").ap()
    v_out = nc.dram_tensor("v", [T // 128, 128, D], BF16, kind="ExternalOutput").ap()
    with ExitStack() as st:
        x_sb = st.enter_context(nc.sbuf_tensor("x_sb", [128, CD * T], F32))
        xhat = st.enter_context(nc.sbuf_tensor("xhat", [128, CD * T], BF16))
        scr = [st.enter_context(nc.sbuf_tensor(f"scr{p}", [128, T], BF16)) for p in range(2)]
        scrq = [st.enter_context(nc.sbuf_tensor(f"scrq{p}", [128, T], BF16)) for p in range(2)]
        scr_f = st.enter_context(nc.sbuf_tensor("scr_f", [128, T], F32))
        rows = [st.enter_context(nc.sbuf_tensor(f"rows{i}", [1, T], F32)) for i in range(4)]
        ones_bf = st.enter_context(nc.sbuf_tensor("ones_bf", [128, 1], BF16))
        ones1 = st.enter_context(nc.sbuf_tensor("ones1", [1, 128], F32))
        ones1b = st.enter_context(nc.sbuf_tensor("ones1b", [1, 128], BF16))
        eps_sb = st.enter_context(nc.sbuf_tensor("eps_sb", [1, 1], F32))
        bqk_sb = st.enter_context(nc.sbuf_tensor("bqk_sb", [128, 2 * H], F32))
        bv_sb = st.enter_context(nc.sbuf_tensor("bv_sb", [1, D], BF16))
        wqk_sb = [st.enter_context(nc.sbuf_tensor(f"wqk{p}", [128, CD * 128], BF16)) for p in range(2)]
        wv_sb = [st.enter_context(nc.sbuf_tensor(f"wv{dk}", [128, 512], BF16)) for dk in range(CD)]
        obuf = [st.enter_context(nc.sbuf_tensor(f"obuf{p}", [128, T], BF16)) for p in range(2)]
        vbuf = [st.enter_context(nc.sbuf_tensor(f"vbuf{p}", [128, 512], BF16)) for p in range(2)]
        s_x = Ctr(st, nc, "s_x")
        s_wA = [Ctr(st, nc, f"s_wA{p}") for p in range(2)]
        s_wV = Ctr(st, nc, "s_wV")
        s_pe = Ctr(st, nc, "s_pe")
        s_act = Ctr(st, nc, "s_act")
        s_v = Ctr(st, nc, "s_v")
        s_out = Ctr(st, nc, "s_out")
        with nc.Block() as block:
            @block.sync
            def _(sync):
                for dk in range(CD):
                    s_x.inc(nc.sync.dma_start(out=x_sb[:, dk * T:(dk + 1) * T], in_=x_in[dk]), 16)
                s_x.inc(nc.sync.dma_start(out=bqk_sb[:, :], in_=bqk), 16)
                s_x.inc(nc.gpsimd.dma_start(out=bv_sb[:, :], in_=bv), 16)
                s_v.inc(nc.vector.memset(ones_bf[:, :], 1.0))
                s_v.inc(nc.vector.memset(ones1[:, :], 1.0))
                s_v.inc(nc.vector.memset(ones1b[:, :], 1.0))
                s_v.inc(nc.vector.memset(eps_sb[:, :], LN_EPS))
                nc.scalar.wait_ge(s_x.h, s_x.n)
                nc.scalar.wait_ge(s_v.h, 4)
                nc.tensor.wait_ge(s_v.h, 4)
                nc.vector.wait_ge(s_x.h, s_x.n)
                with nc.psum_tensor("stat0", [1, T], F32) as st0, \
                     nc.psum_tensor("stat1", [1, T], F32) as st1:
                    emit_ln_stats(nc, s_pe, s_act, s_v, x_sb, scr, scrq, [st0, st1], rows,
                                  ones_bf, eps_sb[0:1, 0:1], 1.0 / D)
                with nc.psum_tensor("bc0", [128, T], F32) as bc0, \
                     nc.psum_tensor("bc1", [128, T], F32) as bc1:
                    emit_bcast_rows(nc, s_pe, [rows[3][0:1, :T], rows[2][0:1, :T]],
                                    [bc0, bc1], ones1, T, s_v.h, s_v.n)
                    nc.vector.wait_ge(s_pe.h, s_pe.n)
                    for dk in range(CD):
                        sl = slice(dk * T, (dk + 1) * T)
                        nc.vector.tensor_tensor(out=scr_f[:, :], in0=x_sb[:, sl], in1=bc0[:, :], op=OP.mult)
                        last = nc.vector.tensor_tensor(out=xhat[:, sl], in0=scr_f[:, :], in1=bc1[:, :], op=OP.add)
                    s_v.inc(last)
                apply_done = s_v.n
                with nc.psum_tensor("qk0", [128, T], F32) as qk0, \
                     nc.psum_tensor("qk1", [128, T], F32) as qk1:
                    qk_ps = [qk0, qk1]
                    peQ = s_pe.n
                    actQ = s_act.n
                    for oc in range(2 * H):
                        p = oc % 2
                        if oc >= 2:
                            nc.sync.wait_ge(s_pe.h, peQ + oc - 1)
                        s_wA[p].inc(nc.sync.dma_start(out=wqk_sb[p][:, :], in_=wqk[oc]), 16)
                        nc.tensor.wait_ge(s_wA[p].h, s_wA[p].n)
                        nc.tensor.wait_ge(s_v.h, apply_done)
                        if oc >= 2:
                            nc.tensor.wait_ge(s_act.h, actQ + oc - 1)
                        last = None
                        for hh in range(2):
                            ts = slice(hh * 512, (hh + 1) * 512)
                            for dk in range(CD):
                                last = nc.tensor.matmul(out=qk_ps[p][:, ts],
                                                        lhsT=wqk_sb[p][:, dk * 128:(dk + 1) * 128],
                                                        rhs=xhat[:, dk * T:(dk + 1) * T][:, ts],
                                                        start=(dk == 0), stop=(dk == CD - 1))
                        s_pe.inc(last)
                        nc.scalar.wait_ge(s_pe.h, s_pe.n)
                        if oc >= 2:
                            nc.scalar.wait_ge(s_out.h, 16 * (oc - 1))
                        s_act.inc(nc.scalar.activation(out=obuf[p][:, :], in_=qk_ps[p][:, :],
                                                       func=AF.Identity, bias=bqk_sb[:, oc:oc + 1]))
                        dst = q_out[oc] if oc < H else k_out[oc - H]
                        s_out.inc(nc.scalar.dma_start(out=dst, in_=obuf[p][:, :]), 16)
                _soft_barrier(nc, [s_pe, s_act, s_v, s_out])
                with nc.psum_tensor("v0", [128, 512], F32) as v0, \
                     nc.psum_tensor("v1", [128, 512], F32) as v1:
                    v_ps = [v0, v1]
                    outV = s_out.n
                    for vg in range(4):
                        vs = slice(vg * 512, (vg + 1) * 512)
                        if vg >= 1:
                            nc.sync.wait_ge(s_pe.h, s_pe.n)
                        for dk in range(CD):
                            s_wV.inc(nc.sync.dma_start(out=wv_sb[dk][:, :], in_=wv[dk][:, vs]), 16)
                        nc.tensor.wait_ge(s_wV.h, s_wV.n)
                        for tt in range(T // 128):
                            u = vg * 8 + tt
                            p = u % 2
                            nc.tensor.wait_ge(s_v.h, apply_done)
                            if u >= 2:
                                nc.tensor.wait_ge(s_act.h, s_act.n - 1)
                            for dk in range(CD):
                                nc.tensor.matmul(out=v_ps[p][:, :], lhsT=xhat[:, dk * T + tt * 128:dk * T + (tt + 1) * 128],
                                                 rhs=wv_sb[dk][:, :], start=(dk == 0), stop=False)
                            last = nc.tensor.matmul(out=v_ps[p][:, :], lhsT=ones1b[:, :],
                                                    rhs=bv_sb[:, vs], start=False, stop=True)
                            s_pe.inc(last)
                            nc.scalar.wait_ge(s_pe.h, s_pe.n)
                            if u >= 2:
                                nc.scalar.wait_ge(s_out.h, outV + 16 * (u - 1))
                            s_act.inc(nc.scalar.activation(out=vbuf[p][:, :], in_=v_ps[p][:, :],
                                                           func=AF.Identity))
                            s_out.inc(nc.scalar.dma_start(out=v_out[tt][:, vs], in_=vbuf[p][:, :]), 16)
                nc.sync.wait_ge(s_out.h, s_out.n)
    return nc


def build_qkv1():
    nc = bass.Bass("TRN2", target_bir_lowering=False)
    x_in = nc.dram_tensor("x", [CD, 128, T], F32, kind="ExternalInput").ap()
    wq = nc.dram_tensor("wq", [H, 128, CD * 128], BF16, kind="ExternalInput").ap()
    wk = nc.dram_tensor("wk", [H, 128, CD * 128], BF16, kind="ExternalInput").ap()
    wv = nc.dram_tensor("wv", [CD, 128, D], BF16, kind="ExternalInput").ap()
    bq = nc.dram_tensor("bq", [128, H], F32, kind="ExternalInput").ap()
    bk = nc.dram_tensor("bk", [128, H], F32, kind="ExternalInput").ap()
    bv = nc.dram_tensor("bv", [1, D], F32, kind="ExternalInput").ap()
    gq = nc.dram_tensor("gq", [128, CD], F32, kind="ExternalInput").ap()
    gkv = nc.dram_tensor("gkv", [128, CD], F32, kind="ExternalInput").ap()
    ghead = nc.dram_tensor("ghead", [128, 1], F32, kind="ExternalInput").ap()
    q_out = nc.dram_tensor("q", [H, 128, T], F16, kind="ExternalOutput").ap()
    k_out = nc.dram_tensor("k", [H, 128, T], F16, kind="ExternalOutput").ap()
    v_out = nc.dram_tensor("v", [T // 128, 128, D], BF16, kind="ExternalOutput").ap()
    with ExitStack() as st:
        x_sb = st.enter_context(nc.sbuf_tensor("x_sb", [128, CD * T], F32))
        xq = st.enter_context(nc.sbuf_tensor("xq", [128, CD * T], BF16))
        xkv = st.enter_context(nc.sbuf_tensor("xkv", [128, CD * T], BF16))
        scr = [st.enter_context(nc.sbuf_tensor(f"scr{p}", [128, T], BF16)) for p in range(2)]
        scrq = [st.enter_context(nc.sbuf_tensor(f"scrq{p}", [128, T], BF16)) for p in range(2)]
        scr_f = st.enter_context(nc.sbuf_tensor("scr_f", [128, T], F32))
        qf32 = [st.enter_context(nc.sbuf_tensor(f"qf32{p}", [128, T], F32)) for p in range(2)]
        rows = [st.enter_context(nc.sbuf_tensor(f"rows{i}", [1, T], F32)) for i in range(4)]
        ones_bf = st.enter_context(nc.sbuf_tensor("ones_bf", [128, 1], BF16))
        ones1 = st.enter_context(nc.sbuf_tensor("ones1", [1, 128], F32))
        ones1b = st.enter_context(nc.sbuf_tensor("ones1b", [1, 128], BF16))
        eps_sb = st.enter_context(nc.sbuf_tensor("eps_sb", [1, 1], F32))
        bq_sb = st.enter_context(nc.sbuf_tensor("bq_sb", [128, H], F32))
        bk_sb = st.enter_context(nc.sbuf_tensor("bk_sb", [128, H], F32))
        bv_sb = st.enter_context(nc.sbuf_tensor("bv_sb", [1, D], BF16))
        gq_sb = st.enter_context(nc.sbuf_tensor("gq_sb", [128, CD], F32))
        gkv_sb = st.enter_context(nc.sbuf_tensor("gkv_sb", [128, CD], F32))
        gh_sb = st.enter_context(nc.sbuf_tensor("gh_sb", [128, 1], F32))
        w_sb = [st.enter_context(nc.sbuf_tensor(f"w{p}", [128, CD * 128], BF16)) for p in range(2)]
        wv_sb = [st.enter_context(nc.sbuf_tensor(f"wv{dk}", [128, 512], BF16)) for dk in range(CD)]
        obuf = [st.enter_context(nc.sbuf_tensor(f"obuf{p}", [128, T], F16)) for p in range(2)]
        vbuf = [st.enter_context(nc.sbuf_tensor(f"vbuf{p}", [128, 512], BF16)) for p in range(2)]
        s_x = Ctr(st, nc, "s_x")
        s_wA = [Ctr(st, nc, f"s_wA{p}") for p in range(2)]
        s_wV = Ctr(st, nc, "s_wV")
        s_pe = Ctr(st, nc, "s_pe")
        s_act = Ctr(st, nc, "s_act")
        s_v = Ctr(st, nc, "s_v")
        s_out = Ctr(st, nc, "s_out")
        with nc.Block() as block:
            @block.sync
            def _(sync):
                for dk in range(CD):
                    s_x.inc(nc.sync.dma_start(out=x_sb[:, dk * T:(dk + 1) * T], in_=x_in[dk]), 16)
                for name, dst in ((bq, bq_sb), (bk, bk_sb), (gq, gq_sb), (gkv, gkv_sb), (ghead, gh_sb)):
                    s_x.inc(nc.sync.dma_start(out=dst[:, :], in_=name), 16)
                s_x.inc(nc.gpsimd.dma_start(out=bv_sb[:, :], in_=bv), 16)
                s_v.inc(nc.vector.memset(ones_bf[:, :], 1.0))
                s_v.inc(nc.vector.memset(ones1[:, :], 1.0))
                s_v.inc(nc.vector.memset(ones1b[:, :], 1.0))
                s_v.inc(nc.vector.memset(eps_sb[:, :], RMS_EPS))
                nc.scalar.wait_ge(s_x.h, s_x.n)
                nc.scalar.wait_ge(s_v.h, 4)
                nc.tensor.wait_ge(s_v.h, 4)
                nc.vector.wait_ge(s_x.h, s_x.n)
                with nc.psum_tensor("stat0", [1, T], F32) as st0, \
                     nc.psum_tensor("stat1", [1, T], F32) as st1:
                    emit_ln_stats(nc, s_pe, s_act, s_v, x_sb, scr, scrq, [st0, st1], rows,
                                  ones_bf, eps_sb[0:1, 0:1], 1.0 / D, rms=True)
                with nc.psum_tensor("bc0", [128, T], F32) as bc0:
                    emit_bcast_rows(nc, s_pe, [rows[3][0:1, :T]], [bc0], ones1, T, s_act.h, s_act.n)
                    nc.vector.wait_ge(s_pe.h, s_pe.n)
                    for dk in range(CD):
                        sl = slice(dk * T, (dk + 1) * T)
                        nc.vector.tensor_tensor(out=scr_f[:, :], in0=x_sb[:, sl], in1=bc0[:, :], op=OP.mult)
                        nc.vector.tensor_scalar(out=xq[:, sl], in0=scr_f[:, :], scalar1=gq_sb[:, dk:dk + 1],
                                                scalar2=None, op0=OP.mult)
                        last = nc.vector.tensor_scalar(out=xkv[:, sl], in0=scr_f[:, :],
                                                       scalar1=gkv_sb[:, dk:dk + 1], scalar2=None, op0=OP.mult)
                    s_v.inc(last)
                apply_done = s_v.n
                for which, (wsrc, bias_sb, xin, dst_t) in enumerate(
                        ((wq, bq_sb, xq, q_out), (wk, bk_sb, xkv, k_out))):
                    _soft_barrier(nc, [s_pe, s_act, s_v, s_out])
                    with nc.psum_tensor(f"qk0_{which}", [128, T], F32) as qk0, \
                         nc.psum_tensor(f"qk1_{which}", [128, T], F32) as qk1, \
                         nc.psum_tensor(f"hs_{which}", [1, T], F32) as hs_ps, \
                         nc.psum_tensor(f"hb_{which}", [128, T], F32) as hb_ps:
                        qk_ps = [qk0, qk1]
                        evQ = s_v.n
                        outQ = s_out.n
                        proj_marks, evac_marks, sq_marks, exp_marks, stt_marks, bc_marks = \
                            {}, {}, {}, {}, {}, {}

                        def _proj(oc):
                            p = oc % 2
                            if oc >= 2:
                                nc.sync.wait_ge(s_pe.h, proj_marks[oc - 2])
                            s_wA[p].inc(nc.sync.dma_start(out=w_sb[p][:, :], in_=wsrc[oc]), 16)
                            nc.tensor.wait_ge(s_wA[p].h, s_wA[p].n)
                            nc.tensor.wait_ge(s_v.h, apply_done)
                            if oc >= 2:
                                nc.tensor.wait_ge(s_act.h, evac_marks[oc - 2])
                            last = None
                            for hh in range(2):
                                ts = slice(hh * 512, (hh + 1) * 512)
                                for dk in range(CD):
                                    last = nc.tensor.matmul(out=qk_ps[p][:, ts],
                                                            lhsT=w_sb[p][:, dk * 128:(dk + 1) * 128],
                                                            rhs=xin[:, dk * T:(dk + 1) * T][:, ts],
                                                            start=(dk == 0), stop=(dk == CD - 1))
                            s_pe.inc(last)
                            proj_marks[oc] = s_pe.n

                        def _chain(oc):
                            p = oc % 2
                            nc.scalar.wait_ge(s_pe.h, proj_marks[oc])
                            if oc >= 2:
                                nc.scalar.wait_ge(s_v.h, stt_marks[oc - 2])
                            s_act.inc(nc.scalar.activation(out=qf32[p][:, :], in_=qk_ps[p][:, :],
                                                           func=AF.Identity, bias=bias_sb[:, oc:oc + 1]))
                            evac_marks[oc] = s_act.n
                            s_act.inc(nc.scalar.activation(out=scr[p][:, :], in_=qf32[p][:, :],
                                                           func=AF.Square))
                            sq_marks[oc] = s_act.n
                            nc.tensor.wait_ge(s_act.h, sq_marks[oc])
                            for hh in range(2):
                                ts = slice(hh * 512, (hh + 1) * 512)
                                s_pe.inc(nc.tensor.matmul(out=hs_ps[:, ts], lhsT=ones_bf[:, :],
                                                          rhs=scr[p][:, ts], start=True, stop=True))
                            hs_mark = s_pe.n
                            nc.scalar.wait_ge(s_pe.h, hs_mark)
                            s_act.inc(nc.scalar.activation(out=rows[0][0:1, :T], in_=hs_ps[0:1, :T],
                                                           func=AF.Ln, bias=eps_sb[0:1, 0:1],
                                                           scale=1.0 / HD))
                            s_act.inc(nc.scalar.activation(out=rows[0][0:1, :T], in_=rows[0][0:1, :T],
                                                           func=AF.Exp, scale=-0.5))
                            exp_marks[oc] = s_act.n
                            nc.tensor.wait_ge(s_act.h, exp_marks[oc])
                            if oc >= 1:
                                nc.tensor.wait_ge(s_v.h, stt_marks[oc - 1])
                            for hh in range(2):
                                ts = slice(hh * 512, (hh + 1) * 512)
                                s_pe.inc(nc.tensor.matmul(out=hb_ps[:, ts], lhsT=ones1[:, :],
                                                          rhs=rows[0][0:1, :T][:, ts],
                                                          start=True, stop=True))
                            bc_marks[oc] = s_pe.n
                            nc.vector.wait_ge(s_pe.h, bc_marks[oc])
                            if oc >= 2:
                                nc.vector.wait_ge(s_out.h, outQ + 16 * (oc - 1))
                            s_v.inc(nc.vector.scalar_tensor_tensor(out=obuf[p][:, :], in0=qf32[p][:, :],
                                                                   scalar=gh_sb[:, 0:1], in1=hb_ps[:, :],
                                                                   op0=OP.mult, op1=OP.mult))
                            stt_marks[oc] = s_v.n
                            nc.scalar.wait_ge(s_v.h, stt_marks[oc])
                            s_out.inc(nc.scalar.dma_start(out=dst_t[oc], in_=obuf[p][:, :]), 16)

                        for j in range(H + 1):
                            if j < H:
                                _proj(j)
                            if j >= 1:
                                _chain(j - 1)
                _soft_barrier(nc, [s_pe, s_act, s_v, s_out])
                with nc.psum_tensor("v0", [128, 512], F32) as v0, \
                     nc.psum_tensor("v1", [128, 512], F32) as v1:
                    v_ps = [v0, v1]
                    outV = s_out.n
                    for vg in range(4):
                        vs = slice(vg * 512, (vg + 1) * 512)
                        if vg >= 1:
                            nc.sync.wait_ge(s_pe.h, s_pe.n)
                        for dk in range(CD):
                            s_wV.inc(nc.sync.dma_start(out=wv_sb[dk][:, :], in_=wv[dk][:, vs]), 16)
                        nc.tensor.wait_ge(s_wV.h, s_wV.n)
                        for tt in range(T // 128):
                            u = vg * 8 + tt
                            p = u % 2
                            nc.tensor.wait_ge(s_v.h, apply_done)
                            if u >= 2:
                                nc.tensor.wait_ge(s_act.h, s_act.n - 1)
                            for dk in range(CD):
                                nc.tensor.matmul(out=v_ps[p][:, :], lhsT=xkv[:, dk * T + tt * 128:dk * T + (tt + 1) * 128],
                                                 rhs=wv_sb[dk][:, :], start=(dk == 0), stop=False)
                            last = nc.tensor.matmul(out=v_ps[p][:, :], lhsT=ones1b[:, :],
                                                    rhs=bv_sb[:, vs], start=False, stop=True)
                            s_pe.inc(last)
                            nc.scalar.wait_ge(s_pe.h, s_pe.n)
                            if u >= 2:
                                nc.scalar.wait_ge(s_out.h, outV + 16 * (u - 1))
                            s_act.inc(nc.scalar.activation(out=vbuf[p][:, :], in_=v_ps[p][:, :],
                                                           func=AF.Identity))
                            s_out.inc(nc.scalar.dma_start(out=v_out[tt][:, vs], in_=vbuf[p][:, :]), 16)
                nc.sync.wait_ge(s_out.h, s_out.n)
    return nc



def build_attn(causal, qk_dt=BF16):
    nc = bass.Bass("TRN2", target_bir_lowering=False)
    TK = 2048
    x_in = nc.dram_tensor("x", [CD, 128, T], F32, kind="ExternalInput").ap()
    q_in = nc.dram_tensor("q", [H, 128, T], qk_dt, kind="ExternalInput").ap()
    k_in = nc.dram_tensor("k", [H, 128, TK], qk_dt, kind="ExternalInput").ap()
    v_in = nc.dram_tensor("v", [H, 128, TK], BF16, kind="ExternalInput").ap()
    wo = nc.dram_tensor("wo", [CD, 128, CD * 128], BF16, kind="ExternalInput").ap()
    bo = nc.dram_tensor("bo", [128, CD], F32, kind="ExternalInput").ap()
    qpos = nc.dram_tensor("qpos", [1, T], F32, kind="ExternalInput").ap()
    x_out = nc.dram_tensor("x_out", [CD, 128, T], F32, kind="ExternalOutput").ap()
    NKC = TK // 128
    with ExitStack() as st:
        x_sb = st.enter_context(nc.sbuf_tensor("x_sb", [128, CD * T], F32))
        q_sb = st.enter_context(nc.sbuf_tensor("q_sb", [128, H * T], qk_dt))
        pbuf = st.enter_context(nc.sbuf_tensor("pbuf", [128, NKC * T], BF16))
        attn_sb = st.enter_context(nc.sbuf_tensor("attn_sb", [128, H * T], BF16))
        kbuf = [st.enter_context(nc.sbuf_tensor(f"kbuf{p}", [128, TK], qk_dt)) for p in range(2)]
        vbuf = [st.enter_context(nc.sbuf_tensor(f"vbuf{p}", [128, TK], BF16)) for p in range(2)]
        av_sb = [st.enter_context(nc.sbuf_tensor(f"av_sb{p}", [128, T], BF16)) for p in range(2)]
        mq2row = st.enter_context(nc.sbuf_tensor("mq2row", [1, H], F32))
        mk2row = st.enter_context(nc.sbuf_tensor("mk2row", [1, H], F32))
        negb_cols = st.enter_context(nc.sbuf_tensor("negb_cols", [128, H], F32))
        qgb = st.enter_context(nc.sbuf_tensor("qgb", [128, T], F16))
        kcols = st.enter_context(nc.sbuf_tensor("kcols", [128, NKC], F32))
        kcols_i = st.enter_context(nc.sbuf_tensor("kcols_i", [128, NKC], I32))
        rows = [st.enter_context(nc.sbuf_tensor(f"rows{i}", [1, T], F32)) for i in range(2)]
        r2 = st.enter_context(nc.sbuf_tensor("r2", [1, 512], F32))
        sc1 = [st.enter_context(nc.sbuf_tensor(f"sc1_{i}", [1, 1], F32)) for i in range(4)]
        ones_bf = st.enter_context(nc.sbuf_tensor("ones_bf", [128, 1], BF16))
        ones1 = st.enter_context(nc.sbuf_tensor("ones1", [1, 128], F32))
        bo_sb = st.enter_context(nc.sbuf_tensor("bo_sb", [128, CD], F32))
        wo_sb = [st.enter_context(nc.sbuf_tensor(f"wo{p}", [128, CD * 128], BF16)) for p in range(2)]
        s_x = Ctr(st, nc, "s_x")
        s_k = [Ctr(st, nc, f"s_k{p}") for p in range(2)]
        s_vv = [Ctr(st, nc, f"s_vv{p}") for p in range(2)]
        s_wO = [Ctr(st, nc, f"s_wO{p}") for p in range(2)]
        s_pe = Ctr(st, nc, "s_pe")
        s_act = Ctr(st, nc, "s_act")
        s_v = Ctr(st, nc, "s_v")
        s_out = Ctr(st, nc, "s_out")
        s_hd = Ctr(st, nc, "s_hd")
        with nc.Block() as block:
            @block.sync
            def _(sync):
                for dk in range(CD):
                    s_x.inc(nc.sync.dma_start(out=x_sb[:, dk * T:(dk + 1) * T], in_=x_in[dk]), 16)
                for hh in range(H):
                    s_x.inc(nc.sync.dma_start(out=q_sb[:, hh * T:(hh + 1) * T], in_=q_in[hh]), 16)
                s_x.inc(nc.sync.dma_start(out=bo_sb[:, :], in_=bo), 16)
                s_x.inc(nc.sync.dma_start(out=rows[0][0:1, :], in_=qpos), 16)
                s_v.inc(nc.vector.memset(ones_bf[:, :], 1.0))
                s_v.inc(nc.vector.memset(ones1[:, :], 1.0))
                nc.gpsimd.iota(out=kcols_i[:, :], pattern=[[128, NKC]], base=0,
                               channel_multiplier=1).then_inc(s_v.h, 1)
                s_v.n += 1
                nc.vector.wait_ge(s_v.h, 3)
                nc.vector.wait_ge(s_x.h, s_x.n)
                s_v.inc(nc.vector.tensor_copy(out=kcols[:, :], in_=kcols_i[:, :]))
                with nc.psum_tensor("qg_ps", [128, T], F32) as qg_ps:
                    nc.tensor.wait_ge(s_v.h, 2)
                    nc.tensor.wait_ge(s_x.h, s_x.n)
                    for hh in range(2):
                        ts = slice(hh * 512, (hh + 1) * 512)
                        s_pe.inc(nc.tensor.matmul(out=qg_ps[:, ts], lhsT=ones1[:, :],
                                                  rhs=rows[0][0:1, :][:, ts], start=True, stop=True))
                    nc.scalar.wait_ge(s_pe.h, s_pe.n)
                    s_act.inc(nc.scalar.activation(out=qgb[:, :], in_=qg_ps[:, :], func=AF.Identity))
                with nc.psum_tensor("sc0", [128, 512], F32) as sc0, \
                     nc.psum_tensor("sc1p", [128, 512], F32) as sc1p, \
                     nc.psum_tensor("av", [128, T], F32) as av_ps, \
                     nc.psum_tensor("sums", [1, T], F32) as sum_ps, \
                     nc.psum_tensor("rb", [128, T], F32) as rb_ps:
                    sc_ps = [sc0, sc1p]
                    nc.scalar.wait_ge(s_x.h, s_x.n)
                    s_act.inc(nc.scalar.activation(out=pbuf[:, :H * T], in_=q_sb[:, :], func=AF.Square))
                    red_marks = []
                    for h in range(H):
                        reg = sum_ps[0:1, :T] if h % 2 == 0 else rb_ps[0:1, :T]
                        nc.tensor.wait_ge(s_act.h, s_act.n)
                        if h >= 2:
                            nc.tensor.wait_ge(s_v.h, red_marks[h - 2])
                        for hh in range(2):
                            cs = slice(hh * 512, (hh + 1) * 512)
                            s_pe.inc(nc.tensor.matmul(out=reg[:, cs], lhsT=ones_bf[:, :],
                                                      rhs=pbuf[:, h * T:(h + 1) * T][:, cs],
                                                      start=True, stop=True))
                        nc.vector.wait_ge(s_pe.h, s_pe.n)
                        s_v.inc(nc.vector.reduce_max(out=mq2row[0:1, h:h + 1], in_=reg, axis=AX.X))
                        red_marks.append(s_v.n)
                    nc.scalar.wait_ge(s_pe.h, s_pe.n)
                    for h in range(H):
                        p = h % 2
                        if h >= 2:
                            nc.sync.wait_ge(s_v.h, s_v.n - 4)
                        s_k[p].inc(nc.sync.dma_start(out=kbuf[p][:, :], in_=k_in[h]), 16)
                        nc.scalar.wait_ge(s_k[p].h, s_k[p].n)
                        s_act.inc(nc.scalar.activation(out=pbuf[:, p * TK:(p + 1) * TK],
                                                       in_=kbuf[p][:, :], func=AF.Square))
                        nc.tensor.wait_ge(s_act.h, s_act.n)
                        if h >= 1:
                            nc.tensor.wait_ge(s_v.h, s_v.n)
                        for ks in range(4):
                            dst = sum_ps if ks < 2 else rb_ps
                            cs = slice((ks % 2) * 512, (ks % 2) * 512 + 512)
                            s_pe.inc(nc.tensor.matmul(out=dst[0:1, cs], lhsT=ones_bf[:, :],
                                                      rhs=pbuf[:, p * TK + ks * 512:p * TK + (ks + 1) * 512],
                                                      start=True, stop=True))
                        nc.vector.wait_ge(s_pe.h, s_pe.n)
                        s_v.inc(nc.vector.reduce_max(out=sc1[0][0:1, :], in_=sum_ps[0:1, :], axis=AX.X))
                        s_v.inc(nc.vector.reduce_max(out=sc1[3][0:1, :], in_=rb_ps[0:1, :T], axis=AX.X))
                        s_v.inc(nc.vector.tensor_tensor(out=mk2row[0:1, h:h + 1], in0=sc1[0][0:1, :],
                                                        in1=sc1[3][0:1, :], op=OP.max))
                    nc.scalar.wait_ge(s_v.h, s_v.n)
                    s_act.inc(nc.scalar.activation(out=mq2row[0:1, :], in_=mq2row[0:1, :], func=AF.Sqrt))
                    s_act.inc(nc.scalar.activation(out=mk2row[0:1, :], in_=mk2row[0:1, :], func=AF.Sqrt))
                    nc.vector.wait_ge(s_act.h, s_act.n)
                    s_v.inc(nc.vector.tensor_tensor(out=mq2row[0:1, :], in0=mq2row[0:1, :],
                                                    in1=mk2row[0:1, :], op=OP.mult))
                    s_v.inc(nc.vector.tensor_scalar(out=mq2row[0:1, :], in0=mq2row[0:1, :],
                                                    scalar1=-INVSQRT_HD, scalar2=None, op0=OP.mult))
                    nc.tensor.wait_ge(s_v.h, s_v.n)
                    s_pe.inc(nc.tensor.matmul(out=rb_ps[:, 0:H], lhsT=ones1[:, :],
                                              rhs=mq2row[0:1, :], start=True, stop=True))
                    nc.scalar.wait_ge(s_pe.h, s_pe.n)
                    s_act.inc(nc.scalar.activation(out=negb_cols[:, :], in_=rb_ps[:, 0:H], func=AF.Identity))
                    nc.tensor.wait_ge(s_act.h, s_act.n)
                    nc.tensor.wait_ge(s_v.h, s_v.n)
                    for h in range(H):
                        p = h % 2
                        if h >= 2:
                            nc.sync.wait_ge(s_hd.h, h - 1)
                        s_k[p].inc(nc.sync.dma_start(out=kbuf[p][:, :], in_=k_in[h]), 16)
                        s_vv[p].inc(nc.sync.dma_start(out=vbuf[p][:, :], in_=v_in[h]), 16)
                        negb = negb_cols[:, h:h + 1]
                        bound_act = s_act.n
                        nc.tensor.wait_ge(s_vv[p].h, s_vv[p].n)
                        nc.tensor.wait_ge(s_k[p].h, s_k[p].n)
                        units = [(kc, qh) for kc in range(NKC) for qh in range(2)]

                        def _sums_av(u):
                            kc, qh = units[u]
                            ts = slice(qh * 512, (qh + 1) * 512)
                            psl = slice(kc * T + qh * 512, kc * T + (qh + 1) * 512)
                            s_pe.inc(nc.tensor.matmul(out=sum_ps[0:1, ts], lhsT=ones_bf[:, :],
                                                      rhs=pbuf[:, psl], start=(kc == 0),
                                                      stop=(kc == NKC - 1)))
                            s_pe.inc(nc.tensor.matmul(out=av_ps[:, ts],
                                                      lhsT=vbuf[p][:, kc * 128:(kc + 1) * 128],
                                                      rhs=pbuf[:, psl], start=(kc == 0),
                                                      stop=(kc == NKC - 1)))

                        mask_marks = []
                        for u, (kc, qh) in enumerate(units):
                            up = u % 2
                            ts = slice(qh * 512, (qh + 1) * 512)
                            psl = slice(kc * T + qh * 512, kc * T + (qh + 1) * 512)
                            if u >= 2:
                                nc.tensor.wait_ge(s_act.h, bound_act + (u - 1))
                            s_pe.inc(nc.tensor.matmul(out=sc_ps[up][:, :],
                                                      lhsT=kbuf[p][:, kc * 128:(kc + 1) * 128],
                                                      rhs=q_sb[:, h * T:(h + 1) * T][:, ts],
                                                      start=True, stop=True))
                            nc.scalar.wait_ge(s_pe.h, s_pe.n)
                            s_act.inc(nc.scalar.activation(out=pbuf[:, psl], in_=sc_ps[up][:, :],
                                                           func=AF.Exp, bias=negb,
                                                           scale=INVSQRT_HD))
                            if causal:
                                nc.vector.wait_ge(s_act.h, s_act.n)
                                s_v.inc(nc.vector.scalar_tensor_tensor(
                                    out=pbuf[:, psl], in0=qgb[:, ts], scalar=kcols[:, kc:kc + 1],
                                    in1=pbuf[:, psl], op0=OP.is_ge, op1=OP.mult))
                            mask_marks.append(s_v.n if causal else s_act.n)
                            if u >= 1:
                                if causal:
                                    nc.tensor.wait_ge(s_v.h, mask_marks[u - 1])
                                else:
                                    nc.tensor.wait_ge(s_act.h, mask_marks[u - 1])
                                _sums_av(u - 1)
                        if causal:
                            nc.tensor.wait_ge(s_v.h, mask_marks[-1])
                        else:
                            nc.tensor.wait_ge(s_act.h, mask_marks[-1])
                        _sums_av(len(units) - 1)
                        nc.vector.wait_ge(s_pe.h, s_pe.n)
                        s_v.inc(nc.vector.tensor_copy(out=rows[1][0:1, :], in_=sum_ps[0:1, :]))
                        s_v.inc(nc.vector.reciprocal(out=rows[1][0:1, :], in_=rows[1][0:1, :]))
                        nc.tensor.wait_ge(s_v.h, s_v.n)
                        for hh in range(2):
                            ts = slice(hh * 512, (hh + 1) * 512)
                            s_pe.inc(nc.tensor.matmul(out=rb_ps[:, ts], lhsT=ones1[:, :],
                                                      rhs=rows[1][0:1, :][:, ts], start=True, stop=True))
                        nc.scalar.wait_ge(s_pe.h, s_pe.n)
                        s_act.inc(nc.scalar.activation(out=av_sb[p][:, :], in_=av_ps[:, :], func=AF.Identity))
                        nc.vector.wait_ge(s_act.h, s_act.n)
                        s_v.inc(nc.vector.tensor_tensor(out=attn_sb[:, h * T:(h + 1) * T],
                                                        in0=av_sb[p][:, :], in1=rb_ps[:, :], op=OP.mult))
                        s_hd.inc(nc.vector.engine_nop())
                _soft_barrier(nc, [s_pe, s_act, s_v, s_out])
                with nc.psum_tensor("o0", [128, T], F32) as o0, \
                     nc.psum_tensor("o1", [128, T], F32) as o1:
                    o_ps = [o0, o1]
                    attn_done = s_v.n
                    peO = s_pe.n
                    evO_base = s_v.n
                    for oc in range(CD):
                        p = oc % 2
                        if oc >= 2:
                            nc.sync.wait_ge(s_pe.h, peO + oc - 1)
                        s_wO[p].inc(nc.sync.dma_start(out=wo_sb[p][:, :], in_=wo[oc]), 16)
                        nc.tensor.wait_ge(s_wO[p].h, s_wO[p].n)
                        nc.tensor.wait_ge(s_v.h, attn_done)
                        if oc >= 2:
                            nc.tensor.wait_ge(s_v.h, evO_base + oc - 1)
                        last = None
                        for hh in range(2):
                            ts = slice(hh * 512, (hh + 1) * 512)
                            for dk in range(CD):
                                last = nc.tensor.matmul(out=o_ps[p][:, ts],
                                                        lhsT=wo_sb[p][:, dk * 128:(dk + 1) * 128],
                                                        rhs=attn_sb[:, dk * T:(dk + 1) * T][:, ts],
                                                        start=(dk == 0), stop=(dk == CD - 1))
                        s_pe.inc(last)
                        nc.vector.wait_ge(s_pe.h, s_pe.n)
                        s_v.inc(nc.vector.scalar_tensor_tensor(
                            out=x_sb[:, oc * T:(oc + 1) * T], in0=o_ps[p][:, :], scalar=bo_sb[:, oc:oc + 1],
                            in1=x_sb[:, oc * T:(oc + 1) * T], op0=OP.add, op1=OP.add))
                nc.sync.wait_ge(s_v.h, s_v.n)
                for dk in range(CD):
                    s_out.inc(nc.sync.dma_start(out=x_out[dk], in_=x_sb[:, dk * T:(dk + 1) * T]), 16)
                nc.sync.wait_ge(s_out.h, s_out.n)
    return nc



def build_wcast():
    nc = bass.Bass("TRN2", target_bir_lowering=False)
    wc_in = nc.dram_tensor("wc_in", [32, 128, CD * 128], F32, kind="ExternalInput").ap()
    wc_out = nc.dram_tensor("wc_out", [32, 128, CD * 128], BF16, kind="ExternalOutput").ap()
    with ExitStack() as st:
        buf = [st.enter_context(nc.sbuf_tensor(f"b{p}", [128, CD * 128], BF16)) for p in range(2)]
        s_in = [Ctr(st, nc, f"s_in{p}") for p in range(2)]
        s_out = [Ctr(st, nc, f"s_out{p}") for p in range(2)]
        with nc.Block() as block:
            @block.sync
            def _(sync):
                for u in range(32):
                    p = u % 2
                    if u >= 2:
                        nc.gpsimd.wait_ge(s_out[p].h, s_out[p].n)
                    s_in[p].inc(nc.gpsimd.dma_start(out=buf[p][:, :], in_=wc_in[u]), 16)
                    nc.sync.wait_ge(s_in[p].h, s_in[p].n)
                    s_out[p].inc(nc.sync.dma_start(out=wc_out[u], in_=buf[p][:, :]), 16)
                nc.sync.wait_ge(s_out[0].h, s_out[0].n)
                nc.sync.wait_ge(s_out[1].h, s_out[1].n)
    return nc


def _wcast_host(arrs):
    cat = np.ascontiguousarray(np.concatenate(arrs, axis=0), dtype=np.float32)
    n = cat.shape[0]
    assert n == 256, n
    per = n // NCORES
    in_maps = [{"wc_in": np.ascontiguousarray(cat[c * per:(c + 1) * per])} for c in range(NCORES)]
    res = _run("wcast", build_wcast, in_maps)
    out = np.concatenate([r["wc_out"] for r in res], axis=0)
    outs = []
    off = 0
    for a in arrs:
        outs.append(np.ascontiguousarray(out[off:off + a.shape[0]]))
        off += a.shape[0]
    return outs



def build_prep():
    nc = bass.Bass("TRN2", target_bir_lowering=False)
    gu_idx = nc.dram_tensor("gu_idx", [12, 128, CD * 128], I32, kind="ExternalInput").ap()
    dn_idx = nc.dram_tensor("dn_idx", [2, 128, C0 * 128], I32, kind="ExternalInput").ap()
    lut = nc.dram_tensor("lut", [1, 16], F32, kind="ExternalInput").ap()
    w2t = nc.dram_tensor("w2t", [C1, 128, 256], F32, kind="ExternalInput").ap()
    h128 = nc.dram_tensor("h128", [128, 128], F32, kind="ExternalInput").ap()
    wc_in = nc.dram_tensor("wc_in", [32, 128, CD * 128], F32, kind="ExternalInput").ap()
    gu_w = nc.dram_tensor("gu_w", [12, 128, CD * 128], BF16, kind="ExternalOutput").ap()
    dn_w = nc.dram_tensor("dn_w", [2, 128, C0 * 128], BF16, kind="ExternalOutput").ap()
    w2f = nc.dram_tensor("w2f", [C1, 128, 256], BF16, kind="ExternalOutput").ap()
    wc_out = nc.dram_tensor("wc_out", [32, 128, CD * 128], BF16, kind="ExternalOutput").ap()
    WMAX = C0 * 128
    W = 256
    with ExitStack() as st:
        idx_sb = [st.enter_context(nc.sbuf_tensor(f"idx{p}", [128, WMAX], I32)) for p in range(2)]
        idxb = st.enter_context(nc.sbuf_tensor("idxb", [128, WMAX], BF16))
        w_bf = [st.enter_context(nc.sbuf_tensor(f"wbf{p}", [128, WMAX], BF16)) for p in range(2)]
        msk = st.enter_context(nc.sbuf_tensor("msk", [128, WMAX], BF16))
        lut_sb = st.enter_context(nc.sbuf_tensor("lut_sb", [1, 16], F32))
        lut_col = st.enter_context(nc.sbuf_tensor("lut_col", [128, 16], F32))
        ones1 = st.enter_context(nc.sbuf_tensor("ones1", [1, 128], F32))
        lut_ps = st.enter_context(nc.psum_tensor("lut_ps", [128, 16], F32))
        s_lut = Ctr(st, nc, "s_lut")
        s_in = [Ctr(st, nc, f"s_in{p}") for p in range(2)]
        s_pe = Ctr(st, nc, "s_pe")
        s_v = Ctr(st, nc, "s_v")
        s_out = Ctr(st, nc, "s_out")
        f_abf = st.enter_context(nc.sbuf_tensor("f_abf", [128, C1 * 128], BF16))
        f_a = st.enter_context(nc.sbuf_tensor("f_a", [128, C1 * 128], F32))
        f_b = st.enter_context(nc.sbuf_tensor("f_b", [128, C1 * 128], F32))
        h_sb = st.enter_context(nc.sbuf_tensor("h_sb", [128, 128], BF16))
        f_ps = [st.enter_context(nc.psum_tensor(f"f_ps{p}", [128, 128], F32)) for p in range(2)]
        f_in = Ctr(st, nc, "f_in")
        f_pe = Ctr(st, nc, "f_pe")
        f_a_c = Ctr(st, nc, "f_a_c")
        f_out = Ctr(st, nc, "f_out")
        c_buf = [st.enter_context(nc.sbuf_tensor(f"c_b{p}", [128, CD * 128], BF16)) for p in range(2)]
        c_in = [Ctr(st, nc, f"c_in{p}") for p in range(2)]
        c_out = [Ctr(st, nc, f"c_out{p}") for p in range(2)]
        with nc.Block() as block:
            @block.sync
            def _(sync):
                for u in range(32):
                    p = u % 2
                    if u >= 2:
                        nc.gpsimd.wait_ge(c_out[p].h, c_out[p].n)
                    c_in[p].inc(nc.gpsimd.dma_start(out=c_buf[p][:, :], in_=wc_in[u]), 16)
                    nc.scalar.wait_ge(c_in[p].h, c_in[p].n)
                    c_out[p].inc(nc.scalar.dma_start(out=wc_out[u], in_=c_buf[p][:, :]), 16)
                f_in.inc(nc.gpsimd.dma_start(out=h_sb[:, :], in_=h128), 16)
                FW = 128
                for sl in range(2):
                    co = sl * FW
                    if sl >= 1:
                        nc.gpsimd.wait_ge(f_pe.h, f_pe.n)
                    for c in range(C1):
                        f_in.inc(nc.gpsimd.dma_start(out=f_abf[:, c * FW:(c + 1) * FW],
                                                     in_=w2t[c][:, co:co + FW]), 16)
                    nc.tensor.wait_ge(f_in.h, f_in.n)
                    pe0 = f_pe.n
                    ac0 = f_a_c.n
                    for c in range(C1):
                        p = c % 2
                        if c >= 2:
                            nc.tensor.wait_ge(f_a_c.h, ac0 + c - 1)
                        f_pe.inc(nc.tensor.matmul(out=f_ps[p][:, 0:FW], lhsT=h_sb[:, :],
                                                  rhs=f_abf[:, c * FW:(c + 1) * FW], start=True, stop=True))
                        nc.scalar.wait_ge(f_pe.h, f_pe.n)
                        f_a_c.inc(nc.scalar.activation(out=f_b[:, c * FW:(c + 1) * FW], in_=f_ps[p][:, 0:FW],
                                                       func=AF.Identity, scale=1.0 / math.sqrt(I1)))
                    nc.vector.wait_ge(f_a_c.h, f_a_c.n)
                    if sl >= 1:
                        nc.vector.wait_ge(f_out.h, 16 * C1)
                    fsrc, fdst = f_b, f_a
                    for s in range(6):
                        step = 1 << s
                        last = None
                        for c in range(C1):
                            if c & step:
                                continue
                            cc = c + step
                            nc.vector.tensor_tensor(out=fdst[:, c * FW:(c + 1) * FW],
                                                    in0=fsrc[:, c * FW:(c + 1) * FW],
                                                    in1=fsrc[:, cc * FW:(cc + 1) * FW], op=OP.add)
                            last = nc.vector.tensor_tensor(out=fdst[:, cc * FW:(cc + 1) * FW],
                                                           in0=fsrc[:, c * FW:(c + 1) * FW],
                                                           in1=fsrc[:, cc * FW:(cc + 1) * FW],
                                                           op=OP.subtract)
                        f_a_c.inc(last)
                        nc.vector.wait_ge(f_a_c.h, f_a_c.n)
                        fsrc, fdst = fdst, fsrc
                    nc.gpsimd.wait_ge(f_a_c.h, f_a_c.n)
                    for c in range(C1):
                        f_out.inc(nc.gpsimd.dma_start(out=w2f[c][:, co:co + FW],
                                                      in_=fsrc[:, c * FW:(c + 1) * FW]), 16)
                s_lut.inc(nc.sync.dma_start(out=lut_sb[:, :], in_=lut), 16)
                nc.vector.wait_ge(s_lut.h, 16)
                s_v.inc(nc.vector.memset(ones1[:, :], 1.0))
                nc.tensor.wait_ge(s_v.h, 1)
                s_pe.inc(nc.tensor.matmul(out=lut_ps[:, :], lhsT=ones1[:, :], rhs=lut_sb[:, :],
                                          start=True, stop=True))
                nc.scalar.wait_ge(s_pe.h, s_pe.n)
                s_v.inc(nc.scalar.copy(out=lut_col[:, :], in_=lut_ps[:, :]))
                units = [(gu_idx, gu_w, CD * 128)] * 12 + [(dn_idx, dn_w, C0 * 128)] * 2
                for ui, (usrc, udst, width) in enumerate(units):
                    u = ui if ui < 12 else ui - 12
                    p = ui % 2
                    if ui >= 2:
                        nc.sync.wait_ge(s_v.h, 2 + (ui - 2) + 1)
                    s_in[p].inc(nc.sync.dma_start(out=idx_sb[p][:, :width], in_=usrc[u]), 16)
                    nc.vector.wait_ge(s_in[p].h, s_in[p].n)
                    nc.vector.wait_ge(s_v.h, 2)
                    if ui >= 2:
                        nc.vector.wait_ge(s_out.h, 16 * (ui - 1))
                    nc.vector.tensor_copy(out=idxb[:, :width], in_=idx_sb[p][:, :width])
                    last = None
                    for k in range(16):
                        if k == 0:
                            nc.vector.tensor_scalar(out=w_bf[p][:, :width], in0=idxb[:, :width],
                                                    scalar1=0.0, scalar2=lut_col[:, 0:1],
                                                    op0=OP.is_equal, op1=OP.mult)
                        else:
                            nc.vector.tensor_scalar(out=msk[:, :width], in0=idxb[:, :width],
                                                    scalar1=float(k), scalar2=lut_col[:, k:k + 1],
                                                    op0=OP.is_equal, op1=OP.mult)
                            last = nc.vector.tensor_tensor(out=w_bf[p][:, :width],
                                                           in0=w_bf[p][:, :width],
                                                           in1=msk[:, :width], op=OP.add)
                    s_v.inc(last)
                    nc.gpsimd.wait_ge(s_v.h, 2 + ui + 1)
                    s_out.inc(nc.gpsimd.dma_start(out=udst[u], in_=w_bf[p][:, :width]), 16)
                nc.gpsimd.wait_ge(s_out.h, 16 * 14)
                nc.gpsimd.wait_ge(f_out.h, 16 * 2 * C1)
                nc.gpsimd.wait_ge(c_out[0].h, c_out[0].n)
                nc.gpsimd.wait_ge(c_out[1].h, c_out[1].n)
    return nc


def _prep_host(lut, gate_idx, up_idx, down_idx, W2, cast_arrs):
    gidxT = np.ascontiguousarray(gate_idx.T)
    uidxT = np.ascontiguousarray(up_idx.T)
    didxT = np.ascontiguousarray(down_idx.T)
    gu = np.concatenate([_pw(gidxT, pad_oc=C0P), _pw(uidxT, pad_oc=C0P)], axis=0)
    dn = _pw(didxT)
    lut2 = np.ascontiguousarray(np.asarray(lut, dtype=np.float32).reshape(1, 16))
    w2t = np.ascontiguousarray(np.asarray(W2, dtype=np.float32).T)
    h128 = _hadamard(128)
    cat = np.ascontiguousarray(np.concatenate(cast_arrs, axis=0), dtype=np.float32)
    per = cat.shape[0] // NCORES
    in_maps = []
    for c in range(NCORES):
        in_maps.append({
            "gu_idx": np.ascontiguousarray(gu[c * 12:(c + 1) * 12]),
            "dn_idx": np.ascontiguousarray(dn[c * 2:(c + 1) * 2]),
            "lut": lut2,
            "w2t": np.ascontiguousarray(w2t[:, c * 256:(c + 1) * 256].reshape(C1, 128, 256)),
            "h128": h128,
            "wc_in": np.ascontiguousarray(cat[c * per:(c + 1) * per]),
        })
    res = _run("prep", build_prep, in_maps)
    gu_w = np.concatenate([r["gu_w"] for r in res], axis=0)
    dn_w = np.concatenate([r["dn_w"] for r in res], axis=0)
    wg0 = np.ascontiguousarray(gu_w[:C0])
    wu0 = np.ascontiguousarray(gu_w[C0P:C0P + C0])
    wd0 = np.ascontiguousarray(dn_w)
    w2f_full = np.empty((I1, D), dtype=NPBF16)
    for c in range(NCORES):
        w2f_full[:, c * 256:(c + 1) * 256] = res[c]["w2f"].reshape(I1, 256)
    w2f_t = _pw(w2f_full)
    cast_out = np.concatenate([r["wc_out"] for r in res], axis=0)
    outs = []
    off = 0
    for a in cast_arrs:
        outs.append(np.ascontiguousarray(cast_out[off:off + a.shape[0]]))
        off += a.shape[0]
    return wg0, wu0, wd0, w2f_t, outs



def _shard_tokens(x):
    shards = []
    for c in range(NCORES):
        b, hh = c // 2, c % 2
        shards.append(_fm(x[b, hh * T:(hh + 1) * T, :]).reshape(CD, 128, T))
    return shards


def _unshard_tokens(shards):
    out = np.empty((4, 2048, D), dtype=np.float32)
    for c in range(NCORES):
        b, hh = c // 2, c % 2
        out[b, hh * T:(hh + 1) * T, :] = shards[c].reshape(D, T).T
    return out


def _run(key, build_fn, in_maps):
    if key not in _PROG_CACHE:
        _PROG_CACHE[key] = build_fn()
    res = run_bass_kernel_spmd(_PROG_CACHE[key], in_maps, core_ids=list(range(NCORES)))
    return res.results


def _ghost_host(lut, gate_idx, up_idx, down_idx):
    gidxT = np.ascontiguousarray(gate_idx.T)
    uidxT = np.ascontiguousarray(up_idx.T)
    didxT = np.ascontiguousarray(down_idx.T)
    gu = np.concatenate([_pw(gidxT, pad_oc=C0P), _pw(uidxT, pad_oc=C0P)], axis=0)
    dn = _pw(didxT)
    lut2 = np.ascontiguousarray(np.asarray(lut, dtype=np.float32).reshape(1, 16))
    in_maps = [{"gu_idx": np.ascontiguousarray(gu[c * 12:(c + 1) * 12]),
                "dn_idx": np.ascontiguousarray(dn[c * 2:(c + 1) * 2]),
                "lut": lut2} for c in range(NCORES)]
    res = _run("ghost", build_ghost, in_maps)
    gu_w = np.concatenate([r["gu_w"] for r in res], axis=0)
    dn_w = np.concatenate([r["dn_w"] for r in res], axis=0)
    return (np.ascontiguousarray(gu_w[:C0]), np.ascontiguousarray(gu_w[C0P:C0P + C0]),
            np.ascontiguousarray(dn_w))


def _w2fold_host(W2):
    w2t = np.ascontiguousarray(np.asarray(W2, dtype=np.float32).T)
    h128 = _hadamard(128)
    in_maps = [{"w2t": np.ascontiguousarray(w2t[:, c * 256:(c + 1) * 256].reshape(C1, 128, 256)),
                "h128": h128} for c in range(NCORES)]
    res = _run("w2fold", build_w2fold, in_maps)
    w2f = np.empty((I1, D), dtype=NPBF16)
    for c in range(NCORES):
        w2f[:, c * 256:(c + 1) * 256] = res[c]["w2f"].reshape(I1, 256)
    return _pw(w2f)


def _ffn_host(x_shards, wg_t, wu_t, wd_t, C):
    wdtype = BF16 if wg_t.dtype == NPBF16 else F32
    wdtype_d = BF16 if wd_t.dtype == NPBF16 else F32
    key = ("ffn", C, str(wg_t.dtype), str(wd_t.dtype))
    in_maps = [{"x": x_shards[c], "wg": wg_t, "wu": wu_t, "wd": wd_t} for c in range(NCORES)]
    res = _run(key, lambda: build_ffn(C, wdtype, wdtype_d), in_maps)
    return [np.ascontiguousarray(r["x_out"]) for r in res]


def _attn_host(x_shards, qs, ks, vs, wo_t, bo_cols, causal, qk_dt):
    in_maps = []
    for c in range(NCORES):
        b, hh = c // 2, c % 2
        k_full = np.ascontiguousarray(np.concatenate([ks[2 * b], ks[2 * b + 1]], axis=2))
        v_full = np.concatenate([vs[2 * b], vs[2 * b + 1]], axis=0)
        v_h = np.ascontiguousarray(
            v_full.reshape(16, 128, 16, 128).transpose(2, 1, 0, 3).reshape(H, 128, 2048))
        qpos = np.arange(hh * T, (hh + 1) * T, dtype=np.float32).reshape(1, T)
        in_maps.append({"x": x_shards[c], "q": qs[c], "k": k_full, "v": v_h,
                        "wo": wo_t, "bo": bo_cols, "qpos": qpos})
    res = _run(("attn", causal, str(qk_dt)), lambda: build_attn(causal, qk_dt), in_maps)
    return [np.ascontiguousarray(r["x_out"]) for r in res]


def kernel(L0_attn_x, L0_attn_qkv_w, L0_attn_qkv_b, L0_attn_out_w, L0_attn_out_b,
           L0_ffn_lut, L0_ffn_gate_idx, L0_ffn_up_idx, L0_ffn_down_idx,
           L1_attn_g_q, L1_attn_Wq, L1_attn_bq, L1_attn_g_kv, L1_attn_Wkv, L1_attn_bkv,
           L1_attn_g_head, L1_attn_Wo, L1_attn_bo, L1_ffn_W1, L1_ffn_W2, L1_ffn_W3):
    f32 = lambda a: np.asarray(a, dtype=np.float32)
    qkvT = np.ascontiguousarray(f32(L0_attn_qkv_w).T)
    wqk0_t = _pw(np.ascontiguousarray(qkvT[:, :2 * D]))
    wv0_r = np.ascontiguousarray(qkvT[:, 2 * D:].reshape(CD, 128, D))
    bqk0 = _pb(f32(L0_attn_qkv_b)[:2 * D])
    bv0 = np.ascontiguousarray(f32(L0_attn_qkv_b)[2 * D:].reshape(1, D))
    wo0_t = _pw(np.ascontiguousarray(f32(L0_attn_out_w).T))
    bo0 = _pb(f32(L0_attn_out_b))
    wq1_t = _pw(f32(L1_attn_Wq))
    wk1_t = _pw(np.ascontiguousarray(f32(L1_attn_Wkv)[:, :D]))
    wv1_r = np.ascontiguousarray(f32(L1_attn_Wkv)[:, D:].reshape(CD, 128, D))
    wg1_t = _pw(np.ascontiguousarray(f32(L1_ffn_W1).T))
    wu1_t = _pw(np.ascontiguousarray(f32(L1_ffn_W3).T))
    wg0, wu0, wd0, w2f, casts = _prep_host(
        f32(L0_ffn_lut), np.asarray(L0_ffn_gate_idx), np.asarray(L0_ffn_up_idx),
        np.asarray(L0_ffn_down_idx), f32(L1_ffn_W2),
        [wqk0_t, wv0_r.reshape(CD, 128, D), wo0_t, wq1_t, wk1_t, wv1_r.reshape(CD, 128, D),
         _pw(f32(L1_attn_Wo)), wg1_t, wu1_t])
    (wqk0_t, wv0_r, wo0_t, wq1_t, wk1_t, wv1_r, wo1_t, wg1_t, wu1_t) = casts
    bq1 = _pb(f32(L1_attn_bq))
    bk1 = _pb(f32(L1_attn_bkv)[:D])
    bv1 = np.ascontiguousarray(f32(L1_attn_bkv)[D:].reshape(1, D))
    gq1 = _pb(f32(L1_attn_g_q))
    gkv1 = _pb(f32(L1_attn_g_kv))
    gh1 = np.ascontiguousarray(f32(L1_attn_g_head).reshape(128, 1))
    bo1 = _pb(f32(L1_attn_bo))

    x = _shard_tokens(f32(L0_attn_x))
    x = _ffn_host(x, wg0, wu0, wd0, C0)
    in_maps = [{"x": x[c], "wqk": wqk0_t, "wv": wv0_r, "bqk": bqk0, "bv": bv0}
               for c in range(NCORES)]
    res = _run("qkv0", build_qkv0, in_maps)
    x = _attn_host(x, [r["q"] for r in res], [r["k"] for r in res], [r["v"] for r in res],
                   wo0_t, bo0, True, BF16)
    x = _ffn_host(x, wg0, wu0, wd0, C0)
    x = _ffn_host(x, wg1_t, wu1_t, w2f, C1)
    in_maps = [{"x": x[c], "wq": wq1_t, "wk": wk1_t, "wv": wv1_r, "bq": bq1, "bk": bk1,
                "bv": bv1, "gq": gq1, "gkv": gkv1, "ghead": gh1} for c in range(NCORES)]
    res = _run("qkv1", build_qkv1, in_maps)
    x = _attn_host(x, [r["q"] for r in res], [r["k"] for r in res], [r["v"] for r in res],
                   wo1_t, bo1, False, F16)
    x = _ffn_host(x, wg1_t, wu1_t, w2f, C1)
    return _unshard_tokens(x)


# version 10
# speedup vs baseline: 1.0468x; 1.0241x over previous
import math
from contextlib import ExitStack

import numpy as np
import ml_dtypes

import concourse.bass as bass
import concourse.mybir as mybir
from concourse.bass_utils import run_bass_kernel_spmd

F32 = mybir.dt.float32
BF16 = mybir.dt.bfloat16
F16 = mybir.dt.float16
I32 = mybir.dt.int32
AF = mybir.ActivationFunctionType
OP = mybir.AluOpType
AX = mybir.AxisListType

NPBF16 = ml_dtypes.bfloat16

D = 2048
CD = D // 128
T = 1024
NCORES = 8
H = 16
HD = 128
I0 = 5504
C0 = I0 // 128
C0P = 48
I1 = 8192
C1 = I1 // 128
LN_EPS = 1e-5
RMS_EPS = float(np.finfo(np.float32).eps)
INVSQRT_HD = 1.0 / math.sqrt(HD)

_PROG_CACHE = {}



def _pw(w, pad_oc=None):
    K, O = w.shape
    KC, OC = K // 128, O // 128
    t = np.ascontiguousarray(
        w.reshape(KC, 128, OC, 128).transpose(2, 1, 0, 3).reshape(OC, 128, KC * 128))
    if pad_oc is not None and pad_oc > OC:
        t = np.concatenate([t, np.zeros((pad_oc - OC, 128, KC * 128), dtype=t.dtype)], axis=0)
    return t


def _pb(b):
    O = b.shape[0]
    return np.ascontiguousarray(np.asarray(b, dtype=np.float32).reshape(O // 128, 128).T)


def _fm(x):
    return np.ascontiguousarray(np.asarray(x, dtype=np.float32).T)


def _hadamard(n):
    h = np.array([[1.0]], dtype=np.float32)
    while h.shape[0] < n:
        h = np.block([[h, h], [h, -h]])
    return np.ascontiguousarray(h.astype(np.float32))


class Ctr:

    def __init__(self, st, nc, name):
        self.h = st.enter_context(nc.semaphore(name))
        self.n = 0

    def inc(self, instr, k=1):
        instr.then_inc(self.h, k)
        self.n += k
        return instr



def build_ghost():
    nc = bass.Bass("TRN2", target_bir_lowering=False)
    gu_idx = nc.dram_tensor("gu_idx", [12, 128, CD * 128], I32, kind="ExternalInput").ap()
    dn_idx = nc.dram_tensor("dn_idx", [2, 128, C0 * 128], I32, kind="ExternalInput").ap()
    lut = nc.dram_tensor("lut", [1, 16], F32, kind="ExternalInput").ap()
    gu_w = nc.dram_tensor("gu_w", [12, 128, CD * 128], BF16, kind="ExternalOutput").ap()
    dn_w = nc.dram_tensor("dn_w", [2, 128, C0 * 128], BF16, kind="ExternalOutput").ap()
    WMAX = C0 * 128
    with ExitStack() as st:
        idx_sb = [st.enter_context(nc.sbuf_tensor(f"idx{p}", [128, WMAX], I32)) for p in range(2)]
        idxb = st.enter_context(nc.sbuf_tensor("idxb", [128, WMAX], BF16))
        w_bf = [st.enter_context(nc.sbuf_tensor(f"wbf{p}", [128, WMAX], BF16)) for p in range(2)]
        msk = st.enter_context(nc.sbuf_tensor("msk", [128, WMAX], BF16))
        lut_sb = st.enter_context(nc.sbuf_tensor("lut_sb", [1, 16], F32))
        lut_col = st.enter_context(nc.sbuf_tensor("lut_col", [128, 16], F32))
        ones1 = st.enter_context(nc.sbuf_tensor("ones1", [1, 128], F32))
        lut_ps = st.enter_context(nc.psum_tensor("lut_ps", [128, 16], F32))
        s_lut = Ctr(st, nc, "s_lut")
        s_in = [Ctr(st, nc, f"s_in{p}") for p in range(2)]
        s_pe = Ctr(st, nc, "s_pe")
        s_v = Ctr(st, nc, "s_v")
        s_out = Ctr(st, nc, "s_out")
        with nc.Block() as block:
            @block.sync
            def _(sync):
                s_lut.inc(nc.sync.dma_start(out=lut_sb[:, :], in_=lut), 16)
                nc.vector.wait_ge(s_lut.h, 16)
                s_v.inc(nc.vector.memset(ones1[:, :], 1.0))
                nc.tensor.wait_ge(s_v.h, 1)
                s_pe.inc(nc.tensor.matmul(out=lut_ps[:, :], lhsT=ones1[:, :], rhs=lut_sb[:, :],
                                          start=True, stop=True))
                nc.scalar.wait_ge(s_pe.h, 1)
                s_v.inc(nc.scalar.copy(out=lut_col[:, :], in_=lut_ps[:, :]))

                units = [(gu_idx, gu_w, CD * 128)] * 12 + [(dn_idx, dn_w, C0 * 128)] * 2
                for ui, (src, dst, width) in enumerate(units):
                    u = ui if ui < 12 else ui - 12
                    p = ui % 2
                    if ui >= 2:
                        nc.sync.wait_ge(s_v.h, 2 + (ui - 2) + 1)
                    s_in[p].inc(nc.sync.dma_start(out=idx_sb[p][:, :width], in_=src[u]), 16)
                    nc.vector.wait_ge(s_in[p].h, s_in[p].n)
                    nc.vector.wait_ge(s_v.h, 2)
                    if ui >= 2:
                        nc.vector.wait_ge(s_out.h, 16 * (ui - 1))
                    nc.vector.tensor_copy(out=idxb[:, :width], in_=idx_sb[p][:, :width])
                    last = None
                    for k in range(16):
                        if k == 0:
                            nc.vector.tensor_scalar(out=w_bf[p][:, :width], in0=idxb[:, :width],
                                                    scalar1=0.0, scalar2=lut_col[:, 0:1],
                                                    op0=OP.is_equal, op1=OP.mult)
                        else:
                            nc.vector.tensor_scalar(out=msk[:, :width], in0=idxb[:, :width],
                                                    scalar1=float(k), scalar2=lut_col[:, k:k + 1],
                                                    op0=OP.is_equal, op1=OP.mult)
                            last = nc.vector.tensor_tensor(out=w_bf[p][:, :width],
                                                           in0=w_bf[p][:, :width],
                                                           in1=msk[:, :width], op=OP.add)
                    s_v.inc(last)
                    nc.gpsimd.wait_ge(s_v.h, 2 + ui + 1)
                    s_out.inc(nc.gpsimd.dma_start(out=dst[u], in_=w_bf[p][:, :width]), 16)
                nc.gpsimd.wait_ge(s_out.h, 16 * 14)
    return nc



def build_w2fold():
    nc = bass.Bass("TRN2", target_bir_lowering=False)
    w2t = nc.dram_tensor("w2t", [C1, 128, 256], F32, kind="ExternalInput").ap()
    h128 = nc.dram_tensor("h128", [128, 128], F32, kind="ExternalInput").ap()
    w2f = nc.dram_tensor("w2f", [C1, 128, 256], BF16, kind="ExternalOutput").ap()
    W = 256
    with ExitStack() as st:
        a_bf = st.enter_context(nc.sbuf_tensor("arena_abf", [128, C1 * W], BF16))
        a = st.enter_context(nc.sbuf_tensor("arena_a", [128, C1 * W], F32))
        b = st.enter_context(nc.sbuf_tensor("arena_b", [128, C1 * W], F32))
        h_sb = st.enter_context(nc.sbuf_tensor("h_sb", [128, 128], BF16))
        ps = [st.enter_context(nc.psum_tensor(f"ps{p}", [128, W], F32)) for p in range(2)]
        s_in = Ctr(st, nc, "s_in")
        s_pe = Ctr(st, nc, "s_pe")
        s_a = Ctr(st, nc, "s_a")
        s_out = Ctr(st, nc, "s_out")
        with nc.Block() as block:
            @block.sync
            def _(sync):
                s_in.inc(nc.gpsimd.dma_start(out=h_sb[:, :], in_=h128), 16)
                for c in range(C1):
                    s_in.inc(nc.gpsimd.dma_start(out=a_bf[:, c * W:(c + 1) * W], in_=w2t[c]), 16)
                nc.tensor.wait_ge(s_in.h, 16 * (C1 + 1))
                for c in range(C1):
                    p = c % 2
                    if c >= 2:
                        nc.tensor.wait_ge(s_a.h, c - 1)
                    s_pe.inc(nc.tensor.matmul(out=ps[p][:, :], lhsT=h_sb[:, :],
                                              rhs=a_bf[:, c * W:(c + 1) * W], start=True, stop=True))
                    nc.scalar.wait_ge(s_pe.h, c + 1)
                    s_a.inc(nc.scalar.activation(out=b[:, c * W:(c + 1) * W], in_=ps[p][:, :],
                                                 func=AF.Identity, scale=1.0 / math.sqrt(I1)))
                nc.vector.wait_ge(s_a.h, C1)
                src, dst = b, a
                last = None
                for s in range(6):
                    step = 1 << s
                    for c in range(C1):
                        if c & step:
                            continue
                        cc = c + step
                        nc.vector.tensor_tensor(out=dst[:, c * W:(c + 1) * W],
                                                in0=src[:, c * W:(c + 1) * W],
                                                in1=src[:, cc * W:(cc + 1) * W], op=OP.add)
                        last = nc.vector.tensor_tensor(out=dst[:, cc * W:(cc + 1) * W],
                                                       in0=src[:, c * W:(c + 1) * W],
                                                       in1=src[:, cc * W:(cc + 1) * W],
                                                       op=OP.subtract)
                    src, dst = dst, src
                s_a.inc(last)
                nc.gpsimd.wait_ge(s_a.h, C1 + 1)
                for c in range(C1):
                    s_out.inc(nc.gpsimd.dma_start(out=w2f[c], in_=src[:, c * W:(c + 1) * W]), 16)
                nc.gpsimd.wait_ge(s_out.h, 16 * C1)
    return nc



def emit_ln_stats(nc, s_pe, s_act, s_v, x_sb, scr_a, scr_b, stat_ps, rows, ones_bf,
                  eps_ap, inv_n, rms=False, n_tok=T):
    nh = n_tok // 512
    for dk in range(CD):
        sl = slice(dk * n_tok, (dk + 1) * n_tok)
        pa = dk % 2
        if dk >= 2:
            nc.scalar.wait_ge(s_pe.h, s_pe.n)
        if not rms:
            s_act.inc(nc.scalar.activation(out=scr_a[pa][:, :n_tok], in_=x_sb[:, sl], func=AF.Identity))
        s_act.inc(nc.scalar.activation(out=scr_b[pa][:, :n_tok], in_=x_sb[:, sl], func=AF.Square))
        nc.tensor.wait_ge(s_act.h, s_act.n)
        for hh in range(nh):
            ts = slice(hh * 512, (hh + 1) * 512)
            if not rms:
                s_pe.inc(nc.tensor.matmul(out=stat_ps[0][:, ts], lhsT=ones_bf[:, :],
                                          rhs=scr_a[pa][:, ts], start=(dk == 0), stop=(dk == CD - 1)))
            s_pe.inc(nc.tensor.matmul(out=stat_ps[1][:, ts], lhsT=ones_bf[:, :],
                                      rhs=scr_b[pa][:, ts], start=(dk == 0), stop=(dk == CD - 1)))
    nc.scalar.wait_ge(s_pe.h, s_pe.n)
    if not rms:
        s_act.inc(nc.scalar.activation(out=rows[0][0:1, :n_tok], in_=stat_ps[0][0:1, :n_tok],
                                       func=AF.Identity, scale=inv_n))
    s_act.inc(nc.scalar.activation(out=rows[1][0:1, :n_tok], in_=stat_ps[1][0:1, :n_tok],
                                   func=AF.Identity, scale=inv_n))
    nc.vector.wait_ge(s_act.h, s_act.n)
    if not rms:
        s_v.inc(nc.vector.scalar_tensor_tensor(out=rows[2][0:1, :n_tok], in0=rows[0][0:1, :n_tok],
                                               scalar=-1.0, in1=rows[0][0:1, :n_tok],
                                               op0=OP.mult, op1=OP.mult))
        s_v.inc(nc.vector.tensor_tensor(out=rows[2][0:1, :n_tok], in0=rows[1][0:1, :n_tok],
                                        in1=rows[2][0:1, :n_tok], op=OP.add))
        var_row = rows[2][0:1, :n_tok]
        nc.scalar.wait_ge(s_v.h, s_v.n)
    else:
        var_row = rows[1][0:1, :n_tok]
    s_act.inc(nc.scalar.activation(out=rows[3][0:1, :n_tok], in_=var_row, func=AF.Ln, bias=eps_ap))
    s_act.inc(nc.scalar.activation(out=rows[3][0:1, :n_tok], in_=rows[3][0:1, :n_tok], func=AF.Exp,
                                   scale=-0.5))
    if not rms:
        nc.vector.wait_ge(s_act.h, s_act.n)
        s_v.inc(nc.vector.scalar_tensor_tensor(out=rows[2][0:1, :n_tok], in0=rows[0][0:1, :n_tok],
                                               scalar=-1.0, in1=rows[3][0:1, :n_tok],
                                               op0=OP.mult, op1=OP.mult))


def emit_bcast_rows(nc, s_pe, srcs, dsts, ones1, n_tok, wait_sem=None, wait_val=None):
    if wait_sem is not None:
        nc.tensor.wait_ge(wait_sem, wait_val)
    for src, dst in zip(srcs, dsts):
        for hh in range(n_tok // 512):
            ts = slice(hh * 512, (hh + 1) * 512)
            s_pe.inc(nc.tensor.matmul(out=dst[:, ts], lhsT=ones1[:, :], rhs=src[:, ts],
                                      start=True, stop=True))



DEBUG_FFN = False


def build_ffn(C, wdtype, wdtype_d=None):
    if wdtype_d is None:
        wdtype_d = wdtype
    nc = bass.Bass("TRN2", target_bir_lowering=False)
    tb = 512
    if DEBUG_FFN:
        xh_dbg = nc.dram_tensor("xh_dbg", [CD, 128, 512], BF16, kind="ExternalOutput").ap()
        h_dbg = nc.dram_tensor("h_dbg", [C, 128, 512], BF16, kind="ExternalOutput").ap()
        wd_dbg = nc.dram_tensor("wd_dbg", [2, 128, C * 128], BF16, kind="ExternalOutput").ap()
    x_in = nc.dram_tensor("x", [CD, 128, T], F32, kind="ExternalInput").ap()
    wg = nc.dram_tensor("wg", [C, 128, CD * 128], wdtype, kind="ExternalInput").ap()
    wu = nc.dram_tensor("wu", [C, 128, CD * 128], wdtype, kind="ExternalInput").ap()
    wd = nc.dram_tensor("wd", [CD, 128, C * 128], wdtype_d, kind="ExternalInput").ap()
    x_out = nc.dram_tensor("x_out", [CD, 128, T], F32, kind="ExternalOutput").ap()
    n_tb = T // tb
    with ExitStack() as st:
        x_sb = st.enter_context(nc.sbuf_tensor("x_sb", [128, CD * T], F32))
        xhat = st.enter_context(nc.sbuf_tensor("xhat", [128, CD * tb], BF16))
        h_sb = st.enter_context(nc.sbuf_tensor("h_sb", [128, C * tb], BF16))
        scr = [st.enter_context(nc.sbuf_tensor(f"scr{p}", [128, T], BF16)) for p in range(2)]
        scrq = [st.enter_context(nc.sbuf_tensor(f"scrq{p}", [128, T], BF16)) for p in range(2)]
        scr_f = st.enter_context(nc.sbuf_tensor("scr_f", [128, tb], F32))
        sg = [st.enter_context(nc.sbuf_tensor(f"sg{p}", [128, tb], BF16)) for p in range(2)]
        rows = [st.enter_context(nc.sbuf_tensor(f"rows{i}", [1, T], F32)) for i in range(4)]
        ones_bf = st.enter_context(nc.sbuf_tensor("ones_bf", [128, 1], BF16))
        ones1 = st.enter_context(nc.sbuf_tensor("ones1", [1, 128], F32))
        eps_sb = st.enter_context(nc.sbuf_tensor("eps_sb", [1, 1], F32))
        wg_sb = [st.enter_context(nc.sbuf_tensor(f"wg{p}", [128, CD * 128], BF16)) for p in range(2)]
        wu_sb = [st.enter_context(nc.sbuf_tensor(f"wu{p}", [128, CD * 128], BF16)) for p in range(2)]
        wd1 = st.enter_context(nc.sbuf_tensor("wd1", [128, C * 128], BF16))
        s_x = Ctr(st, nc, "s_x")
        s_wA = [Ctr(st, nc, f"s_wA{p}") for p in range(2)]
        s_wD = [Ctr(st, nc, f"s_wD{p}") for p in range(2)]
        s_pe = Ctr(st, nc, "s_pe")
        s_act = Ctr(st, nc, "s_act")
        s_v = Ctr(st, nc, "s_v")
        s_out = Ctr(st, nc, "s_out")
        with nc.Block() as block:
            @block.sync
            def _(sync):
                for dk in range(CD):
                    s_x.inc(nc.sync.dma_start(out=x_sb[:, dk * T:(dk + 1) * T], in_=x_in[dk]), 16)
                s_v.inc(nc.vector.memset(ones_bf[:, :], 1.0))
                s_v.inc(nc.vector.memset(ones1[:, :], 1.0))
                s_v.inc(nc.vector.memset(eps_sb[:, :], LN_EPS))
                nc.scalar.wait_ge(s_x.h, 16 * CD)
                nc.scalar.wait_ge(s_v.h, 3)
                nc.tensor.wait_ge(s_v.h, 3)
                nc.vector.wait_ge(s_x.h, 16 * CD)
                with nc.psum_tensor("stat0", [1, T], F32) as st0, \
                     nc.psum_tensor("stat1", [1, T], F32) as st1:
                    emit_ln_stats(nc, s_pe, s_act, s_v, x_sb, scr, scrq, [st0, st1], rows,
                                  ones_bf, eps_sb[0:1, 0:1], 1.0 / D)
                for tbi in range(n_tb):
                    t0 = tbi * tb
                    with nc.psum_tensor(f"bc0_{tbi}", [128, tb], F32) as bc0, \
                         nc.psum_tensor(f"bc1_{tbi}", [128, tb], F32) as bc1:
                        emit_bcast_rows(nc, s_pe, [rows[3][0:1, t0:t0 + tb], rows[2][0:1, t0:t0 + tb]],
                                        [bc0, bc1], ones1, tb, s_v.h, s_v.n)
                        nc.vector.wait_ge(s_pe.h, s_pe.n)
                        for dk in range(CD):
                            sl = slice(dk * T + t0, dk * T + t0 + tb)
                            nc.vector.tensor_tensor(out=scr_f[:, :], in0=x_sb[:, sl],
                                                    in1=bc0[:, :], op=OP.mult)
                            last = nc.vector.tensor_tensor(out=xhat[:, dk * tb:(dk + 1) * tb],
                                                           in0=scr_f[:, :], in1=bc1[:, :], op=OP.add)
                        s_v.inc(last)
                    apply_done = s_v.n
                    if DEBUG_FFN and tbi == 0:
                        nc.sync.wait_ge(s_v.h, apply_done)
                        for dk in range(CD):
                            s_out.inc(nc.sync.dma_start(out=xh_dbg[dk], in_=xhat[:, dk * tb:(dk + 1) * tb]), 16)
                    with nc.psum_tensor(f"g0_{tbi}", [128, tb], F32) as g0, \
                         nc.psum_tensor(f"g1_{tbi}", [128, tb], F32) as g1, \
                         nc.psum_tensor(f"u0_{tbi}", [128, tb], F32) as u0, \
                         nc.psum_tensor(f"u1_{tbi}", [128, tb], F32) as u1:
                        g_ps, u_ps = [g0, g1], [u0, u1]
                        silu_base = s_act.n
                        mul_base = s_v.n
                        peA_base = s_pe.n
                        for c in range(C):
                            p = c % 2
                            if c >= 2:
                                nc.sync.wait_ge(s_pe.h, peA_base + 2 * c - 2)
                            s_wA[p].inc(nc.sync.dma_start(out=wg_sb[p][:, :], in_=wg[c]), 16)
                            s_wA[p].inc(nc.sync.dma_start(out=wu_sb[p][:, :], in_=wu[c]), 16)
                            nc.tensor.wait_ge(s_wA[p].h, s_wA[p].n)
                            nc.tensor.wait_ge(s_v.h, apply_done)
                            if c >= 2:
                                nc.tensor.wait_ge(s_act.h, silu_base + c - 1)
                                nc.tensor.wait_ge(s_v.h, mul_base + c - 1)
                            last = None
                            for dk in range(CD):
                                last = nc.tensor.matmul(out=g_ps[p][:, :],
                                                        lhsT=wg_sb[p][:, dk * 128:(dk + 1) * 128],
                                                        rhs=xhat[:, dk * tb:(dk + 1) * tb],
                                                        start=(dk == 0), stop=(dk == CD - 1))
                            s_pe.inc(last)
                            for dk in range(CD):
                                last = nc.tensor.matmul(out=u_ps[p][:, :],
                                                        lhsT=wu_sb[p][:, dk * 128:(dk + 1) * 128],
                                                        rhs=xhat[:, dk * tb:(dk + 1) * tb],
                                                        start=(dk == 0), stop=(dk == CD - 1))
                            s_pe.inc(last)
                            nc.scalar.wait_ge(s_pe.h, s_pe.n - 1)
                            if c >= 2:
                                nc.scalar.wait_ge(s_v.h, mul_base + c - 1)
                            s_act.inc(nc.scalar.activation(out=sg[p][:, :], in_=g_ps[p][:, :],
                                                           func=AF.Silu))
                            nc.vector.wait_ge(s_pe.h, s_pe.n)
                            nc.vector.wait_ge(s_act.h, silu_base + c + 1)
                            s_v.inc(nc.vector.tensor_tensor(out=h_sb[:, c * tb:(c + 1) * tb],
                                                            in0=sg[p][:, :], in1=u_ps[p][:, :],
                                                            op=OP.mult))
                    mulA_end = s_v.n
                    if DEBUG_FFN and tbi == 0:
                        nc.sync.wait_ge(s_v.h, mulA_end)
                        for c in range(C):
                            s_out.inc(nc.sync.dma_start(out=h_dbg[c], in_=h_sb[:, c * tb:(c + 1) * tb]), 16)
                        nc.sync.wait_ge(s_out.h, s_out.n)
                    with nc.psum_tensor(f"d0_{tbi}", [128, tb], F32) as d0, \
                         nc.psum_tensor(f"d1_{tbi}", [128, tb], F32) as d1:
                        d_ps = [d0, d1]
                        wd_sb = [xhat[:, 0:C * 128], wd1[:, :]]
                        evac_base = s_v.n
                        peB_base = s_pe.n
                        for o in range(CD):
                            p = o % 2
                            if o >= 2:
                                nc.sync.wait_ge(s_pe.h, peB_base + o - 1)
                            else:
                                nc.sync.wait_ge(s_pe.h, peB_base)
                            s_wD[p].inc(nc.sync.dma_start(out=wd_sb[p], in_=wd[o]), 16)
                            nc.tensor.wait_ge(s_wD[p].h, s_wD[p].n)
                            nc.tensor.wait_ge(s_v.h, mulA_end)
                            if o >= 2:
                                nc.tensor.wait_ge(s_v.h, evac_base + o - 1)
                            last = None
                            for c in range(C):
                                last = nc.tensor.matmul(out=d_ps[p][:, :],
                                                        lhsT=wd_sb[p][:, c * 128:(c + 1) * 128] if p else wd_sb[0][:, c * 128:(c + 1) * 128],
                                                        rhs=h_sb[:, c * tb:(c + 1) * tb],
                                                        start=(c == 0), stop=(c == C - 1))
                            s_pe.inc(last)
                            nc.vector.wait_ge(s_pe.h, s_pe.n)
                            s_v.inc(nc.vector.scalar_tensor_tensor(
                                out=x_sb[:, o * T + t0:o * T + t0 + tb], in0=d_ps[p][:, :],
                                scalar=0.5, in1=x_sb[:, o * T + t0:o * T + t0 + tb],
                                op0=OP.mult, op1=OP.add))
                nc.sync.wait_ge(s_v.h, s_v.n)
                for dk in range(CD):
                    s_out.inc(nc.sync.dma_start(out=x_out[dk], in_=x_sb[:, dk * T:(dk + 1) * T]), 16)
                nc.sync.wait_ge(s_out.h, s_out.n)
    return nc




def _soft_barrier(nc, ctrs):
    for eng in (nc.tensor, nc.scalar, nc.vector, nc.gpsimd, nc.sync):
        for c in ctrs:
            if c.n > 0:
                eng.wait_ge(c.h, c.n)



def build_qkv0():
    nc = bass.Bass("TRN2", target_bir_lowering=False)
    x_in = nc.dram_tensor("x", [CD, 128, T], F32, kind="ExternalInput").ap()
    wqk = nc.dram_tensor("wqk", [2 * H, 128, CD * 128], BF16, kind="ExternalInput").ap()
    wv = nc.dram_tensor("wv", [CD, 128, D], BF16, kind="ExternalInput").ap()
    bqk = nc.dram_tensor("bqk", [128, 2 * H], F32, kind="ExternalInput").ap()
    bv = nc.dram_tensor("bv", [1, D], F32, kind="ExternalInput").ap()
    q_out = nc.dram_tensor("q", [H, 128, T], BF16, kind="ExternalOutput").ap()
    k_out = nc.dram_tensor("k", [H, 128, T], BF16, kind="ExternalOutput").ap()
    v_out = nc.dram_tensor("v", [T // 128, 128, D], BF16, kind="ExternalOutput").ap()
    with ExitStack() as st:
        x_sb = st.enter_context(nc.sbuf_tensor("x_sb", [128, CD * T], F32))
        xhat = st.enter_context(nc.sbuf_tensor("xhat", [128, CD * T], BF16))
        scr = [st.enter_context(nc.sbuf_tensor(f"scr{p}", [128, T], BF16)) for p in range(2)]
        scrq = [st.enter_context(nc.sbuf_tensor(f"scrq{p}", [128, T], BF16)) for p in range(2)]
        scr_f = st.enter_context(nc.sbuf_tensor("scr_f", [128, T], F32))
        rows = [st.enter_context(nc.sbuf_tensor(f"rows{i}", [1, T], F32)) for i in range(4)]
        ones_bf = st.enter_context(nc.sbuf_tensor("ones_bf", [128, 1], BF16))
        ones1 = st.enter_context(nc.sbuf_tensor("ones1", [1, 128], F32))
        ones1b = st.enter_context(nc.sbuf_tensor("ones1b", [1, 128], BF16))
        eps_sb = st.enter_context(nc.sbuf_tensor("eps_sb", [1, 1], F32))
        bqk_sb = st.enter_context(nc.sbuf_tensor("bqk_sb", [128, 2 * H], F32))
        bv_sb = st.enter_context(nc.sbuf_tensor("bv_sb", [1, D], BF16))
        wqk_sb = [st.enter_context(nc.sbuf_tensor(f"wqk{p}", [128, CD * 128], BF16)) for p in range(2)]
        wv_sb = [st.enter_context(nc.sbuf_tensor(f"wv{dk}", [128, 512], BF16)) for dk in range(CD)]
        obuf = [st.enter_context(nc.sbuf_tensor(f"obuf{p}", [128, T], BF16)) for p in range(2)]
        vbuf = [st.enter_context(nc.sbuf_tensor(f"vbuf{p}", [128, 512], BF16)) for p in range(2)]
        s_x = Ctr(st, nc, "s_x")
        s_wA = [Ctr(st, nc, f"s_wA{p}") for p in range(2)]
        s_wV = Ctr(st, nc, "s_wV")
        s_pe = Ctr(st, nc, "s_pe")
        s_act = Ctr(st, nc, "s_act")
        s_v = Ctr(st, nc, "s_v")
        s_out = Ctr(st, nc, "s_out")
        with nc.Block() as block:
            @block.sync
            def _(sync):
                for dk in range(CD):
                    s_x.inc(nc.sync.dma_start(out=x_sb[:, dk * T:(dk + 1) * T], in_=x_in[dk]), 16)
                s_x.inc(nc.sync.dma_start(out=bqk_sb[:, :], in_=bqk), 16)
                s_x.inc(nc.gpsimd.dma_start(out=bv_sb[:, :], in_=bv), 16)
                s_v.inc(nc.vector.memset(ones_bf[:, :], 1.0))
                s_v.inc(nc.vector.memset(ones1[:, :], 1.0))
                s_v.inc(nc.vector.memset(ones1b[:, :], 1.0))
                s_v.inc(nc.vector.memset(eps_sb[:, :], LN_EPS))
                nc.scalar.wait_ge(s_x.h, s_x.n)
                nc.scalar.wait_ge(s_v.h, 4)
                nc.tensor.wait_ge(s_v.h, 4)
                nc.vector.wait_ge(s_x.h, s_x.n)
                with nc.psum_tensor("stat0", [1, T], F32) as st0, \
                     nc.psum_tensor("stat1", [1, T], F32) as st1:
                    emit_ln_stats(nc, s_pe, s_act, s_v, x_sb, scr, scrq, [st0, st1], rows,
                                  ones_bf, eps_sb[0:1, 0:1], 1.0 / D)
                with nc.psum_tensor("bc0", [128, T], F32) as bc0, \
                     nc.psum_tensor("bc1", [128, T], F32) as bc1:
                    emit_bcast_rows(nc, s_pe, [rows[3][0:1, :T], rows[2][0:1, :T]],
                                    [bc0, bc1], ones1, T, s_v.h, s_v.n)
                    nc.vector.wait_ge(s_pe.h, s_pe.n)
                    for dk in range(CD):
                        sl = slice(dk * T, (dk + 1) * T)
                        nc.vector.tensor_tensor(out=scr_f[:, :], in0=x_sb[:, sl], in1=bc0[:, :], op=OP.mult)
                        last = nc.vector.tensor_tensor(out=xhat[:, sl], in0=scr_f[:, :], in1=bc1[:, :], op=OP.add)
                    s_v.inc(last)
                apply_done = s_v.n
                with nc.psum_tensor("qk0", [128, T], F32) as qk0, \
                     nc.psum_tensor("qk1", [128, T], F32) as qk1:
                    qk_ps = [qk0, qk1]
                    peQ = s_pe.n
                    actQ = s_act.n
                    for oc in range(2 * H):
                        p = oc % 2
                        if oc >= 2:
                            nc.sync.wait_ge(s_pe.h, peQ + oc - 1)
                        s_wA[p].inc(nc.sync.dma_start(out=wqk_sb[p][:, :], in_=wqk[oc]), 16)
                        nc.tensor.wait_ge(s_wA[p].h, s_wA[p].n)
                        nc.tensor.wait_ge(s_v.h, apply_done)
                        if oc >= 2:
                            nc.tensor.wait_ge(s_act.h, actQ + oc - 1)
                        last = None
                        for hh in range(2):
                            ts = slice(hh * 512, (hh + 1) * 512)
                            for dk in range(CD):
                                last = nc.tensor.matmul(out=qk_ps[p][:, ts],
                                                        lhsT=wqk_sb[p][:, dk * 128:(dk + 1) * 128],
                                                        rhs=xhat[:, dk * T:(dk + 1) * T][:, ts],
                                                        start=(dk == 0), stop=(dk == CD - 1))
                        s_pe.inc(last)
                        nc.scalar.wait_ge(s_pe.h, s_pe.n)
                        if oc >= 2:
                            nc.scalar.wait_ge(s_out.h, 16 * (oc - 1))
                        s_act.inc(nc.scalar.activation(out=obuf[p][:, :], in_=qk_ps[p][:, :],
                                                       func=AF.Identity, bias=bqk_sb[:, oc:oc + 1]))
                        dst = q_out[oc] if oc < H else k_out[oc - H]
                        s_out.inc(nc.scalar.dma_start(out=dst, in_=obuf[p][:, :]), 16)
                _soft_barrier(nc, [s_pe, s_act, s_v, s_out])
                with nc.psum_tensor("v0", [128, 512], F32) as v0, \
                     nc.psum_tensor("v1", [128, 512], F32) as v1:
                    v_ps = [v0, v1]
                    outV = s_out.n
                    for vg in range(4):
                        vs = slice(vg * 512, (vg + 1) * 512)
                        if vg >= 1:
                            nc.sync.wait_ge(s_pe.h, s_pe.n)
                        for dk in range(CD):
                            s_wV.inc(nc.sync.dma_start(out=wv_sb[dk][:, :], in_=wv[dk][:, vs]), 16)
                        nc.tensor.wait_ge(s_wV.h, s_wV.n)
                        for tt in range(T // 128):
                            u = vg * 8 + tt
                            p = u % 2
                            nc.tensor.wait_ge(s_v.h, apply_done)
                            if u >= 2:
                                nc.tensor.wait_ge(s_act.h, s_act.n - 1)
                            for dk in range(CD):
                                nc.tensor.matmul(out=v_ps[p][:, :], lhsT=xhat[:, dk * T + tt * 128:dk * T + (tt + 1) * 128],
                                                 rhs=wv_sb[dk][:, :], start=(dk == 0), stop=False)
                            last = nc.tensor.matmul(out=v_ps[p][:, :], lhsT=ones1b[:, :],
                                                    rhs=bv_sb[:, vs], start=False, stop=True)
                            s_pe.inc(last)
                            nc.scalar.wait_ge(s_pe.h, s_pe.n)
                            if u >= 2:
                                nc.scalar.wait_ge(s_out.h, outV + 16 * (u - 1))
                            s_act.inc(nc.scalar.activation(out=vbuf[p][:, :], in_=v_ps[p][:, :],
                                                           func=AF.Identity))
                            s_out.inc(nc.scalar.dma_start(out=v_out[tt][:, vs], in_=vbuf[p][:, :]), 16)
                nc.sync.wait_ge(s_out.h, s_out.n)
    return nc


def build_qkv1():
    nc = bass.Bass("TRN2", target_bir_lowering=False)
    x_in = nc.dram_tensor("x", [CD, 128, T], F32, kind="ExternalInput").ap()
    wq = nc.dram_tensor("wq", [H, 128, CD * 128], BF16, kind="ExternalInput").ap()
    wk = nc.dram_tensor("wk", [H, 128, CD * 128], BF16, kind="ExternalInput").ap()
    wv = nc.dram_tensor("wv", [CD, 128, D], BF16, kind="ExternalInput").ap()
    bq = nc.dram_tensor("bq", [128, H], F32, kind="ExternalInput").ap()
    bk = nc.dram_tensor("bk", [128, H], F32, kind="ExternalInput").ap()
    bv = nc.dram_tensor("bv", [1, D], F32, kind="ExternalInput").ap()
    gq = nc.dram_tensor("gq", [128, CD], F32, kind="ExternalInput").ap()
    gkv = nc.dram_tensor("gkv", [128, CD], F32, kind="ExternalInput").ap()
    ghead = nc.dram_tensor("ghead", [128, 1], F32, kind="ExternalInput").ap()
    q_out = nc.dram_tensor("q", [H, 128, T], F16, kind="ExternalOutput").ap()
    k_out = nc.dram_tensor("k", [H, 128, T], F16, kind="ExternalOutput").ap()
    v_out = nc.dram_tensor("v", [T // 128, 128, D], BF16, kind="ExternalOutput").ap()
    with ExitStack() as st:
        x_sb = st.enter_context(nc.sbuf_tensor("x_sb", [128, CD * T], F32))
        xq = st.enter_context(nc.sbuf_tensor("xq", [128, CD * T], BF16))
        xkv = st.enter_context(nc.sbuf_tensor("xkv", [128, CD * T], BF16))
        scr = [st.enter_context(nc.sbuf_tensor(f"scr{p}", [128, T], BF16)) for p in range(2)]
        scrq = [st.enter_context(nc.sbuf_tensor(f"scrq{p}", [128, T], BF16)) for p in range(2)]
        scr_f = st.enter_context(nc.sbuf_tensor("scr_f", [128, T], F32))
        qf32 = [st.enter_context(nc.sbuf_tensor(f"qf32{p}", [128, T], F32)) for p in range(2)]
        rows = [st.enter_context(nc.sbuf_tensor(f"rows{i}", [1, T], F32)) for i in range(4)]
        ones_bf = st.enter_context(nc.sbuf_tensor("ones_bf", [128, 1], BF16))
        ones1 = st.enter_context(nc.sbuf_tensor("ones1", [1, 128], F32))
        ones1b = st.enter_context(nc.sbuf_tensor("ones1b", [1, 128], BF16))
        eps_sb = st.enter_context(nc.sbuf_tensor("eps_sb", [1, 1], F32))
        bq_sb = st.enter_context(nc.sbuf_tensor("bq_sb", [128, H], F32))
        bk_sb = st.enter_context(nc.sbuf_tensor("bk_sb", [128, H], F32))
        bv_sb = st.enter_context(nc.sbuf_tensor("bv_sb", [1, D], BF16))
        gq_sb = st.enter_context(nc.sbuf_tensor("gq_sb", [128, CD], F32))
        gkv_sb = st.enter_context(nc.sbuf_tensor("gkv_sb", [128, CD], F32))
        gh_sb = st.enter_context(nc.sbuf_tensor("gh_sb", [128, 1], F32))
        w_sb = [st.enter_context(nc.sbuf_tensor(f"w{p}", [128, CD * 128], BF16)) for p in range(2)]
        wv_sb = [st.enter_context(nc.sbuf_tensor(f"wv{dk}", [128, 512], BF16)) for dk in range(CD)]
        obuf = [st.enter_context(nc.sbuf_tensor(f"obuf{p}", [128, T], F16)) for p in range(2)]
        vbuf = [st.enter_context(nc.sbuf_tensor(f"vbuf{p}", [128, 512], BF16)) for p in range(2)]
        s_x = Ctr(st, nc, "s_x")
        s_wA = [Ctr(st, nc, f"s_wA{p}") for p in range(2)]
        s_wV = Ctr(st, nc, "s_wV")
        s_pe = Ctr(st, nc, "s_pe")
        s_act = Ctr(st, nc, "s_act")
        s_v = Ctr(st, nc, "s_v")
        s_out = Ctr(st, nc, "s_out")
        with nc.Block() as block:
            @block.sync
            def _(sync):
                for dk in range(CD):
                    s_x.inc(nc.sync.dma_start(out=x_sb[:, dk * T:(dk + 1) * T], in_=x_in[dk]), 16)
                for name, dst in ((bq, bq_sb), (bk, bk_sb), (gq, gq_sb), (gkv, gkv_sb), (ghead, gh_sb)):
                    s_x.inc(nc.sync.dma_start(out=dst[:, :], in_=name), 16)
                s_x.inc(nc.gpsimd.dma_start(out=bv_sb[:, :], in_=bv), 16)
                s_v.inc(nc.vector.memset(ones_bf[:, :], 1.0))
                s_v.inc(nc.vector.memset(ones1[:, :], 1.0))
                s_v.inc(nc.vector.memset(ones1b[:, :], 1.0))
                s_v.inc(nc.vector.memset(eps_sb[:, :], RMS_EPS))
                nc.scalar.wait_ge(s_x.h, s_x.n)
                nc.scalar.wait_ge(s_v.h, 4)
                nc.tensor.wait_ge(s_v.h, 4)
                nc.vector.wait_ge(s_x.h, s_x.n)
                with nc.psum_tensor("stat0", [1, T], F32) as st0, \
                     nc.psum_tensor("stat1", [1, T], F32) as st1:
                    emit_ln_stats(nc, s_pe, s_act, s_v, x_sb, scr, scrq, [st0, st1], rows,
                                  ones_bf, eps_sb[0:1, 0:1], 1.0 / D, rms=True)
                with nc.psum_tensor("bc0", [128, T], F32) as bc0:
                    emit_bcast_rows(nc, s_pe, [rows[3][0:1, :T]], [bc0], ones1, T, s_act.h, s_act.n)
                    nc.vector.wait_ge(s_pe.h, s_pe.n)
                    for dk in range(CD):
                        sl = slice(dk * T, (dk + 1) * T)
                        nc.vector.tensor_tensor(out=scr_f[:, :], in0=x_sb[:, sl], in1=bc0[:, :], op=OP.mult)
                        nc.vector.tensor_scalar(out=xq[:, sl], in0=scr_f[:, :], scalar1=gq_sb[:, dk:dk + 1],
                                                scalar2=None, op0=OP.mult)
                        last = nc.vector.tensor_scalar(out=xkv[:, sl], in0=scr_f[:, :],
                                                       scalar1=gkv_sb[:, dk:dk + 1], scalar2=None, op0=OP.mult)
                    s_v.inc(last)
                apply_done = s_v.n
                for which, (wsrc, bias_sb, xin, dst_t) in enumerate(
                        ((wq, bq_sb, xq, q_out), (wk, bk_sb, xkv, k_out))):
                    _soft_barrier(nc, [s_pe, s_act, s_v, s_out])
                    with nc.psum_tensor(f"qk0_{which}", [128, T], F32) as qk0, \
                         nc.psum_tensor(f"qk1_{which}", [128, T], F32) as qk1, \
                         nc.psum_tensor(f"hs_{which}", [1, T], F32) as hs_ps, \
                         nc.psum_tensor(f"hb_{which}", [128, T], F32) as hb_ps:
                        qk_ps = [qk0, qk1]
                        evQ = s_v.n
                        outQ = s_out.n
                        proj_marks, evac_marks, sq_marks, exp_marks, stt_marks, bc_marks = \
                            {}, {}, {}, {}, {}, {}

                        def _proj(oc):
                            p = oc % 2
                            if oc >= 2:
                                nc.sync.wait_ge(s_pe.h, proj_marks[oc - 2])
                            s_wA[p].inc(nc.sync.dma_start(out=w_sb[p][:, :], in_=wsrc[oc]), 16)
                            nc.tensor.wait_ge(s_wA[p].h, s_wA[p].n)
                            nc.tensor.wait_ge(s_v.h, apply_done)
                            if oc >= 2:
                                nc.tensor.wait_ge(s_act.h, evac_marks[oc - 2])
                            last = None
                            for hh in range(2):
                                ts = slice(hh * 512, (hh + 1) * 512)
                                for dk in range(CD):
                                    last = nc.tensor.matmul(out=qk_ps[p][:, ts],
                                                            lhsT=w_sb[p][:, dk * 128:(dk + 1) * 128],
                                                            rhs=xin[:, dk * T:(dk + 1) * T][:, ts],
                                                            start=(dk == 0), stop=(dk == CD - 1))
                            s_pe.inc(last)
                            proj_marks[oc] = s_pe.n

                        def _chain(oc):
                            p = oc % 2
                            nc.scalar.wait_ge(s_pe.h, proj_marks[oc])
                            if oc >= 2:
                                nc.scalar.wait_ge(s_v.h, stt_marks[oc - 2])
                            s_act.inc(nc.scalar.activation(out=qf32[p][:, :], in_=qk_ps[p][:, :],
                                                           func=AF.Identity, bias=bias_sb[:, oc:oc + 1]))
                            evac_marks[oc] = s_act.n
                            s_act.inc(nc.scalar.activation(out=scr[p][:, :], in_=qf32[p][:, :],
                                                           func=AF.Square))
                            sq_marks[oc] = s_act.n
                            nc.tensor.wait_ge(s_act.h, sq_marks[oc])
                            for hh in range(2):
                                ts = slice(hh * 512, (hh + 1) * 512)
                                s_pe.inc(nc.tensor.matmul(out=hs_ps[:, ts], lhsT=ones_bf[:, :],
                                                          rhs=scr[p][:, ts], start=True, stop=True))
                            hs_mark = s_pe.n
                            nc.scalar.wait_ge(s_pe.h, hs_mark)
                            s_act.inc(nc.scalar.activation(out=rows[0][0:1, :T], in_=hs_ps[0:1, :T],
                                                           func=AF.Ln, bias=eps_sb[0:1, 0:1],
                                                           scale=1.0 / HD))
                            s_act.inc(nc.scalar.activation(out=rows[0][0:1, :T], in_=rows[0][0:1, :T],
                                                           func=AF.Exp, scale=-0.5))
                            exp_marks[oc] = s_act.n
                            nc.tensor.wait_ge(s_act.h, exp_marks[oc])
                            if oc >= 1:
                                nc.tensor.wait_ge(s_v.h, stt_marks[oc - 1])
                            for hh in range(2):
                                ts = slice(hh * 512, (hh + 1) * 512)
                                s_pe.inc(nc.tensor.matmul(out=hb_ps[:, ts], lhsT=ones1[:, :],
                                                          rhs=rows[0][0:1, :T][:, ts],
                                                          start=True, stop=True))
                            bc_marks[oc] = s_pe.n
                            nc.vector.wait_ge(s_pe.h, bc_marks[oc])
                            if oc >= 2:
                                nc.vector.wait_ge(s_out.h, outQ + 16 * (oc - 1))
                            s_v.inc(nc.vector.scalar_tensor_tensor(out=obuf[p][:, :], in0=qf32[p][:, :],
                                                                   scalar=gh_sb[:, 0:1], in1=hb_ps[:, :],
                                                                   op0=OP.mult, op1=OP.mult))
                            stt_marks[oc] = s_v.n
                            nc.scalar.wait_ge(s_v.h, stt_marks[oc])
                            s_out.inc(nc.scalar.dma_start(out=dst_t[oc], in_=obuf[p][:, :]), 16)

                        for j in range(H + 1):
                            if j < H:
                                _proj(j)
                            if j >= 1:
                                _chain(j - 1)
                _soft_barrier(nc, [s_pe, s_act, s_v, s_out])
                with nc.psum_tensor("v0", [128, 512], F32) as v0, \
                     nc.psum_tensor("v1", [128, 512], F32) as v1:
                    v_ps = [v0, v1]
                    outV = s_out.n
                    for vg in range(4):
                        vs = slice(vg * 512, (vg + 1) * 512)
                        if vg >= 1:
                            nc.sync.wait_ge(s_pe.h, s_pe.n)
                        for dk in range(CD):
                            s_wV.inc(nc.sync.dma_start(out=wv_sb[dk][:, :], in_=wv[dk][:, vs]), 16)
                        nc.tensor.wait_ge(s_wV.h, s_wV.n)
                        for tt in range(T // 128):
                            u = vg * 8 + tt
                            p = u % 2
                            nc.tensor.wait_ge(s_v.h, apply_done)
                            if u >= 2:
                                nc.tensor.wait_ge(s_act.h, s_act.n - 1)
                            for dk in range(CD):
                                nc.tensor.matmul(out=v_ps[p][:, :], lhsT=xkv[:, dk * T + tt * 128:dk * T + (tt + 1) * 128],
                                                 rhs=wv_sb[dk][:, :], start=(dk == 0), stop=False)
                            last = nc.tensor.matmul(out=v_ps[p][:, :], lhsT=ones1b[:, :],
                                                    rhs=bv_sb[:, vs], start=False, stop=True)
                            s_pe.inc(last)
                            nc.scalar.wait_ge(s_pe.h, s_pe.n)
                            if u >= 2:
                                nc.scalar.wait_ge(s_out.h, outV + 16 * (u - 1))
                            s_act.inc(nc.scalar.activation(out=vbuf[p][:, :], in_=v_ps[p][:, :],
                                                           func=AF.Identity))
                            s_out.inc(nc.scalar.dma_start(out=v_out[tt][:, vs], in_=vbuf[p][:, :]), 16)
                nc.sync.wait_ge(s_out.h, s_out.n)
    return nc



def build_attn(causal, qk_dt=BF16):
    nc = bass.Bass("TRN2", target_bir_lowering=False)
    TK = 2048
    x_in = nc.dram_tensor("x", [CD, 128, T], F32, kind="ExternalInput").ap()
    q_in = nc.dram_tensor("q", [H, 128, T], qk_dt, kind="ExternalInput").ap()
    k_in = nc.dram_tensor("k", [H, 128, TK], qk_dt, kind="ExternalInput").ap()
    v_in = nc.dram_tensor("v", [H, 128, TK], BF16, kind="ExternalInput").ap()
    wo = nc.dram_tensor("wo", [CD, 128, CD * 128], BF16, kind="ExternalInput").ap()
    bo = nc.dram_tensor("bo", [128, CD], F32, kind="ExternalInput").ap()
    qpos = nc.dram_tensor("qpos", [1, T], F32, kind="ExternalInput").ap()
    x_out = nc.dram_tensor("x_out", [CD, 128, T], F32, kind="ExternalOutput").ap()
    NKC = TK // 128
    with ExitStack() as st:
        x_sb = st.enter_context(nc.sbuf_tensor("x_sb", [128, CD * T], F32))
        q_sb = st.enter_context(nc.sbuf_tensor("q_sb", [128, H * T], qk_dt))
        pbuf = st.enter_context(nc.sbuf_tensor("pbuf", [128, NKC * T], BF16))
        attn_sb = st.enter_context(nc.sbuf_tensor("attn_sb", [128, H * T], BF16))
        kbuf = [st.enter_context(nc.sbuf_tensor(f"kbuf{p}", [128, TK], qk_dt)) for p in range(2)]
        vbuf = [st.enter_context(nc.sbuf_tensor(f"vbuf{p}", [128, TK], BF16)) for p in range(2)]
        av_sb = [st.enter_context(nc.sbuf_tensor(f"av_sb{p}", [128, T], BF16)) for p in range(2)]
        mq2row = st.enter_context(nc.sbuf_tensor("mq2row", [1, H], F32))
        mk2row = st.enter_context(nc.sbuf_tensor("mk2row", [1, H], F32))
        negb_cols = st.enter_context(nc.sbuf_tensor("negb_cols", [128, H], F32))
        qgb = st.enter_context(nc.sbuf_tensor("qgb", [128, T], F16))
        kcols = st.enter_context(nc.sbuf_tensor("kcols", [128, NKC], F32))
        kcols_i = st.enter_context(nc.sbuf_tensor("kcols_i", [128, NKC], I32))
        rows = [st.enter_context(nc.sbuf_tensor(f"rows{i}", [1, T], F32)) for i in range(2)]
        r2 = st.enter_context(nc.sbuf_tensor("r2", [1, 512], F32))
        sc1 = [st.enter_context(nc.sbuf_tensor(f"sc1_{i}", [1, 1], F32)) for i in range(4)]
        ones_bf = st.enter_context(nc.sbuf_tensor("ones_bf", [128, 1], BF16))
        ones1 = st.enter_context(nc.sbuf_tensor("ones1", [1, 128], F32))
        bo_sb = st.enter_context(nc.sbuf_tensor("bo_sb", [128, CD], F32))
        wo_sb = [st.enter_context(nc.sbuf_tensor(f"wo{p}", [128, CD * 128], BF16)) for p in range(2)]
        s_x = Ctr(st, nc, "s_x")
        s_k = [Ctr(st, nc, f"s_k{p}") for p in range(2)]
        s_vv = [Ctr(st, nc, f"s_vv{p}") for p in range(2)]
        s_wO = [Ctr(st, nc, f"s_wO{p}") for p in range(2)]
        s_pe = Ctr(st, nc, "s_pe")
        s_act = Ctr(st, nc, "s_act")
        s_v = Ctr(st, nc, "s_v")
        s_out = Ctr(st, nc, "s_out")
        s_hd = Ctr(st, nc, "s_hd")
        with nc.Block() as block:
            @block.sync
            def _(sync):
                for dk in range(CD):
                    s_x.inc(nc.sync.dma_start(out=x_sb[:, dk * T:(dk + 1) * T], in_=x_in[dk]), 16)
                for hh in range(H):
                    s_x.inc(nc.sync.dma_start(out=q_sb[:, hh * T:(hh + 1) * T], in_=q_in[hh]), 16)
                s_x.inc(nc.sync.dma_start(out=bo_sb[:, :], in_=bo), 16)
                s_x.inc(nc.sync.dma_start(out=rows[0][0:1, :], in_=qpos), 16)
                s_v.inc(nc.vector.memset(ones_bf[:, :], 1.0))
                s_v.inc(nc.vector.memset(ones1[:, :], 1.0))
                nc.gpsimd.iota(out=kcols_i[:, :], pattern=[[128, NKC]], base=0,
                               channel_multiplier=1).then_inc(s_v.h, 1)
                s_v.n += 1
                nc.vector.wait_ge(s_v.h, 3)
                nc.vector.wait_ge(s_x.h, s_x.n)
                s_v.inc(nc.vector.tensor_copy(out=kcols[:, :], in_=kcols_i[:, :]))
                with nc.psum_tensor("qg_ps", [128, T], F32) as qg_ps:
                    nc.tensor.wait_ge(s_v.h, 2)
                    nc.tensor.wait_ge(s_x.h, s_x.n)
                    for hh in range(2):
                        ts = slice(hh * 512, (hh + 1) * 512)
                        s_pe.inc(nc.tensor.matmul(out=qg_ps[:, ts], lhsT=ones1[:, :],
                                                  rhs=rows[0][0:1, :][:, ts], start=True, stop=True))
                    nc.scalar.wait_ge(s_pe.h, s_pe.n)
                    s_act.inc(nc.scalar.activation(out=qgb[:, :], in_=qg_ps[:, :], func=AF.Identity))
                with nc.psum_tensor("sc0", [128, 512], F32) as sc0, \
                     nc.psum_tensor("sc1p", [128, 512], F32) as sc1p, \
                     nc.psum_tensor("av", [128, T], F32) as av_ps, \
                     nc.psum_tensor("sums", [1, T], F32) as sum_ps, \
                     nc.psum_tensor("rb", [128, T], F32) as rb_ps:
                    sc_ps = [sc0, sc1p]
                    nc.scalar.wait_ge(s_x.h, s_x.n)
                    s_act.inc(nc.scalar.activation(out=pbuf[:, :H * T], in_=q_sb[:, :], func=AF.Square))
                    red_marks = []
                    for h in range(H):
                        reg = sum_ps[0:1, :T] if h % 2 == 0 else rb_ps[0:1, :T]
                        nc.tensor.wait_ge(s_act.h, s_act.n)
                        if h >= 2:
                            nc.tensor.wait_ge(s_v.h, red_marks[h - 2])
                        for hh in range(2):
                            cs = slice(hh * 512, (hh + 1) * 512)
                            s_pe.inc(nc.tensor.matmul(out=reg[:, cs], lhsT=ones_bf[:, :],
                                                      rhs=pbuf[:, h * T:(h + 1) * T][:, cs],
                                                      start=True, stop=True))
                        nc.vector.wait_ge(s_pe.h, s_pe.n)
                        s_v.inc(nc.vector.reduce_max(out=mq2row[0:1, h:h + 1], in_=reg, axis=AX.X))
                        red_marks.append(s_v.n)
                    nc.scalar.wait_ge(s_pe.h, s_pe.n)
                    for h in range(H):
                        p = h % 2
                        if h >= 2:
                            nc.sync.wait_ge(s_v.h, s_v.n - 4)
                        s_k[p].inc(nc.sync.dma_start(out=kbuf[p][:, :], in_=k_in[h]), 16)
                        nc.scalar.wait_ge(s_k[p].h, s_k[p].n)
                        s_act.inc(nc.scalar.activation(out=pbuf[:, p * TK:(p + 1) * TK],
                                                       in_=kbuf[p][:, :], func=AF.Square))
                        nc.tensor.wait_ge(s_act.h, s_act.n)
                        if h >= 1:
                            nc.tensor.wait_ge(s_v.h, s_v.n)
                        for ks in range(4):
                            dst = sum_ps if ks < 2 else rb_ps
                            cs = slice((ks % 2) * 512, (ks % 2) * 512 + 512)
                            s_pe.inc(nc.tensor.matmul(out=dst[0:1, cs], lhsT=ones_bf[:, :],
                                                      rhs=pbuf[:, p * TK + ks * 512:p * TK + (ks + 1) * 512],
                                                      start=True, stop=True))
                        nc.vector.wait_ge(s_pe.h, s_pe.n)
                        s_v.inc(nc.vector.reduce_max(out=sc1[0][0:1, :], in_=sum_ps[0:1, :], axis=AX.X))
                        s_v.inc(nc.vector.reduce_max(out=sc1[3][0:1, :], in_=rb_ps[0:1, :T], axis=AX.X))
                        s_v.inc(nc.vector.tensor_tensor(out=mk2row[0:1, h:h + 1], in0=sc1[0][0:1, :],
                                                        in1=sc1[3][0:1, :], op=OP.max))
                    nc.scalar.wait_ge(s_v.h, s_v.n)
                    s_act.inc(nc.scalar.activation(out=mq2row[0:1, :], in_=mq2row[0:1, :], func=AF.Sqrt))
                    s_act.inc(nc.scalar.activation(out=mk2row[0:1, :], in_=mk2row[0:1, :], func=AF.Sqrt))
                    nc.vector.wait_ge(s_act.h, s_act.n)
                    s_v.inc(nc.vector.tensor_tensor(out=mq2row[0:1, :], in0=mq2row[0:1, :],
                                                    in1=mk2row[0:1, :], op=OP.mult))
                    s_v.inc(nc.vector.tensor_scalar(out=mq2row[0:1, :], in0=mq2row[0:1, :],
                                                    scalar1=-INVSQRT_HD, scalar2=None, op0=OP.mult))
                    nc.tensor.wait_ge(s_v.h, s_v.n)
                    s_pe.inc(nc.tensor.matmul(out=rb_ps[:, 0:H], lhsT=ones1[:, :],
                                              rhs=mq2row[0:1, :], start=True, stop=True))
                    nc.scalar.wait_ge(s_pe.h, s_pe.n)
                    s_act.inc(nc.scalar.activation(out=negb_cols[:, :], in_=rb_ps[:, 0:H], func=AF.Identity))
                    nc.tensor.wait_ge(s_act.h, s_act.n)
                    nc.tensor.wait_ge(s_v.h, s_v.n)
                    for h in range(H):
                        p = h % 2
                        if h >= 2:
                            nc.sync.wait_ge(s_hd.h, h - 1)
                        s_k[p].inc(nc.sync.dma_start(out=kbuf[p][:, :], in_=k_in[h]), 16)
                        s_vv[p].inc(nc.sync.dma_start(out=vbuf[p][:, :], in_=v_in[h]), 16)
                        negb = negb_cols[:, h:h + 1]
                        bound_act = s_act.n
                        nc.tensor.wait_ge(s_vv[p].h, s_vv[p].n)
                        nc.tensor.wait_ge(s_k[p].h, s_k[p].n)
                        units = [(kc, qh) for kc in range(NKC) for qh in range(2)]

                        def _sums_av(u):
                            kc, qh = units[u]
                            ts = slice(qh * 512, (qh + 1) * 512)
                            psl = slice(kc * T + qh * 512, kc * T + (qh + 1) * 512)
                            s_pe.inc(nc.tensor.matmul(out=sum_ps[0:1, ts], lhsT=ones_bf[:, :],
                                                      rhs=pbuf[:, psl], start=(kc == 0),
                                                      stop=(kc == NKC - 1)))
                            s_pe.inc(nc.tensor.matmul(out=av_ps[:, ts],
                                                      lhsT=vbuf[p][:, kc * 128:(kc + 1) * 128],
                                                      rhs=pbuf[:, psl], start=(kc == 0),
                                                      stop=(kc == NKC - 1)))

                        mask_marks = []
                        for u, (kc, qh) in enumerate(units):
                            up = u % 2
                            ts = slice(qh * 512, (qh + 1) * 512)
                            psl = slice(kc * T + qh * 512, kc * T + (qh + 1) * 512)
                            if u >= 2:
                                nc.tensor.wait_ge(s_act.h, bound_act + (u - 1))
                            s_pe.inc(nc.tensor.matmul(out=sc_ps[up][:, :],
                                                      lhsT=kbuf[p][:, kc * 128:(kc + 1) * 128],
                                                      rhs=q_sb[:, h * T:(h + 1) * T][:, ts],
                                                      start=True, stop=True))
                            nc.scalar.wait_ge(s_pe.h, s_pe.n)
                            s_act.inc(nc.scalar.activation(out=pbuf[:, psl], in_=sc_ps[up][:, :],
                                                           func=AF.Exp, bias=negb,
                                                           scale=INVSQRT_HD))
                            if causal:
                                nc.vector.wait_ge(s_act.h, s_act.n)
                                s_v.inc(nc.vector.scalar_tensor_tensor(
                                    out=pbuf[:, psl], in0=qgb[:, ts], scalar=kcols[:, kc:kc + 1],
                                    in1=pbuf[:, psl], op0=OP.is_ge, op1=OP.mult))
                            mask_marks.append(s_v.n if causal else s_act.n)
                            if u >= 2:
                                if causal:
                                    nc.tensor.wait_ge(s_v.h, mask_marks[u - 2])
                                else:
                                    nc.tensor.wait_ge(s_act.h, mask_marks[u - 2])
                                _sums_av(u - 2)
                        for tail in (len(units) - 2, len(units) - 1):
                            if causal:
                                nc.tensor.wait_ge(s_v.h, mask_marks[tail])
                            else:
                                nc.tensor.wait_ge(s_act.h, mask_marks[tail])
                            _sums_av(tail)
                        nc.vector.wait_ge(s_pe.h, s_pe.n)
                        s_v.inc(nc.vector.tensor_copy(out=rows[1][0:1, :], in_=sum_ps[0:1, :]))
                        s_v.inc(nc.vector.reciprocal(out=rows[1][0:1, :], in_=rows[1][0:1, :]))
                        nc.tensor.wait_ge(s_v.h, s_v.n)
                        for hh in range(2):
                            ts = slice(hh * 512, (hh + 1) * 512)
                            s_pe.inc(nc.tensor.matmul(out=rb_ps[:, ts], lhsT=ones1[:, :],
                                                      rhs=rows[1][0:1, :][:, ts], start=True, stop=True))
                        nc.scalar.wait_ge(s_pe.h, s_pe.n)
                        s_act.inc(nc.scalar.activation(out=av_sb[p][:, :], in_=av_ps[:, :], func=AF.Identity))
                        nc.vector.wait_ge(s_act.h, s_act.n)
                        s_v.inc(nc.vector.tensor_tensor(out=attn_sb[:, h * T:(h + 1) * T],
                                                        in0=av_sb[p][:, :], in1=rb_ps[:, :], op=OP.mult))
                        s_hd.inc(nc.vector.engine_nop())
                _soft_barrier(nc, [s_pe, s_act, s_v, s_out])
                with nc.psum_tensor("o0", [128, T], F32) as o0, \
                     nc.psum_tensor("o1", [128, T], F32) as o1:
                    o_ps = [o0, o1]
                    attn_done = s_v.n
                    peO = s_pe.n
                    evO_base = s_v.n
                    for oc in range(CD):
                        p = oc % 2
                        if oc >= 2:
                            nc.sync.wait_ge(s_pe.h, peO + oc - 1)
                        s_wO[p].inc(nc.sync.dma_start(out=wo_sb[p][:, :], in_=wo[oc]), 16)
                        nc.tensor.wait_ge(s_wO[p].h, s_wO[p].n)
                        nc.tensor.wait_ge(s_v.h, attn_done)
                        if oc >= 2:
                            nc.tensor.wait_ge(s_v.h, evO_base + oc - 1)
                        last = None
                        for hh in range(2):
                            ts = slice(hh * 512, (hh + 1) * 512)
                            for dk in range(CD):
                                last = nc.tensor.matmul(out=o_ps[p][:, ts],
                                                        lhsT=wo_sb[p][:, dk * 128:(dk + 1) * 128],
                                                        rhs=attn_sb[:, dk * T:(dk + 1) * T][:, ts],
                                                        start=(dk == 0), stop=(dk == CD - 1))
                        s_pe.inc(last)
                        nc.vector.wait_ge(s_pe.h, s_pe.n)
                        s_v.inc(nc.vector.scalar_tensor_tensor(
                            out=x_sb[:, oc * T:(oc + 1) * T], in0=o_ps[p][:, :], scalar=bo_sb[:, oc:oc + 1],
                            in1=x_sb[:, oc * T:(oc + 1) * T], op0=OP.add, op1=OP.add))
                nc.sync.wait_ge(s_v.h, s_v.n)
                for dk in range(CD):
                    s_out.inc(nc.sync.dma_start(out=x_out[dk], in_=x_sb[:, dk * T:(dk + 1) * T]), 16)
                nc.sync.wait_ge(s_out.h, s_out.n)
    return nc



def build_wcast():
    nc = bass.Bass("TRN2", target_bir_lowering=False)
    wc_in = nc.dram_tensor("wc_in", [32, 128, CD * 128], F32, kind="ExternalInput").ap()
    wc_out = nc.dram_tensor("wc_out", [32, 128, CD * 128], BF16, kind="ExternalOutput").ap()
    with ExitStack() as st:
        buf = [st.enter_context(nc.sbuf_tensor(f"b{p}", [128, CD * 128], BF16)) for p in range(2)]
        s_in = [Ctr(st, nc, f"s_in{p}") for p in range(2)]
        s_out = [Ctr(st, nc, f"s_out{p}") for p in range(2)]
        with nc.Block() as block:
            @block.sync
            def _(sync):
                for u in range(32):
                    p = u % 2
                    if u >= 2:
                        nc.gpsimd.wait_ge(s_out[p].h, s_out[p].n)
                    s_in[p].inc(nc.gpsimd.dma_start(out=buf[p][:, :], in_=wc_in[u]), 16)
                    nc.sync.wait_ge(s_in[p].h, s_in[p].n)
                    s_out[p].inc(nc.sync.dma_start(out=wc_out[u], in_=buf[p][:, :]), 16)
                nc.sync.wait_ge(s_out[0].h, s_out[0].n)
                nc.sync.wait_ge(s_out[1].h, s_out[1].n)
    return nc


def _wcast_host(arrs):
    cat = np.ascontiguousarray(np.concatenate(arrs, axis=0), dtype=np.float32)
    n = cat.shape[0]
    assert n == 256, n
    per = n // NCORES
    in_maps = [{"wc_in": np.ascontiguousarray(cat[c * per:(c + 1) * per])} for c in range(NCORES)]
    res = _run("wcast", build_wcast, in_maps)
    out = np.concatenate([r["wc_out"] for r in res], axis=0)
    outs = []
    off = 0
    for a in arrs:
        outs.append(np.ascontiguousarray(out[off:off + a.shape[0]]))
        off += a.shape[0]
    return outs



def build_prep():
    nc = bass.Bass("TRN2", target_bir_lowering=False)
    gu_idx = nc.dram_tensor("gu_idx", [12, 128, CD * 128], I32, kind="ExternalInput").ap()
    dn_idx = nc.dram_tensor("dn_idx", [2, 128, C0 * 128], I32, kind="ExternalInput").ap()
    lut = nc.dram_tensor("lut", [1, 16], F32, kind="ExternalInput").ap()
    w2t = nc.dram_tensor("w2t", [C1, 128, 256], F32, kind="ExternalInput").ap()
    h128 = nc.dram_tensor("h128", [128, 128], F32, kind="ExternalInput").ap()
    wc_in = nc.dram_tensor("wc_in", [32, 128, CD * 128], F32, kind="ExternalInput").ap()
    gu_w = nc.dram_tensor("gu_w", [12, 128, CD * 128], BF16, kind="ExternalOutput").ap()
    dn_w = nc.dram_tensor("dn_w", [2, 128, C0 * 128], BF16, kind="ExternalOutput").ap()
    w2f = nc.dram_tensor("w2f", [C1, 128, 256], BF16, kind="ExternalOutput").ap()
    wc_out = nc.dram_tensor("wc_out", [32, 128, CD * 128], BF16, kind="ExternalOutput").ap()
    WMAX = C0 * 128
    W = 256
    with ExitStack() as st:
        idx_sb = [st.enter_context(nc.sbuf_tensor(f"idx{p}", [128, WMAX], I32)) for p in range(2)]
        idxb = st.enter_context(nc.sbuf_tensor("idxb", [128, WMAX], BF16))
        w_bf = [st.enter_context(nc.sbuf_tensor(f"wbf{p}", [128, WMAX], BF16)) for p in range(2)]
        msk = st.enter_context(nc.sbuf_tensor("msk", [128, WMAX], BF16))
        lut_sb = st.enter_context(nc.sbuf_tensor("lut_sb", [1, 16], F32))
        lut_col = st.enter_context(nc.sbuf_tensor("lut_col", [128, 16], F32))
        ones1 = st.enter_context(nc.sbuf_tensor("ones1", [1, 128], F32))
        lut_ps = st.enter_context(nc.psum_tensor("lut_ps", [128, 16], F32))
        s_lut = Ctr(st, nc, "s_lut")
        s_in = [Ctr(st, nc, f"s_in{p}") for p in range(2)]
        s_pe = Ctr(st, nc, "s_pe")
        s_v = Ctr(st, nc, "s_v")
        s_out = Ctr(st, nc, "s_out")
        f_abf = st.enter_context(nc.sbuf_tensor("f_abf", [128, C1 * 128], BF16))
        f_a = st.enter_context(nc.sbuf_tensor("f_a", [128, C1 * 128], F32))
        f_b = st.enter_context(nc.sbuf_tensor("f_b", [128, C1 * 128], F32))
        h_sb = st.enter_context(nc.sbuf_tensor("h_sb", [128, 128], BF16))
        f_ps = [st.enter_context(nc.psum_tensor(f"f_ps{p}", [128, 128], F32)) for p in range(2)]
        f_in = Ctr(st, nc, "f_in")
        f_pe = Ctr(st, nc, "f_pe")
        f_a_c = Ctr(st, nc, "f_a_c")
        f_out = Ctr(st, nc, "f_out")
        c_buf = [st.enter_context(nc.sbuf_tensor(f"c_b{p}", [128, CD * 128], BF16)) for p in range(2)]
        c_in = [Ctr(st, nc, f"c_in{p}") for p in range(2)]
        c_out = [Ctr(st, nc, f"c_out{p}") for p in range(2)]
        with nc.Block() as block:
            @block.sync
            def _(sync):
                for u in range(32):
                    p = u % 2
                    if u >= 2:
                        nc.gpsimd.wait_ge(c_out[p].h, c_out[p].n)
                    c_in[p].inc(nc.gpsimd.dma_start(out=c_buf[p][:, :], in_=wc_in[u]), 16)
                    nc.scalar.wait_ge(c_in[p].h, c_in[p].n)
                    c_out[p].inc(nc.scalar.dma_start(out=wc_out[u], in_=c_buf[p][:, :]), 16)
                f_in.inc(nc.gpsimd.dma_start(out=h_sb[:, :], in_=h128), 16)
                FW = 128
                for sl in range(2):
                    co = sl * FW
                    if sl >= 1:
                        nc.gpsimd.wait_ge(f_pe.h, f_pe.n)
                    for c in range(C1):
                        f_in.inc(nc.gpsimd.dma_start(out=f_abf[:, c * FW:(c + 1) * FW],
                                                     in_=w2t[c][:, co:co + FW]), 16)
                    nc.tensor.wait_ge(f_in.h, f_in.n)
                    pe0 = f_pe.n
                    ac0 = f_a_c.n
                    for c in range(C1):
                        p = c % 2
                        if c >= 2:
                            nc.tensor.wait_ge(f_a_c.h, ac0 + c - 1)
                        f_pe.inc(nc.tensor.matmul(out=f_ps[p][:, 0:FW], lhsT=h_sb[:, :],
                                                  rhs=f_abf[:, c * FW:(c + 1) * FW], start=True, stop=True))
                        nc.scalar.wait_ge(f_pe.h, f_pe.n)
                        f_a_c.inc(nc.scalar.activation(out=f_b[:, c * FW:(c + 1) * FW], in_=f_ps[p][:, 0:FW],
                                                       func=AF.Identity, scale=1.0 / math.sqrt(I1)))
                    nc.vector.wait_ge(f_a_c.h, f_a_c.n)
                    if sl >= 1:
                        nc.vector.wait_ge(f_out.h, 16 * C1)
                    fsrc, fdst = f_b, f_a
                    for s in range(6):
                        step = 1 << s
                        last = None
                        for c in range(C1):
                            if c & step:
                                continue
                            cc = c + step
                            nc.vector.tensor_tensor(out=fdst[:, c * FW:(c + 1) * FW],
                                                    in0=fsrc[:, c * FW:(c + 1) * FW],
                                                    in1=fsrc[:, cc * FW:(cc + 1) * FW], op=OP.add)
                            last = nc.vector.tensor_tensor(out=fdst[:, cc * FW:(cc + 1) * FW],
                                                           in0=fsrc[:, c * FW:(c + 1) * FW],
                                                           in1=fsrc[:, cc * FW:(cc + 1) * FW],
                                                           op=OP.subtract)
                        f_a_c.inc(last)
                        nc.vector.wait_ge(f_a_c.h, f_a_c.n)
                        fsrc, fdst = fdst, fsrc
                    nc.gpsimd.wait_ge(f_a_c.h, f_a_c.n)
                    for c in range(C1):
                        f_out.inc(nc.gpsimd.dma_start(out=w2f[c][:, co:co + FW],
                                                      in_=fsrc[:, c * FW:(c + 1) * FW]), 16)
                s_lut.inc(nc.sync.dma_start(out=lut_sb[:, :], in_=lut), 16)
                nc.vector.wait_ge(s_lut.h, 16)
                s_v.inc(nc.vector.memset(ones1[:, :], 1.0))
                nc.tensor.wait_ge(s_v.h, 1)
                s_pe.inc(nc.tensor.matmul(out=lut_ps[:, :], lhsT=ones1[:, :], rhs=lut_sb[:, :],
                                          start=True, stop=True))
                nc.scalar.wait_ge(s_pe.h, s_pe.n)
                s_v.inc(nc.scalar.copy(out=lut_col[:, :], in_=lut_ps[:, :]))
                units = [(gu_idx, gu_w, CD * 128)] * 12 + [(dn_idx, dn_w, C0 * 128)] * 2
                for ui, (usrc, udst, width) in enumerate(units):
                    u = ui if ui < 12 else ui - 12
                    p = ui % 2
                    if ui >= 2:
                        nc.sync.wait_ge(s_v.h, 2 + (ui - 2) + 1)
                    s_in[p].inc(nc.sync.dma_start(out=idx_sb[p][:, :width], in_=usrc[u]), 16)
                    nc.vector.wait_ge(s_in[p].h, s_in[p].n)
                    nc.vector.wait_ge(s_v.h, 2)
                    if ui >= 2:
                        nc.vector.wait_ge(s_out.h, 16 * (ui - 1))
                    nc.vector.tensor_copy(out=idxb[:, :width], in_=idx_sb[p][:, :width])
                    last = None
                    for k in range(16):
                        if k == 0:
                            nc.vector.tensor_scalar(out=w_bf[p][:, :width], in0=idxb[:, :width],
                                                    scalar1=0.0, scalar2=lut_col[:, 0:1],
                                                    op0=OP.is_equal, op1=OP.mult)
                        else:
                            nc.vector.tensor_scalar(out=msk[:, :width], in0=idxb[:, :width],
                                                    scalar1=float(k), scalar2=lut_col[:, k:k + 1],
                                                    op0=OP.is_equal, op1=OP.mult)
                            last = nc.vector.tensor_tensor(out=w_bf[p][:, :width],
                                                           in0=w_bf[p][:, :width],
                                                           in1=msk[:, :width], op=OP.add)
                    s_v.inc(last)
                    nc.gpsimd.wait_ge(s_v.h, 2 + ui + 1)
                    s_out.inc(nc.gpsimd.dma_start(out=udst[u], in_=w_bf[p][:, :width]), 16)
                nc.gpsimd.wait_ge(s_out.h, 16 * 14)
                nc.gpsimd.wait_ge(f_out.h, 16 * 2 * C1)
                nc.gpsimd.wait_ge(c_out[0].h, c_out[0].n)
                nc.gpsimd.wait_ge(c_out[1].h, c_out[1].n)
    return nc


def _prep_host(lut, gate_idx, up_idx, down_idx, W2, cast_arrs):
    gidxT = np.ascontiguousarray(gate_idx.T)
    uidxT = np.ascontiguousarray(up_idx.T)
    didxT = np.ascontiguousarray(down_idx.T)
    gu = np.concatenate([_pw(gidxT, pad_oc=C0P), _pw(uidxT, pad_oc=C0P)], axis=0)
    dn = _pw(didxT)
    lut2 = np.ascontiguousarray(np.asarray(lut, dtype=np.float32).reshape(1, 16))
    w2t = np.ascontiguousarray(np.asarray(W2, dtype=np.float32).T)
    h128 = _hadamard(128)
    cat = np.ascontiguousarray(np.concatenate(cast_arrs, axis=0), dtype=np.float32)
    per = cat.shape[0] // NCORES
    in_maps = []
    for c in range(NCORES):
        in_maps.append({
            "gu_idx": np.ascontiguousarray(gu[c * 12:(c + 1) * 12]),
            "dn_idx": np.ascontiguousarray(dn[c * 2:(c + 1) * 2]),
            "lut": lut2,
            "w2t": np.ascontiguousarray(w2t[:, c * 256:(c + 1) * 256].reshape(C1, 128, 256)),
            "h128": h128,
            "wc_in": np.ascontiguousarray(cat[c * per:(c + 1) * per]),
        })
    res = _run("prep", build_prep, in_maps)
    gu_w = np.concatenate([r["gu_w"] for r in res], axis=0)
    dn_w = np.concatenate([r["dn_w"] for r in res], axis=0)
    wg0 = np.ascontiguousarray(gu_w[:C0])
    wu0 = np.ascontiguousarray(gu_w[C0P:C0P + C0])
    wd0 = np.ascontiguousarray(dn_w)
    w2f_full = np.empty((I1, D), dtype=NPBF16)
    for c in range(NCORES):
        w2f_full[:, c * 256:(c + 1) * 256] = res[c]["w2f"].reshape(I1, 256)
    w2f_t = _pw(w2f_full)
    cast_out = np.concatenate([r["wc_out"] for r in res], axis=0)
    outs = []
    off = 0
    for a in cast_arrs:
        outs.append(np.ascontiguousarray(cast_out[off:off + a.shape[0]]))
        off += a.shape[0]
    return wg0, wu0, wd0, w2f_t, outs



def _shard_tokens(x):
    shards = []
    for c in range(NCORES):
        b, hh = c // 2, c % 2
        shards.append(_fm(x[b, hh * T:(hh + 1) * T, :]).reshape(CD, 128, T))
    return shards


def _unshard_tokens(shards):
    out = np.empty((4, 2048, D), dtype=np.float32)
    for c in range(NCORES):
        b, hh = c // 2, c % 2
        out[b, hh * T:(hh + 1) * T, :] = shards[c].reshape(D, T).T
    return out


def _run(key, build_fn, in_maps):
    if key not in _PROG_CACHE:
        _PROG_CACHE[key] = build_fn()
    res = run_bass_kernel_spmd(_PROG_CACHE[key], in_maps, core_ids=list(range(NCORES)))
    return res.results


def _ghost_host(lut, gate_idx, up_idx, down_idx):
    gidxT = np.ascontiguousarray(gate_idx.T)
    uidxT = np.ascontiguousarray(up_idx.T)
    didxT = np.ascontiguousarray(down_idx.T)
    gu = np.concatenate([_pw(gidxT, pad_oc=C0P), _pw(uidxT, pad_oc=C0P)], axis=0)
    dn = _pw(didxT)
    lut2 = np.ascontiguousarray(np.asarray(lut, dtype=np.float32).reshape(1, 16))
    in_maps = [{"gu_idx": np.ascontiguousarray(gu[c * 12:(c + 1) * 12]),
                "dn_idx": np.ascontiguousarray(dn[c * 2:(c + 1) * 2]),
                "lut": lut2} for c in range(NCORES)]
    res = _run("ghost", build_ghost, in_maps)
    gu_w = np.concatenate([r["gu_w"] for r in res], axis=0)
    dn_w = np.concatenate([r["dn_w"] for r in res], axis=0)
    return (np.ascontiguousarray(gu_w[:C0]), np.ascontiguousarray(gu_w[C0P:C0P + C0]),
            np.ascontiguousarray(dn_w))


def _w2fold_host(W2):
    w2t = np.ascontiguousarray(np.asarray(W2, dtype=np.float32).T)
    h128 = _hadamard(128)
    in_maps = [{"w2t": np.ascontiguousarray(w2t[:, c * 256:(c + 1) * 256].reshape(C1, 128, 256)),
                "h128": h128} for c in range(NCORES)]
    res = _run("w2fold", build_w2fold, in_maps)
    w2f = np.empty((I1, D), dtype=NPBF16)
    for c in range(NCORES):
        w2f[:, c * 256:(c + 1) * 256] = res[c]["w2f"].reshape(I1, 256)
    return _pw(w2f)


def _ffn_host(x_shards, wg_t, wu_t, wd_t, C):
    wdtype = BF16 if wg_t.dtype == NPBF16 else F32
    wdtype_d = BF16 if wd_t.dtype == NPBF16 else F32
    key = ("ffn", C, str(wg_t.dtype), str(wd_t.dtype))
    in_maps = [{"x": x_shards[c], "wg": wg_t, "wu": wu_t, "wd": wd_t} for c in range(NCORES)]
    res = _run(key, lambda: build_ffn(C, wdtype, wdtype_d), in_maps)
    return [np.ascontiguousarray(r["x_out"]) for r in res]


def _attn_host(x_shards, qs, ks, vs, wo_t, bo_cols, causal, qk_dt):
    in_maps = []
    for c in range(NCORES):
        b, hh = c // 2, c % 2
        k_full = np.ascontiguousarray(np.concatenate([ks[2 * b], ks[2 * b + 1]], axis=2))
        v_full = np.concatenate([vs[2 * b], vs[2 * b + 1]], axis=0)
        v_h = np.ascontiguousarray(
            v_full.reshape(16, 128, 16, 128).transpose(2, 1, 0, 3).reshape(H, 128, 2048))
        qpos = np.arange(hh * T, (hh + 1) * T, dtype=np.float32).reshape(1, T)
        in_maps.append({"x": x_shards[c], "q": qs[c], "k": k_full, "v": v_h,
                        "wo": wo_t, "bo": bo_cols, "qpos": qpos})
    res = _run(("attn", causal, str(qk_dt)), lambda: build_attn(causal, qk_dt), in_maps)
    return [np.ascontiguousarray(r["x_out"]) for r in res]


def kernel(L0_attn_x, L0_attn_qkv_w, L0_attn_qkv_b, L0_attn_out_w, L0_attn_out_b,
           L0_ffn_lut, L0_ffn_gate_idx, L0_ffn_up_idx, L0_ffn_down_idx,
           L1_attn_g_q, L1_attn_Wq, L1_attn_bq, L1_attn_g_kv, L1_attn_Wkv, L1_attn_bkv,
           L1_attn_g_head, L1_attn_Wo, L1_attn_bo, L1_ffn_W1, L1_ffn_W2, L1_ffn_W3):
    f32 = lambda a: np.asarray(a, dtype=np.float32)
    qkvT = np.ascontiguousarray(f32(L0_attn_qkv_w).T)
    wqk0_t = _pw(np.ascontiguousarray(qkvT[:, :2 * D]))
    wv0_r = np.ascontiguousarray(qkvT[:, 2 * D:].reshape(CD, 128, D))
    bqk0 = _pb(f32(L0_attn_qkv_b)[:2 * D])
    bv0 = np.ascontiguousarray(f32(L0_attn_qkv_b)[2 * D:].reshape(1, D))
    wo0_t = _pw(np.ascontiguousarray(f32(L0_attn_out_w).T))
    bo0 = _pb(f32(L0_attn_out_b))
    wq1_t = _pw(f32(L1_attn_Wq))
    wk1_t = _pw(np.ascontiguousarray(f32(L1_attn_Wkv)[:, :D]))
    wv1_r = np.ascontiguousarray(f32(L1_attn_Wkv)[:, D:].reshape(CD, 128, D))
    wg1_t = _pw(np.ascontiguousarray(f32(L1_ffn_W1).T))
    wu1_t = _pw(np.ascontiguousarray(f32(L1_ffn_W3).T))
    wg0, wu0, wd0, w2f, casts = _prep_host(
        f32(L0_ffn_lut), np.asarray(L0_ffn_gate_idx), np.asarray(L0_ffn_up_idx),
        np.asarray(L0_ffn_down_idx), f32(L1_ffn_W2),
        [wqk0_t, wv0_r.reshape(CD, 128, D), wo0_t, wq1_t, wk1_t, wv1_r.reshape(CD, 128, D),
         _pw(f32(L1_attn_Wo)), wg1_t, wu1_t])
    (wqk0_t, wv0_r, wo0_t, wq1_t, wk1_t, wv1_r, wo1_t, wg1_t, wu1_t) = casts
    bq1 = _pb(f32(L1_attn_bq))
    bk1 = _pb(f32(L1_attn_bkv)[:D])
    bv1 = np.ascontiguousarray(f32(L1_attn_bkv)[D:].reshape(1, D))
    gq1 = _pb(f32(L1_attn_g_q))
    gkv1 = _pb(f32(L1_attn_g_kv))
    gh1 = np.ascontiguousarray(f32(L1_attn_g_head).reshape(128, 1))
    bo1 = _pb(f32(L1_attn_bo))

    x = _shard_tokens(f32(L0_attn_x))
    x = _ffn_host(x, wg0, wu0, wd0, C0)
    in_maps = [{"x": x[c], "wqk": wqk0_t, "wv": wv0_r, "bqk": bqk0, "bv": bv0}
               for c in range(NCORES)]
    res = _run("qkv0", build_qkv0, in_maps)
    x = _attn_host(x, [r["q"] for r in res], [r["k"] for r in res], [r["v"] for r in res],
                   wo0_t, bo0, True, BF16)
    x = _ffn_host(x, wg0, wu0, wd0, C0)
    x = _ffn_host(x, wg1_t, wu1_t, w2f, C1)
    in_maps = [{"x": x[c], "wq": wq1_t, "wk": wk1_t, "wv": wv1_r, "bq": bq1, "bk": bk1,
                "bv": bv1, "gq": gq1, "gkv": gkv1, "ghead": gh1} for c in range(NCORES)]
    res = _run("qkv1", build_qkv1, in_maps)
    x = _attn_host(x, [r["q"] for r in res], [r["k"] for r in res], [r["v"] for r in res],
                   wo1_t, bo1, False, F16)
    x = _ffn_host(x, wg1_t, wu1_t, w2f, C1)
    return _unshard_tokens(x)
